# version 12
# baseline (speedup 1.0000x reference)
import math
from contextlib import ExitStack

import numpy as np
import ml_dtypes
import concourse.bass as bass
import concourse.mybir as mybir
from concourse.bass_utils import run_bass_kernel_spmd

F32 = mybir.dt.float32
BF16 = mybir.dt.bfloat16
AF = mybir.ActivationFunctionType
ALU = mybir.AluOpType
AX = mybir.AxisListType

NCORES = 8
NTT0 = 16
T = 8192
D = 2048
TPC = T // NCORES
HD = 128
FFI = 8192
LN_EPS = 1e-5


class Ch:

    def __init__(self, sem):
        self.sem = sem
        self.n = 0
        self._waited = {}

    def inc(self, instr, k=1):
        instr.then_inc(self.sem, k)
        self.n += k
        return self.n

    def wait(self, eng, n=None):
        n = self.n if n is None else n
        if n <= 0:
            return
        key = eng.engine
        if self._waited.get(key, 0) >= n:
            return
        eng.wait_ge(self.sem, n)
        self._waited[key] = n


def chans(ctx, nc, *names):
    return [Ch(ctx.enter_context(nc.semaphore(n))) for n in names]


class Prelude:

    def __init__(self, nc, ctx, block):
        self.nc = nc
        self.ctx = ctx
        self.items = []
        self.ch = Ch(ctx.enter_context(nc.semaphore("c_init")))
        self._emitted = False

    def add(self, name, arr, dtype):
        h = self.ctx.enter_context(self.nc.sbuf_tensor("c_" + name, list(arr.shape), dtype))
        t = self.nc.inline_tensor(np.ascontiguousarray(arr), "ct_" + name).ap()
        self.items.append((name, t, h))
        return h

    def scalar_col(self, name, value):
        return self.add(name, np.full((128, 1), value, np.float32), F32)

    def emit_loads(self, eng, ch_self=None):
        for (_, t, h) in self.items:
            eng.dma_start(out=h[:], in_=t).then_inc(self.ch.sem, 16)
            self.ch.n += 16
        self._emitted = True

    def wait(self, eng):
        self.ch.wait(eng)



def build_lnx(rows):
    nc = bass.Bass("TRN2", target_bir_lowering=False)
    x = nc.dram_tensor("x", [rows, D], F32, kind="ExternalInput").ap()
    xn = nc.dram_tensor("xn", [rows, D], BF16, kind="ExternalOutput").ap()
    nt = rows // 128
    with ExitStack() as ctx:
        E = ctx.enter_context
        xt = [E(nc.sbuf_tensor(f"xt{i}", [128, D], F32)) for i in range(2)]
        xo = [E(nc.sbuf_tensor(f"xo{i}", [128, D], BF16)) for i in range(2)]
        sq = E(nc.sbuf_tensor("sq", [128, D], F32))
        ssum = E(nc.sbuf_tensor("ssum", [128, 1], F32))
        ssq = E(nc.sbuf_tensor("ssq", [128, 1], F32))
        mu = E(nc.sbuf_tensor("mu", [128, 1], F32))
        nmu = E(nc.sbuf_tensor("nmu", [128, 1], F32))
        var = E(nc.sbuf_tensor("var", [128, 1], F32))
        rstd = E(nc.sbuf_tensor("rstd", [128, 1], F32))
        ld, cv, cs, st = chans(ctx, nc, "ld", "cv", "cs", "st")
        block = E(nc.Block())
        pre = Prelude(nc, ctx, block)
        c_eps = pre.scalar_col("eps", LN_EPS)
        c_nh = pre.scalar_col("nh", -0.5)

        VOPT = 6
        SOPT = 3

        @block.sync
        def _(sync):
            pre.emit_loads(sync)
            for i in range(nt):
                if i >= 2:
                    st.wait(sync, 16 * (i - 1))
                sync.dma_start(out=xt[i % 2][:], in_=x[i * 128:(i + 1) * 128, :]).then_inc(ld.sem, 16)
                ld.n += 16

        @block.scalar
        def _(s):
            pre.wait(s)
            for i in range(nt):
                ld.wait(s, 16 * (i + 1))
                cv.wait(s, i * VOPT)
                cs.wait(s)
                cs.inc(s.activation(sq[:], xt[i % 2][:], AF.Square, accum_out=ssq[:, 0:1]))
                cv.wait(s, i * VOPT + 5)
                cs.wait(s)
                cs.inc(s.activation(var[:], var[:], AF.Ln, bias=c_eps[:, 0:1]))
                cs.wait(s)
                cs.inc(s.activation(rstd[:], var[:], AF.Exp, scale=c_nh[:, 0:1]))

        @block.vector
        def _(v):
            for i in range(nt):
                ld.wait(v, 16 * (i + 1))
                cs.wait(v, i * SOPT + 1)
                cv.wait(v)
                cv.inc(v.reduce_sum(ssum[:], xt[i % 2][:], axis=AX.X))
                cv.wait(v)
                cv.inc(v.tensor_scalar(mu[:], ssum[:], 1.0 / D, None, ALU.mult))
                cv.wait(v)
                cv.inc(v.tensor_scalar(nmu[:], mu[:], -1.0, None, ALU.mult))
                cv.wait(v)
                cv.inc(v.tensor_tensor(var[:], mu[:], nmu[:], ALU.mult))
                cv.wait(v)
                cv.inc(v.scalar_tensor_tensor(var[:], ssq[:], 1.0 / D, var[:], ALU.mult, ALU.add))
                cs.wait(v, i * SOPT + 3)
                cv.wait(v)
                cv.inc(v.scalar_tensor_tensor(
                    xo[i % 2][:], xt[i % 2][:], nmu[:, 0:1], rstd[:, 0:1].broadcast_to([128, D]),
                    ALU.add, ALU.mult))

        @block.gpsimd
        def _(g):
            for i in range(nt):
                cv.wait(g, i * VOPT + 6)
                g.dma_start(out=xn[i * 128:(i + 1) * 128, :], in_=xo[i % 2][:]).then_inc(st.sem, 16)
                st.n += 16
            st.wait(g)
    return nc


def build_ffn():
    nc = bass.Bass("TRN2", target_bir_lowering=False)
    xnfm = nc.dram_tensor("xnfm", [D, TPC], BF16, kind="ExternalInput").ap()
    wg = nc.dram_tensor("wg", [D, FFI], F32, kind="ExternalInput").ap()
    wu = nc.dram_tensor("wu", [D, FFI], F32, kind="ExternalInput").ap()
    wd = nc.dram_tensor("wd", [FFI, D], F32, kind="ExternalInput").ap()
    g_out = nc.dram_tensor("g", [TPC, D], F32, kind="ExternalOutput").ap()
    hsp_d = nc.dram_tensor("hsp", [TPC, FFI], BF16).ap()

    KC = D // 128
    NT = FFI // 512
    MT = TPC // 128

    with ExitStack() as ctx:
        E = ctx.enter_context
        block = E(nc.Block())
        ld, lw, mmc, ca, cv, st = chans(ctx, nc, "ld", "lw", "mmc", "ca", "cv", "st")
        ld2, lw2, mm2, ca2, st2 = chans(ctx, nc, "ld2", "lw2", "mm2", "ca2", "st2")

        p1v = ExitStack()
        E1v = p1v.enter_context
        silu_t = [E1v(nc.sbuf_tensor(f"silu_t{i}", [128, 512], F32)) for i in range(2)]
        h_t = E1v(nc.sbuf_tensor("h_t", [128, 512], BF16))
        hs_t = [E1v(nc.sbuf_tensor(f"hs_t{i}", [128, 512], BF16)) for i in range(2)]
        a_t = E1v(nc.sbuf_tensor("a_t", [128, 512], BF16))
        m1 = E1v(nc.sbuf_tensor("m1", [128, 128], BF16))
        n1 = E1v(nc.sbuf_tensor("n1", [128, 128], BF16))
        m2 = E1v(nc.sbuf_tensor("m2", [128, 128], BF16))
        n2 = E1v(nc.sbuf_tensor("n2", [128, 128], BF16))
        t2 = E1v(nc.sbuf_tensor("t2", [128, 128], BF16))
        msk = E1v(nc.sbuf_tensor("msk", [128, 128], BF16))
        hfmB = E1v(nc.sbuf_tensor("hfmB", [128, 16 * TPC], BF16, is_transpose_dest=True))

        def hfmB_v(cc):
            return hfmB[:].rearrange("p (cc t) -> p cc t", t=TPC)[:, cc, :]

        p1 = ExitStack()
        E1 = p1.enter_context
        xnsb = E1(nc.sbuf_tensor("xnsb", [128, KC * TPC], BF16))
        wgsb = [E1(nc.sbuf_tensor(f"wgsb{i}", [128, KC * 512], BF16)) for i in range(2)]
        wusb = [E1(nc.sbuf_tensor(f"wusb{i}", [128, KC * 512], BF16)) for i in range(2)]
        psg = [E1(nc.psum_tensor(f"psg{i}", [128, 512], F32)) for i in range(2)]
        psu = [E1(nc.psum_tensor(f"psu{i}", [128, 512], F32)) for i in range(2)]

        VOPT1 = 17

        def wv(sb, buf, kc):
            return sb[buf][:].rearrange("p (kc n) -> p kc n", n=512)[:, kc, :]

        def xn_v(kc, mt):
            return xnsb[:].rearrange("p (kc t) -> p kc t", t=TPC)[:, kc, mt * 128:(mt + 1) * 128]

        @block.gpsimd
        def _(g):
            g.dma_start(out=xnsb[:].rearrange("p (kc t) -> p kc t", t=TPC),
                        in_=xnfm.rearrange("(kc p) t -> p kc t", p=128)).then_inc(ld.sem, 16)
            ld.n += 16
            for nt in range(NT):
                if nt >= 2:
                    mmc.wait(g, (nt - 1) * MT)
                for (w_, sb_) in ((wg, wgsb), (wu, wusb)):
                    g.dma_start(
                        out=sb_[nt % 2][:].rearrange("p (kc n) -> p kc n", n=512),
                        in_=w_[:, nt * 512:(nt + 1) * 512].rearrange("(kc p) n -> p kc n", p=128),
                    ).then_inc(lw.sem, 16)
                    lw.n += 16

        @block.tensor
        def _(t):
            ld.wait(t, 16)
            for nt in range(NT):
                lw.wait(t, 32 * (nt + 1))
                for mt in range(MT):
                    i = nt * MT + mt
                    ca.wait(t, max(0, i - 1))
                    cv.wait(t, max(0, (i - 2) * VOPT1 + 1))
                    for kc in range(KC):
                        t.matmul(psg[i % 2][:], xn_v(kc, mt), wv(wgsb, nt % 2, kc),
                                 start=(kc == 0), stop=(kc == KC - 1))
                    for kc in range(KC):
                        ii = t.matmul(psu[i % 2][:], xn_v(kc, mt), wv(wusb, nt % 2, kc),
                                      start=(kc == 0), stop=(kc == KC - 1))
                    mmc.inc(ii)

        @block.scalar
        def _(s):
            for nt in range(NT):
                for mt in range(MT):
                    i = nt * MT + mt
                    mmc.wait(s, i + 1)
                    cv.wait(s, max(0, (i - 2) * VOPT1 + 1))
                    ca.wait(s)
                    ca.inc(s.activation(silu_t[i % 2][:], psg[i % 2][:], AF.Silu))

        @block.vector
        def _(v):
            for nt in range(NT):
                for mt in range(MT):
                    i = nt * MT + mt
                    base = i * VOPT1
                    ca.wait(v, i + 1)
                    mmc.wait(v, i + 1)
                    def V(instr):
                        cv.inc(instr)
                    cv.wait(v)
                    V(v.tensor_tensor(h_t[:], silu_t[i % 2][:], psu[i % 2][:], ALU.mult))
                    cv.wait(v)
                    V(v.tensor_tensor(a_t[:], h_t[:], h_t[:], ALU.mult))
                    av = a_t[:].rearrange("p (g f) -> p g f", f=4)
                    hv = h_t[:].rearrange("p (g f) -> p g f", f=4)
                    hsv = hs_t[i % 2][:].rearrange("p (g f) -> p g f", f=4)
                    cv.wait(v)
                    V(v.tensor_tensor(m1[:], av[:, :, 0], av[:, :, 1], ALU.max))
                    V(v.tensor_tensor(n1[:], av[:, :, 0], av[:, :, 1], ALU.min))
                    V(v.tensor_tensor(m2[:], av[:, :, 2], av[:, :, 3], ALU.max))
                    cv.wait(v)
                    V(v.tensor_tensor(n2[:], av[:, :, 2], av[:, :, 3], ALU.min))
                    cv.wait(v)
                    V(v.tensor_tensor(m1[:], m1[:], m2[:], ALU.min))
                    V(v.tensor_tensor(n1[:], n1[:], n2[:], ALU.max))
                    cv.wait(v)
                    V(v.tensor_tensor(t2[:], m1[:], n1[:], ALU.max))
                    st.wait(v, 16 * max(0, i - 1))
                    for j in range(4):
                        cv.wait(v)
                        V(v.tensor_tensor(msk[:], av[:, :, j], t2[:], ALU.is_ge))
                        cv.wait(v)
                        V(v.tensor_tensor(hsv[:, :, j], hv[:, :, j], msk[:], ALU.mult))

        @block.sync
        def _(sy):
            for nt in range(NT):
                for mt in range(MT):
                    i = nt * MT + mt
                    cv.wait(sy, (i + 1) * VOPT1)
                    sy.dma_start(out=hsp_d[mt * 128:(mt + 1) * 128, nt * 512:(nt + 1) * 512],
                                 in_=hs_t[i % 2][:]).then_inc(st.sem, 16)
                    st.n += 16
                if nt < 4:
                    st.wait(sy, 16 * MT * (nt + 1))
                    for cc in range(4 * nt, 4 * nt + 4):
                        sy.dma_start_transpose(hfmB_v(cc),
                                               hsp_d[:, cc * 128:(cc + 1) * 128]).then_inc(ld2.sem, 16)
                        ld2.n += 16

        p1.close()

        p2 = ExitStack()
        E2 = p2.enter_context
        hfmA = E2(nc.sbuf_tensor("hfmA", [128, 48 * TPC], BF16, is_transpose_dest=True))
        wdsb = [E2(nc.sbuf_tensor(f"wdsb{i}", [128, 16 * 512], BF16)) for i in range(2)]
        gsb = E2(nc.sbuf_tensor("gsb", [128, MT * 512], F32))
        psd = [E2(nc.psum_tensor(f"psd{i}", [128, 512], F32)) for i in range(MT)]

        def hfm_v(cc):
            t_, c_ = (hfmB, cc) if cc < 16 else (hfmA, cc - 16)
            return t_[:].rearrange("p (cc t) -> p cc t", t=TPC)[:, c_, :]

        def hfm_mt(cc, mt):
            t_, c_ = (hfmB, cc) if cc < 16 else (hfmA, cc - 16)
            return t_[:].rearrange("p (cc t) -> p cc t", t=TPC)[:, c_, mt * 128:(mt + 1) * 128]

        def wd_v(buf, j):
            return wdsb[buf][:].rearrange("p (j n) -> p j n", n=512)[:, j, :]

        @block.scalar
        def _(s):
            for cc in range(16, 64):
                st.wait(s, 16 * MT * (cc // 4 + 1))
                s.dma_start_transpose(hfm_v(cc), hsp_d[:, cc * 128:(cc + 1) * 128]).then_inc(ld2.sem, 16)
                ld2.n += 16
            for oc in range(4):
                for mt in range(MT):
                    i = oc * MT + mt
                    mm2.wait(s, oc * 4 + 4)
                    ca2.wait(s)
                    ca2.inc(s.activation(
                        gsb[:].rearrange("p (mt n) -> p mt n", n=512)[:, mt, :],
                        psd[mt][:], AF.Silu))

        @block.gpsimd
        def _(g):
            for oc in range(4):
                for ccg in range(4):
                    it = oc * 4 + ccg
                    if it >= 2:
                        mm2.wait(g, it - 1)
                    g.dma_start(
                        out=wdsb[it % 2][:].rearrange("p (j n) -> p j n", n=512),
                        in_=wd[ccg * 2048:(ccg + 1) * 2048, oc * 512:(oc + 1) * 512]
                            .rearrange("(j p) n -> p j n", p=128),
                    ).then_inc(lw2.sem, 16)
                    lw2.n += 16

        @block.tensor
        def _(t):
            for oc in range(4):
                ca2.wait(t, oc * MT)
                for ccg in range(4):
                    it = oc * 4 + ccg
                    ld2.wait(t, 16 * 16 * (ccg + 1))
                    lw2.wait(t, 16 * (it + 1))
                    for mt in range(MT):
                        for j in range(16):
                            cc = ccg * 16 + j
                            ii = t.matmul(psd[mt][:], hfm_mt(cc, mt), wd_v(it % 2, j),
                                          start=(ccg == 0 and j == 0), stop=(ccg == 3 and j == 15))
                    mm2.inc(ii)

        @block.sync
        def _(sy):
            for oc in range(4):
                for mt in range(MT):
                    i = oc * MT + mt
                    ca2.wait(sy, i + 1)
                    sy.dma_start(out=g_out[mt * 128:(mt + 1) * 128, oc * 512:(oc + 1) * 512],
                                 in_=gsb[:].rearrange("p (mt n) -> p mt n", n=512)[:, mt, :]
                                 ).then_inc(st2.sem, 16)
                    st2.n += 16
            st2.wait(sy)
        p2.close()
    return nc





def build_l0mix(nqt=NTT0):
    nc = bass.Bass("TRN2", target_bir_lowering=False)
    xnfm = nc.dram_tensor("xnfm", [D, T], BF16, kind="ExternalInput").ap()
    wq1 = nc.dram_tensor("wq1", [D, 256], F32, kind="ExternalInput").ap()
    wq2 = nc.dram_tensor("wq2", [D, 256], F32, kind="ExternalInput").ap()
    wk = nc.dram_tensor("wk", [D, 256], F32, kind="ExternalInput").ap()
    wv = nc.dram_tensor("wv", [D, 256], F32, kind="ExternalInput").ap()
    wlam = nc.dram_tensor("wlam", [D, 2], F32, kind="ExternalInput").ap()
    y0 = nc.dram_tensor("y0", [256, T], BF16, kind="ExternalOutput").ap()

    KC = D // 128
    NTT = T // 512
    SC = 1.0 / math.sqrt(128.0)
    import ml_dtypes as _md

    ASEQ = []
    for qt in range(nqt):
        for hp in range(2):
            for ktg in range(qt + 1):
                for which in range(2):
                    ASEQ.append((qt, hp, ktg, which))
    CSEQ = [(qt, hp) for qt in range(nqt) for hp in range(2)]

    with ExitStack() as ctx:
        E = ctx.enter_context
        block = E(nc.Block())
        pre = Prelude(nc, ctx, block)
        c_sc = pre.scalar_col("sc", SC)
        c_h = pre.scalar_col("h", 0.5)
        c_m1 = pre.scalar_col("m1", -1.0)
        ones_cf = pre.add("ones_cf", np.ones((128, 1), np.float32), F32)
        ones_rf = pre.add("ones_rf", np.ones((1, 128), np.float32), F32)
        ones_bb = pre.add("ones_bb", np.ones((128, 128), _md.bfloat16), BF16)
        identf = pre.add("identf", np.eye(128, dtype=np.float32), F32)
        dm = np.zeros((128, 2048), np.float32)
        for j in range(4):
            kk, qq = np.meshgrid(np.arange(128), np.arange(512), indexing="ij")
            dm[:, j * 512:(j + 1) * 512] = (128 * j + kk <= qq)
        dmask = pre.add("dmask", dm.astype(_md.bfloat16), BF16)

        lx, lw_, mmp, cve, csq, mmn, cbs = chans(ctx, nc, "lx", "lw", "mmp", "cve", "csq", "mmn", "cbs")
        ca, cvm, cvc, sty, cls = chans(ctx, nc, "ca", "cvm", "cvc", "sty", "cls")

        q1sb = E(nc.sbuf_tensor("q1sb", [128, 2 * T], BF16))
        q2sb = E(nc.sbuf_tensor("q2sb", [128, 2 * T], BF16))
        ksb = E(nc.sbuf_tensor("ksb", [128, 2 * T], BF16))
        vtok = E(nc.sbuf_tensor("vtok", [128, (T // 128) * 256], BF16))
        lam_t = E(nc.sbuf_tensor("lam_t", [128, (T // 128) * 2], F32))
        lamrow = E(nc.sbuf_tensor("lamrow", [1, 512], F32))
        nB = E(nc.sbuf_tensor("nB", [128, 1], F32))
        bmx = E(nc.sbuf_tensor("bmx", [1, 2], F32))
        red1 = E(nc.sbuf_tensor("red1", [1, 1], F32))
        e11 = E(nc.sbuf_tensor("e11", [1, 1], F32))
        sqs = E(nc.sbuf_tensor("sqs", [128, 512], F32))
        psb = [E(nc.sbuf_tensor(f"psb{i}", [128, 2048], BF16)) for i in range(2)]
        b1sb = E(nc.sbuf_tensor("b1sb", [128, 512], F32))
        b2sb = E(nc.sbuf_tensor("b2sb", [128, 512], F32))
        t1sb = E(nc.sbuf_tensor("t1sb", [128, 512], F32))
        w2sb = E(nc.sbuf_tensor("w2sb", [128, 512], F32))
        ytile = [E(nc.sbuf_tensor(f"ytile{i}", [128, 512], BF16)) for i in range(2)]

        S_ps = [E(nc.psum_tensor(f"S_ps{i}", [128, 1024], F32)) for i in range(2)]
        O_ps = [E(nc.psum_tensor(f"O_ps{i}", [128, 512], F32)) for i in range(2)]
        l_ps = [E(nc.psum_tensor(f"l_ps{i}", [128, 512], F32)) for i in range(2)]

        def psq(oi):
            if oi < 2:
                return S_ps[0][:, oi * 512:(oi + 1) * 512]
            if oi < 4:
                return S_ps[1][:, (oi - 2) * 512:(oi - 1) * 512]
            return O_ps[oi - 4][:, 0:512]

        def psv(sub):
            return l_ps[sub % 2][:, 0:258]

        def psn(i):
            return l_ps[i % 2][0:1, 0:512]

        def sqb(i):
            return (sqs if i % 2 == 0 else b1sb)[:]

        def q1v(h, sl):
            return q1sb[:].rearrange("p (h t) -> p h t", t=T)[:, h, sl]

        def q2v(h, sl):
            return q2sb[:].rearrange("p (h t) -> p h t", t=T)[:, h, sl]

        def kvv(h, sl):
            return ksb[:].rearrange("p (h t) -> p h t", t=T)[:, h, sl]

        def vtv(i128, h):
            return vtok[:].rearrange("p (i c) -> p i c", c=256)[:, i128, h * 128:(h + 1) * 128]

        def lamtv(i128):
            return lam_t[:].rearrange("p (i c) -> p i c", c=2)[:, i128, :]

        pp = ExitStack()
        E1 = pp.enter_context
        xnt = [E1(nc.sbuf_tensor("xnt0", [128, KC * 512], BF16))]
        wsb = {nm: E1(nc.sbuf_tensor(nm + "s", [128, KC * 256], BF16))
               for nm in ("wq1", "wq2", "wk")}
        wvlsb = E1(nc.sbuf_tensor("wvls", [128, KC * 258], BF16))

        def xv(kc, sl):
            return xnt[0][:].rearrange("p (kc t) -> p kc t", t=512)[:, kc, sl]

        def wvw(nm, kc, sl):
            return wsb[nm][:].rearrange("p (kc n) -> p kc n", n=256)[:, kc, sl]

        OUTS = [("wq1", 0, q1v), ("wq1", 1, q1v), ("wq2", 0, q2v), ("wq2", 1, q2v),
                ("wk", 0, kvv), ("wk", 1, kvv)]
        GPT = 10
        EVT = 26
        NPROJ_MM = NTT * GPT
        CVE_P = 1 + NTT * EVT
        CVE_MS = CVE_P + 1

        def cve_red(i):
            return CVE_MS + (i + 1) * 2

        CSQ_SQ = 96
        CSQ_LN = CSQ_SQ + 1
        CSQ_EXP = CSQ_LN + 1
        MMN_NORM = 96
        MMN_B = MMN_NORM + 1
        CBS_RED = 1
        CBS_NEG = 2
        CBS_NB = 3
        NCOMB = 12
        GSEQ = []
        _pending = None
        for _idx, (_qt, _hp, _ktg, _wh) in enumerate(ASEQ):
            GSEQ.append(("qk", _idx, 0))
            GSEQ.append(("qk", _idx, 1))
            if _pending is not None:
                GSEQ.append(("pv", _pending))
                _pending = None
            if _ktg == _qt and _wh == 1:
                GSEQ.append(("pv", _idx))
            else:
                _pending = _idx
        if _pending is not None:
            GSEQ.append(("pv", _pending))
        QKGRP = {}
        PVGRP = {}
        for _g, item in enumerate(GSEQ):
            if item[0] == "qk":
                QKGRP[(item[1], item[2])] = _g + 1
            else:
                PVGRP[item[1]] = _g + 1

        @block.sync
        def _(sy):
            pre.emit_loads(sy)
            for tt in range(NTT):
                if tt >= 1:
                    mmp.wait(sy, tt * GPT)
                sy.dma_start(
                    out=xnt[0][:].rearrange("p (kc t) -> p kc t", t=512),
                    in_=xnfm[:, tt * 512:(tt + 1) * 512].rearrange("(kc p) t -> p kc t", p=128),
                ).then_inc(lx.sem, 16)
                lx.n += 16

        @block.gpsimd
        def _(g):
            for nm, src in (("wq1", wq1), ("wq2", wq2), ("wk", wk)):
                g.dma_start(out=wsb[nm][:].rearrange("p (kc n) -> p kc n", n=256),
                            in_=src.rearrange("(kc p) n -> p kc n", p=128)).then_inc(lw_.sem, 16)
                lw_.n += 16
            g.dma_start(out=wvlsb[:].rearrange("p (kc n) -> p kc n", n=258)[:, :, 0:256],
                        in_=wv.rearrange("(kc p) n -> p kc n", p=128)).then_inc(lw_.sem, 16)
            lw_.n += 16
            g.dma_start(out=wvlsb[:].rearrange("p (kc n) -> p kc n", n=258)[:, :, 256:258],
                        in_=wlam.rearrange("(kc p) n -> p kc n", p=128)).then_inc(lw_.sem, 16)
            lw_.n += 16
            for ci in range(len(CSEQ)):
                qt, hp = CSEQ[ci]
                cvc.wait(g, ci * NCOMB + NCOMB)
                g.dma_start(out=y0[hp * 128:(hp + 1) * 128, qt * 512:(qt + 1) * 512],
                            in_=ytile[ci % 2][:]).then_inc(sty.sem, 16)
                sty.n += 16
            sty.wait(g)

        @block.tensor
        def _(t):
            lw_.wait(t, 16 * 5)
            pre.wait(t)
            for tt in range(NTT):
                lx.wait(t, 16 * (tt + 1))
                cve.wait(t, max(0, (tt - 1) * EVT + EVT) + (1 if tt > 0 else 0))
                for oi, (nm, h, _) in enumerate(OUTS):
                    for kc in range(KC):
                        ii = t.matmul(psq(oi), wvw(nm, kc, slice(h * 128, (h + 1) * 128)),
                                      xv(kc, slice(0, 512)),
                                      start=(kc == 0), stop=(kc == KC - 1))
                    mmp.inc(ii)
                for sub in range(4):
                    if sub >= 2:
                        cve.wait(t, 1 + tt * EVT + 6 + 5 * sub)
                    for kc in range(KC):
                        ii = t.matmul(psv(sub),
                                      xv(kc, slice(sub * 128, (sub + 1) * 128)),
                                      wvlsb[:].rearrange("p (kc n) -> p kc n", n=258)[:, kc, :],
                                      start=(kc == 0), stop=(kc == KC - 1))
                    mmp.inc(ii)
            for i in range(96):
                csq.wait(t, i + 1)
                cve.wait(t, max(CVE_MS, CVE_MS + 2 * i - 2))
                ii = t.matmul(psn(i), ones_cf[:, 0:1], sqb(i), start=True, stop=True)
                mmn.inc(ii)
            cbs.wait(t, CBS_NEG)
            ii = t.matmul(l_ps[0][:, 0:1], ones_rf[0:1, :], e11[0:1, 0:1], start=True, stop=True)
            mmn.inc(ii)
            cbs.wait(t, CBS_NB)
            prev_combine_ci = None

            def emit_pv(idx):
                qt, hp, ktg, which = ASEQ[idx]
                first = ktg == 0
                last = ktg == qt
                ci = CSEQ.index((qt, hp))
                ca.wait(t, 2 * (idx + 1))
                if ktg == qt:
                    cvm.wait(t, 2 * sum(1 for a in ASEQ[:idx + 1] if a[2] == a[0]))
                if first:
                    cvc.wait(t, max(0, (ci - 1) * NCOMB + NCOMB))
                    cls.wait(t, max(0, (ci - 1) * 7 + 7))
                for j in range(4):
                    kt = ktg * 4 + j
                    t.matmul(l_ps[which][:, 0:512], ones_bb[:],
                             psb[which][:, j * 512:(j + 1) * 512],
                             start=(first and j == 0), stop=(last and j == 3))
                    ii = t.matmul(O_ps[which][:, 0:512], vtv(kt, which),
                                  psb[which][:, j * 512:(j + 1) * 512],
                                  start=(first and j == 0), stop=(last and j == 3))
                mmp.inc(ii)

            pending = None
            for idx, (qt, hp, ktg, which) in enumerate(ASEQ):
                qsrc = q1v if which == 0 else q2v
                ci = CSEQ.index((qt, hp))
                for half in range(2):
                    ca.wait(t, max(0, 2 * idx + half - 1))
                    if prev_combine_ci is not None and half == 0:
                        cvc.wait(t, prev_combine_ci * NCOMB + 1)
                        prev_combine_ci = None
                    for j in range(2):
                        kt = ktg * 4 + half * 2 + j
                        ii = t.matmul(S_ps[half][:, j * 512:(j + 1) * 512],
                                      kvv(which, slice(kt * 128, (kt + 1) * 128)),
                                      qsrc(hp, slice(qt * 512, (qt + 1) * 512)),
                                      start=True, stop=True)
                    mmp.inc(ii)
                if pending is not None:
                    emit_pv(pending)
                    pending = None
                if not (ktg == qt and which == 1):
                    pending = idx
                    continue
                emit_pv(idx)
                if True:
                    ca.wait(t, 2 * (idx + 1))
                    cvc.wait(t, max(0, (ci - 1) * NCOMB + 1))
                    for j in range(4):
                        ii = t.transpose(S_ps[0][0:1, 512 + j * 128:512 + (j + 1) * 128],
                                         lamtv(qt * 4 + j)[:, hp:hp + 1], identf[:])
                    mmn.inc(ii)
                    cls.wait(t, ci * 7 + 4)
                    cvc.wait(t, ci * NCOMB + 3)
                    ii = t.matmul(l_ps[0][:, 0:512], ones_rf[0:1, :], lamrow[0:1, :], start=True, stop=True)
                    mmn.inc(ii)
                    prev_combine_ci = ci
            if pending is not None:
                emit_pv(pending)

        @block.scalar
        def _(s):
            pre.wait(s)
            cve.wait(s, CVE_P)
            for i in range(96):
                oi, tt = i % 6, i // 6
                nm, h, view = OUTS[oi]
                mmn.wait(s, max(0, i - 1))
                csq.wait(s)
                csq.inc(s.activation(sqb(i), view(h, slice(tt * 512, (tt + 1) * 512)), AF.Square))
            cve.wait(s, cve_red(95))
            csq.wait(s)
            csq.inc(s.activation(bmx[0:1, 0:2], bmx[0:1, 0:2], AF.Ln))
            cbs.wait(s, CBS_RED)
            csq.wait(s)
            csq.inc(s.activation(e11[0:1, 0:1], red1[0:1, 0:1], AF.Exp, scale=c_h[0:1, 0:1]))
            cbs.wait(s, CBS_NB)
            for idx, (qt, hp, ktg, which) in enumerate(ASEQ):
                for half in range(2):
                    mmp.wait(s, NPROJ_MM + QKGRP[(idx, half)])
                    ca.wait(s)
                    ca.inc(s.activation(psb[which][:, half * 1024:(half + 1) * 1024],
                                        S_ps[half][:], AF.Exp,
                                        bias=nB[:, 0:1], scale=c_sc[:, 0:1]))
                if ktg == qt and which == 1:
                    ci = CSEQ.index((qt, hp))
                    cvc.wait(s, ci * NCOMB + 1)
                    cls.wait(s)
                    cls.inc(s.activation(lamrow[0:1, :], lamrow[0:1, :], AF.Exp,
                                         scale=c_m1[0:1, 0:1]))
                    cvc.wait(s, ci * NCOMB + 2)
                    cls.wait(s)
                    cls.inc(s.activation(lamrow[0:1, :], lamrow[0:1, :], AF.Ln))
                    cls.wait(s)
                    cls.inc(s.activation(lamrow[0:1, :], lamrow[0:1, :], AF.Exp,
                                         scale=c_m1[0:1, 0:1]))
                    mmp.wait(s, NPROJ_MM + PVGRP[idx])
                    if ci >= 1:
                        cvc.wait(s, (ci - 1) * NCOMB + 9)
                    cls.wait(s)
                    cls.inc(s.activation(b1sb[:], l_ps[0][:, 0:512], AF.Ln))
                    cls.wait(s)
                    cls.inc(s.activation(b1sb[:], b1sb[:], AF.Exp, scale=c_m1[:, 0:1]))
                    if ci >= 1:
                        cvc.wait(s, (ci - 1) * NCOMB + 10)
                    cls.wait(s)
                    cls.inc(s.activation(w2sb[:], l_ps[1][:, 0:512], AF.Ln))
                    cls.wait(s)
                    cls.inc(s.activation(w2sb[:], w2sb[:], AF.Exp, scale=c_m1[:, 0:1]))

        @block.vector
        def _(v):
            cve.inc(v.nop())
            for tt in range(NTT):
                for oi, (nm, h, view) in enumerate(OUTS):
                    mmp.wait(v, tt * GPT + oi + 1)
                    cve.wait(v)
                    cve.inc(v.tensor_copy(view(h, slice(tt * 512, (tt + 1) * 512)), psq(oi)))
                for sub in range(4):
                    mmp.wait(v, tt * GPT + 7 + sub)
                    i128 = tt * 4 + sub
                    cve.wait(v)
                    cve.inc(v.tensor_copy(vtv(i128, 0), psv(sub)[:, 0:128]))
                    cve.wait(v)
                    cve.inc(v.tensor_copy(vtv(i128, 1), psv(sub)[:, 128:256]))
                    cve.wait(v)
                    cve.inc(v.tensor_copy(lamtv(i128), psv(sub)[:, 256:258]))
                    cve.wait(v)
                    cve.inc(v.nop())
                    cve.wait(v)
                    cve.inc(v.nop())
            cve.inc(v.memset(bmx[0:1, 0:2], 0.0))
            for i in range(96):
                mmn.wait(v, i + 1)
                cidx = 0 if (i % 6) < 4 else 1
                cve.wait(v)
                cve.inc(v.reduce_max(red1[0:1, 0:1], psn(i), axis=AX.X))
                cve.wait(v)
                cve.inc(v.tensor_tensor(bmx[0:1, cidx:cidx + 1], bmx[0:1, cidx:cidx + 1],
                                        red1[0:1, 0:1], ALU.max))
            csq.wait(v, CSQ_LN)
            cbs.wait(v)
            cbs.inc(v.tensor_tensor(red1[0:1, 0:1], bmx[0:1, 0:1], bmx[0:1, 1:2], ALU.add))
            csq.wait(v, CSQ_EXP)
            cbs.wait(v)
            cbs.inc(v.tensor_scalar(e11[0:1, 0:1], e11[0:1, 0:1], -SC, None, ALU.mult))
            mmn.wait(v, MMN_B)
            cbs.wait(v)
            cbs.inc(v.tensor_copy(nB[:, 0:1], l_ps[0][:, 0:1]))
            ei = 0
            for idx, (qt, hp, ktg, which) in enumerate(ASEQ):
                ei += 2
                if ktg == qt:
                    ca.wait(v, ei)
                    for half in range(2):
                        cvm.wait(v)
                        cvm.inc(v.tensor_tensor(psb[which][:, half * 1024:(half + 1) * 1024],
                                                psb[which][:, half * 1024:(half + 1) * 1024],
                                                dmask[:, half * 1024:(half + 1) * 1024], ALU.mult))
                if ktg == qt and which == 1:
                    ci = CSEQ.index((qt, hp))
                    mmp.wait(v, NPROJ_MM + PVGRP[idx])
                    mmn.wait(v, MMN_B + ci * 2 + 1)
                    cvc.wait(v)
                    cvc.inc(v.tensor_copy(lamrow[0:1, :], S_ps[0][0:1, 512:1024]))
                    cls.wait(v, ci * 7 + 1)
                    cvc.wait(v)
                    cvc.inc(v.tensor_scalar(lamrow[0:1, :], lamrow[0:1, :], 1.0, None, ALU.add))
                    cls.wait(v, ci * 7 + 5)
                    cvc.wait(v)
                    cvc.inc(v.tensor_tensor(t1sb[:], l_ps[0][:, 0:512], b1sb[:], ALU.mult))
                    cvc.wait(v)
                    cvc.inc(v.tensor_scalar(t1sb[:], t1sb[:], -1.0, 2.0, ALU.mult, ALU.add))
                    cvc.wait(v)
                    cvc.inc(v.tensor_tensor(b1sb[:], b1sb[:], t1sb[:], ALU.mult))
                    cls.wait(v, ci * 7 + 7)
                    cvc.wait(v)
                    cvc.inc(v.tensor_tensor(sqs[:, 0:512], l_ps[1][:, 0:512], w2sb[:], ALU.mult))
                    cvc.wait(v)
                    cvc.inc(v.tensor_scalar(sqs[:, 0:512], sqs[:, 0:512], -1.0, 2.0, ALU.mult, ALU.add))
                    cvc.wait(v)
                    cvc.inc(v.tensor_tensor(w2sb[:], w2sb[:], sqs[:, 0:512], ALU.mult))
                    cvc.wait(v)
                    cvc.inc(v.tensor_tensor(t1sb[:], O_ps[0][:, 0:512], b1sb[:], ALU.mult))
                    cvc.wait(v)
                    cvc.inc(v.tensor_tensor(b2sb[:], O_ps[1][:, 0:512], w2sb[:], ALU.mult))
                    mmn.wait(v, MMN_B + ci * 2 + 2)
                    cvc.wait(v)
                    cvc.inc(v.tensor_tensor(b2sb[:], b2sb[:], l_ps[0][:, 0:512], ALU.mult))
                    sty.wait(v, 16 * max(0, ci - 1))
                    cvc.wait(v)
                    cvc.inc(v.tensor_tensor(ytile[ci % 2][:], t1sb[:], b2sb[:], ALU.subtract))
        pp.close()
    return nc


def build_cmb(CH, with_ln):
    nc = bass.Bass("TRN2", target_bir_lowering=False)
    x_in = nc.dram_tensor("x", [TPC, D], F32, kind="ExternalInput").ap()
    yloc = nc.dram_tensor("yloc", [CH, TPC], BF16, kind="ExternalInput").ap()
    wO = nc.dram_tensor("wO", [CH, D], F32, kind="ExternalInput").ap()
    g_in = nc.dram_tensor("g", [TPC, D], F32, kind="ExternalInput").ap()
    xo = nc.dram_tensor("xo", [TPC, D], F32, kind="ExternalOutput").ap()
    if with_ln:
        xn_out = nc.dram_tensor("xn", [TPC, D], BF16, kind="ExternalOutput").ap()
    NH = CH // 2048
    KCH = 2048 // 128
    MT = TPC // 128
    VOPF = 9 if with_ln else 3
    F0 = (MT if NH > 1 else 0)
    NST = 2 if with_ln else 1

    with ExitStack() as ctx:
        E = ctx.enter_context
        block = E(nc.Block())
        pre = Prelude(nc, ctx, block)
        c_eps = pre.scalar_col("eps", LN_EPS)
        c_nh = pre.scalar_col("nh", -0.5)
        ly, lw_, lx, mmc, cve, csc, sto = chans(ctx, nc, "ly", "lw", "lx", "mmc", "cve", "csc", "sto")
        ysb = E(nc.sbuf_tensor("ysb", [128, KCH * TPC], BF16))
        wsb = E(nc.sbuf_tensor("wsb", [128, KCH * D], BF16))
        if NH > 1:
            s_sb = E(nc.sbuf_tensor("s_sb", [128, MT * D], F32))
        xt = [E(nc.sbuf_tensor("xt0", [128, D], F32))]
        gt_ = [E(nc.sbuf_tensor("gt0", [128, D], F32))]
        xot = [E(nc.sbuf_tensor("xot0", [128, D], F32))]
        xnt = [E(nc.sbuf_tensor("xnt0", [128, D], BF16))] if with_ln else [None]
        u_sb = E(nc.sbuf_tensor("u_sb", [128, D], F32))
        sqs = E(nc.sbuf_tensor("sqs", [128, D], F32)) if with_ln else u_sb
        ssum = E(nc.sbuf_tensor("ssum", [128, 1], F32))
        ssq = E(nc.sbuf_tensor("ssq", [128, 1], F32))
        mu = E(nc.sbuf_tensor("mu", [128, 1], F32))
        nmu = E(nc.sbuf_tensor("nmu", [128, 1], F32))
        var = E(nc.sbuf_tensor("var", [128, 1], F32))
        rstd = E(nc.sbuf_tensor("rstd", [128, 1], F32))
        ps = [E(nc.psum_tensor(f"ps{i}", [128, D], F32)) for i in range(2)]

        def yv(kc, sl):
            return ysb[:].rearrange("p (kc t) -> p kc t", t=TPC)[:, kc, sl]

        def wv(kc):
            return wsb[:].rearrange("p (kc n) -> p kc n", n=D)[:, kc, :]

        def sv(mt):
            return s_sb[:].rearrange("p (mt n) -> p mt n", n=D)[:, mt, :]

        def consume_tok(gi):
            h, mt = divmod(gi, MT)
            if NH > 1 and h == 0:
                return mt + 1
            return F0 + mt * VOPF + 1

        @block.sync
        def _(sy):
            pre.emit_loads(sy)
            for h in range(NH):
                if h > 0:
                    mmc.wait(sy, h * MT)
                sy.dma_start(out=ysb[:].rearrange("p (kc t) -> p kc t", t=TPC),
                             in_=yloc[h * 2048:(h + 1) * 2048, :].rearrange("(kc p) t -> p kc t", p=128)
                             ).then_inc(ly.sem, 16)
                ly.n += 16
            for mt in range(MT):
                if mt >= 1:
                    cve.wait(sy, F0 + (mt - 1) * VOPF + 3)
                sy.dma_start(out=xt[0][:], in_=x_in[mt * 128:(mt + 1) * 128, :]).then_inc(lx.sem, 16)
                sy.dma_start(out=gt_[0][:], in_=g_in[mt * 128:(mt + 1) * 128, :]).then_inc(lx.sem, 16)
                lx.n += 32

        @block.gpsimd
        def _(g):
            for h in range(NH):
                if h > 0:
                    mmc.wait(g, h * MT)
                g.dma_start(out=wsb[:].rearrange("p (kc n) -> p kc n", n=D),
                            in_=wO[h * 2048:(h + 1) * 2048, :].rearrange("(kc p) n -> p kc n", p=128)
                            ).then_inc(lw_.sem, 16)
                lw_.n += 16
            for mt in range(MT):
                cve.wait(g, F0 + mt * VOPF + 3)
                g.dma_start(out=xo[mt * 128:(mt + 1) * 128, :], in_=xot[0][:]).then_inc(sto.sem, 16)
                sto.n += 16
                if with_ln:
                    cve.wait(g, F0 + mt * VOPF + 9)
                    g.dma_start(out=xn_out[mt * 128:(mt + 1) * 128, :], in_=xnt[0][:]).then_inc(sto.sem, 16)
                    sto.n += 16
            sto.wait(g)

        @block.tensor
        def _(t):
            pre.wait(t)
            for h in range(NH):
                ly.wait(t, 16 * (h + 1))
                lw_.wait(t, 16 * (h + 1))
                for mt in range(MT):
                    gi = h * MT + mt
                    if gi >= 2:
                        cve.wait(t, consume_tok(gi - 2))
                    for kc in range(KCH):
                        for oc in range(4):
                            ii = t.matmul(ps[mt % 2][:, oc * 512:(oc + 1) * 512],
                                          yv(kc, slice(mt * 128, (mt + 1) * 128)),
                                          wv(kc)[:, oc * 512:(oc + 1) * 512],
                                          start=(kc == 0), stop=(kc == KCH - 1))
                    mmc.inc(ii)

        @block.vector
        def _(v):
            if NH > 1:
                for mt in range(MT):
                    mmc.wait(v, mt + 1)
                    cve.wait(v)
                    cve.inc(v.tensor_copy(sv(mt), ps[mt % 2][:]))
            hl = NH - 1
            for mt in range(MT):
                gi = hl * MT + mt
                mmc.wait(v, gi + 1)
                lx.wait(v, 32 * (mt + 1))
                cve.wait(v)
                if NH > 1:
                    cve.inc(v.tensor_tensor(u_sb[:], ps[mt % 2][:], sv(mt), ALU.add))
                else:
                    cve.inc(v.tensor_copy(u_sb[:], ps[mt % 2][:]))
                cve.wait(v)
                cve.inc(v.tensor_tensor(u_sb[:], u_sb[:], gt_[0][:], ALU.mult))
                sto.wait(v, 16 * mt * NST)
                cve.wait(v)
                cve.inc(v.tensor_tensor(xot[0][:], xt[0][:], u_sb[:], ALU.add))
                if with_ln:
                    cve.wait(v)
                    cve.inc(v.reduce_sum(ssum[:], xot[0][:], axis=AX.X))
                    cve.wait(v)
                    cve.inc(v.tensor_scalar(mu[:], ssum[:], 1.0 / D, None, ALU.mult))
                    cve.wait(v)
                    cve.inc(v.tensor_scalar(nmu[:], mu[:], -1.0, None, ALU.mult))
                    cve.wait(v)
                    cve.inc(v.tensor_tensor(var[:], mu[:], nmu[:], ALU.mult))
                    csc.wait(v, mt * 3 + 1)
                    cve.wait(v)
                    cve.inc(v.scalar_tensor_tensor(var[:], ssq[:], 1.0 / D, var[:], ALU.mult, ALU.add))
                    csc.wait(v, mt * 3 + 3)
                    cve.wait(v)
                    cve.inc(v.scalar_tensor_tensor(
                        xnt[0][:], xot[0][:], nmu[:, 0:1], rstd[:, 0:1].broadcast_to([128, D]),
                        ALU.add, ALU.mult))

        if with_ln:
            @block.scalar
            def _(s):
                pre.wait(s)
                for mt in range(MT):
                    cve.wait(s, F0 + mt * VOPF + 3)
                    csc.wait(s)
                    csc.inc(s.activation(sqs[:], xot[0][:], AF.Square, accum_out=ssq[:, 0:1]))
                    cve.wait(s, F0 + mt * VOPF + 8)
                    csc.wait(s)
                    csc.inc(s.activation(var[:], var[:], AF.Ln, bias=c_eps[:, 0:1]))
                    csc.wait(s)
                    csc.inc(s.activation(rstd[:], var[:], AF.Exp, scale=c_nh[:, 0:1]))
    return nc


def build_l1mix(stage=6):
    nc = bass.Bass("TRN2", target_bir_lowering=False)
    xnfm = nc.dram_tensor("xnfm", [D, T], BF16, kind="ExternalInput").ap()
    wq = nc.dram_tensor("wq", [D, 256], F32, kind="ExternalInput").ap()
    wk = nc.dram_tensor("wk", [D, 256], F32, kind="ExternalInput").ap()
    wv = nc.dram_tensor("wv", [D, 512], F32, kind="ExternalInput").ap()
    wz = nc.dram_tensor("wz", [D, 512], F32, kind="ExternalInput").ap()
    wba = nc.dram_tensor("wba", [D, 8], F32, kind="ExternalInput").ap()
    convw = nc.dram_tensor("convw", [1024, 4], F32, kind="ExternalInput").ap()
    dtb8 = nc.dram_tensor("dtb8", [8], F32, kind="ExternalInput").ap()
    alog8 = nc.dram_tensor("alog8", [8], F32, kind="ExternalInput").ap()
    normw = nc.dram_tensor("normw", [128], F32, kind="ExternalInput").ap()
    y1 = nc.dram_tensor("y1", [T, 512], BF16, kind="ExternalOutput").ap()
    ktok_d = nc.dram_tensor("ktok_d", [T, 256], BF16).ap()
    vtok_d = nc.dram_tensor("vtok_d", [T, 512], BF16).ap()
    o_d = nc.dram_tensor("o_d", [T, 512], BF16).ap()

    KC = D // 128
    NTT = T // 512
    NCH = T // 128
    RS = 8200
    import ml_dtypes as _md

    with ExitStack() as ctx:
        E = ctx.enter_context
        block = E(nc.Block())
        pre = Prelude(nc, ctx, block)
        c_m1 = pre.scalar_col("m1", -1.0)
        c_nh = pre.scalar_col("nh", -0.5)
        c_one = pre.scalar_col("one", 1.0)
        c_epsl = pre.scalar_col("epsl", 1e-24)
        c_epsr = pre.scalar_col("epsr", 1e-6)
        c_i128 = pre.scalar_col("i128", 1.0 / 128.0)
        ones_cf = pre.add("ones_cf", np.ones((128, 1), np.float32), F32)
        ones_rf = pre.add("ones_rf", np.ones((1, 128), np.float32), F32)
        ones_bb = pre.add("ones_bb", np.ones((128, 128), _md.bfloat16), BF16)
        identf = pre.add("identf", np.eye(128, dtype=np.float32), F32)
        su = np.triu(np.ones((128, 128), np.float32), 1)
        si = np.triu(np.ones((128, 128), np.float32), 0)
        sl_ = np.tril(np.ones((128, 128), np.float32), -1)
        imask8 = pre.add("imask8", np.tile(si, (1, 8)).astype(_md.bfloat16), BF16)
        lmask8 = pre.add("lmask8", np.tile(sl_, (1, 8)).astype(_md.bfloat16), BF16)
        ident8 = pre.add("ident8", np.tile(np.eye(128, dtype=np.float32), (1, 8)).astype(_md.bfloat16), BF16)
        twoI8 = pre.add("twoI8", np.tile(2 * np.eye(128, dtype=np.float32), (1, 8)).astype(_md.bfloat16), BF16)
        onesUT = pre.add("onesUT", np.triu(np.ones((128, 128), np.float32), 0).astype(np.float32), F32)
        ones_ff = pre.add("ones_ff", np.ones((128, 128), np.float32), F32)

        lx, lwt, mmq, cvp, csp = chans(ctx, nc, "lx", "lwt", "mmq", "cvp", "csp")
        mmt, cvt, stt_ = chans(ctx, nc, "mmt", "cvt", "stt")
        mmv, cvv, lba = chans(ctx, nc, "mmv", "cvv", "lba")
        cvc, csc2, mmn2, cvn2, csn2, lcw = chans(ctx, nc, "cvc", "csc2", "mmn2", "cvn2", "csn2", "lcw")
        stk, cgc = chans(ctx, nc, "stk", "cgc")
        mmn_, cvn, csn = chans(ctx, nc, "mmn", "cvn", "csn")
        mmr, cvr, csr, lkv, sto_ = chans(ctx, nc, "mmr", "cvr", "csr", "lkv", "sto")
        mmg, cvg, csg, lg, stg = chans(ctx, nc, "mmg", "cvg", "csg", "lg", "stg")
        lgn, lgx, lgo, lgw = chans(ctx, nc, "lgn", "lgx", "lgo", "lgw")

        beta_t = E(nc.sbuf_tensor("beta_t", [128, NCH * 4], F32))
        g_t = E(nc.sbuf_tensor("g_t", [128, NCH * 4], F32))
        pqa = ExitStack()
        pkv = ExitStack()
        qkvA = pqa.enter_context(nc.sbuf_tensor("qkvA", [128, 4 * RS], BF16))
        qkvB = pkv.enter_context(nc.sbuf_tensor("qkvB", [128, 4 * RS], BF16))

        def qkvrow(r, sl2):
            t_, rr = (qkvA, r) if r < 4 else (qkvB, r - 4)
            base = rr * RS + 4
            return t_[:, base + sl2.start: base + sl2.stop]

        def qkvrowraw(r, a, b):
            t_, rr = (qkvA, r) if r < 4 else (qkvB, r - 4)
            return t_[:, rr * RS + a: rr * RS + b]

        def betav(ci, h=None):
            v_ = beta_t[:].rearrange("p (c e) -> p c e", e=4)[:, ci, :]
            return v_ if h is None else v_[:, h:h + 1]

        def gv(ci, h=None):
            v_ = g_t[:].rearrange("p (c e) -> p c e", e=4)[:, ci, :]
            return v_ if h is None else v_[:, h:h + 1]

        BB = [E(nc.psum_tensor(f"BB{i}", [128, 1024], F32)) for i in range(4)]

        class BankView:
            def __init__(self, t, off):
                self.t, self.off = t, off

            def __getitem__(self, idx):
                if not isinstance(idx, tuple):
                    idx = (idx, slice(None))
                rs, cs = idx
                a = cs.start if cs.start is not None else 0
                b = cs.stop if cs.stop is not None else 512
                return self.t[rs, self.off + a:self.off + b]

        B = [BankView(BB[i // 2], (i % 2) * 512) for i in range(8)]

        def col(h):
            return slice(h * 128, (h + 1) * 128)

        ppx = ExitStack()
        xnt = [ppx.enter_context(nc.sbuf_tensor(f"xnt{i}", [128, KC * 512], BF16))
               for i in range(2)]
        bafm = ppx.enter_context(nc.sbuf_tensor("bafm", [16, T], BF16))
        bafm_d = nc.dram_tensor("bafm_d", [16, T], BF16).ap()

        def xv(tt, kc, sl2):
            return xnt[tt % 2][:].rearrange("p (kc t) -> p kc t", t=512)[:, kc, sl2]

        def wvw(ws, n, kc, sl2):
            return ws[:].rearrange("p (kc n) -> p kc n", n=n)[:, kc, sl2]

        GPT = 5
        EVT = 5
        GPV = 4
        EVV = 4

        @block.sync
        def _(sy):
            pre.emit_loads(sy)
            for it_ in range(2 * NTT):
                tt = it_ % NTT
                j = it_ - 2
                if j >= 0:
                    if j < NTT:
                        mmq.wait(sy, (j + 1) * GPT)
                    else:
                        mmv.wait(sy, (j - NTT + 1) * GPV)
                sy.dma_start(out=xnt[it_ % 2][:].rearrange("p (kc t) -> p kc t", t=512),
                             in_=xnfm[:, tt * 512:(tt + 1) * 512].rearrange("(kc p) t -> p kc t", p=128),
                             ).then_inc(lx.sem, 16)
                lx.n += 16

        pp1 = ExitStack()
        E1 = pp1.enter_context
        wqs = E1(nc.sbuf_tensor("wqs", [128, KC * 256], BF16))
        wks = E1(nc.sbuf_tensor("wks", [128, KC * 256], BF16))
        wbas = E1(nc.sbuf_tensor("wbas", [128, KC * 8], BF16))

        @block.gpsimd
        def _(g):
            for ws, src, n in ((wqs, wq, 256), (wks, wk, 256), (wbas, wba, 8)):
                g.dma_start(out=ws[:].rearrange("p (kc n) -> p kc n", n=n),
                            in_=src.rearrange("(kc p) n -> p kc n", p=128)).then_inc(lwt.sem, 16)
                lwt.n += 16

        @block.tensor
        def _(t):
            pre.wait(t)
            lwt.wait(t, 16 * 3)
            for tt in range(NTT):
                lx.wait(t, 16 * (tt + 1))
                cvp.wait(t, max(0, (tt - 1) * EVT + EVT) + (1 if tt > 0 else 0))
                for oi in range(4):
                    ws = wqs if oi < 2 else wks
                    csl = slice((oi % 2) * 128, (oi % 2) * 128 + 128)
                    for kc in range(KC):
                        ii = t.matmul(B[oi][:, 0:512], wvw(ws, 256, kc, csl),
                                      xv(tt, kc, slice(0, 512)), start=(kc == 0), stop=(kc == KC - 1))
                    mmq.inc(ii)
                cvp.wait(t, 1 + tt * EVT)
                for kc in range(KC):
                    ii = t.matmul(B[4][0:8, 0:512], wvw(wbas, 8, kc, slice(0, 8)),
                                  xv(tt, kc, slice(0, 512)), start=(kc == 0), stop=(kc == KC - 1))
                mmq.inc(ii)

        @block.vector
        def _(v):
            cvp.inc(v.memset(bafm[:], 0.0))
            for tt in range(NTT):
                base = 1 + tt * GPT
                tsl = slice(tt * 512, (tt + 1) * 512)
                for oi in range(4):
                    mmq.wait(v, tt * GPT + oi + 1)
                    cvp.wait(v)
                    cvp.inc(v.tensor_copy(qkvrow(oi, tsl), B[oi][:, 0:512]))
                mmq.wait(v, tt * GPT + 5)
                cvp.wait(v)
                cvp.inc(v.tensor_copy(bafm[0:8, tsl], B[4][0:8, 0:512]))
        pp1.close()

        pp2 = ExitStack()
        E2a = pp2.enter_context
        wvs = E2a(nc.sbuf_tensor("wvs", [128, KC * 512], BF16))

        @block.gpsimd
        def _(g):
            mmq.wait(g, NTT * GPT)
            g.dma_start(out=wvs[:].rearrange("p (kc n) -> p kc n", n=512),
                        in_=wv.rearrange("(kc p) n -> p kc n", p=128)).then_inc(lwt.sem, 16)
            lwt.n += 16
            cvp.wait(g, 1 + NTT * EVT)
            g.dma_start(out=bafm_d, in_=bafm[:]).then_inc(stk.sem, 16)
            stk.n += 16

        @block.tensor
        def _(t):
            lwt.wait(t, 16 * 4)
            for tt in range(NTT):
                lx.wait(t, 16 * (NTT + tt + 1))
                cvv.wait(t, max(0, (tt - 1) * EVV + EVV))
                for sub in range(4):
                    for kc in range(KC):
                        ii = t.matmul(B[sub][:, 0:512], wvw(wvs, 512, kc, col(sub)),
                                      xv(tt, kc, slice(0, 512)), start=(kc == 0), stop=(kc == KC - 1))
                    mmv.inc(ii)

        @block.vector
        def _(v):
            for tt in range(NTT):
                base = tt * GPV
                tsl = slice(tt * 512, (tt + 1) * 512)
                for sub in range(4):
                    mmv.wait(v, base + sub + 1)
                    cvv.wait(v)
                    cvv.inc(v.tensor_copy(qkvrow(4 + sub, tsl), B[sub][:, 0:512]))
        pp2.close()
        ppx.close()

        pt = ExitStack()
        E2 = pt.enter_context
        dtbbc = E2(nc.sbuf_tensor("dtbbc", [128, 8], F32))
        albc = E2(nc.sbuf_tensor("albc", [128, 8], F32))
        dtbr = E2(nc.sbuf_tensor("dtbr", [1, 8], F32))
        alr = E2(nc.sbuf_tensor("alr", [1, 8], F32))
        batok = E2(nc.sbuf_tensor("batok", [128, NCH * 16], BF16, is_transpose_dest=True))
        scb = E2(nc.sbuf_tensor("scb", [128, NCH * 4], F32))
        scb2 = E2(nc.sbuf_tensor("scb2", [128, NCH * 4], F32))

        def batv(c0, c1):
            return batok[:].rearrange("p (c e) -> p c e", e=16)[:, :, c0:c1]

        @block.sync
        def _(sy):
            mmv.wait(sy, NTT * GPV)
            sy.dma_start(out=dtbr[0:1, :], in_=dtb8[:].unsqueeze(0)).then_inc(lg.sem, 16)
            sy.dma_start(out=alr[0:1, :], in_=alog8[:].unsqueeze(0)).then_inc(lg.sem, 16)
            lg.n += 32
            stk.wait(sy, 16)
            for ci in range(NCH):
                sy.dma_start_transpose(
                    batok[:].rearrange("p (c e) -> p c e", e=16)[:, ci, :],
                    bafm_d[:, ci * 128:(ci + 1) * 128]).then_inc(lba.sem, 16)
                lba.n += 16

        @block.tensor
        def _(t):
            lg.wait(t, 32)
            cvv.wait(t, NTT * EVV)
            t.matmul(B[1][:, 0:8], ones_rf[0:1, :], dtbr[0:1, :], start=True, stop=True)
            ii = t.matmul(B[2][:, 0:8], ones_rf[0:1, :], alr[0:1, :], start=True, stop=True)
            mmt.inc(ii)

        @block.vector
        def _(v):
            mmt.wait(v, 1)
            cvt.inc(v.tensor_copy(dtbbc[:], B[1][:, 0:8]))
            cvt.wait(v)
            cvt.inc(v.tensor_copy(albc[:], B[2][:, 0:8]))
            lba.wait(v, 16 * NCH)
            sv1 = scb[:].rearrange("p (c e) -> p c e", e=4)
            sv2 = scb2[:].rearrange("p (c e) -> p c e", e=4)
            g_v = g_t[:].rearrange("p (c e) -> p c e", e=4)
            dtbv = dtbbc[:].rearrange("p (o e) -> p o e", o=1)[:, :, 4:8].broadcast_to([128, NCH, 4])
            alv = albc[:].rearrange("p (o e) -> p o e", o=1)[:, :, 4:8].broadcast_to([128, NCH, 4])
            cvt.wait(v)
            cvt.inc(v.tensor_tensor(sv1, batv(4, 8), dtbv, ALU.add))
            cvt.wait(v)
            cvt.inc(v.tensor_scalar(sv2, sv1, -1.0, None, ALU.mult))
            cvt.wait(v)
            cvt.inc(v.tensor_tensor(sv2, sv1, sv2, ALU.max))
            csn.wait(v, 2)
            cvt.wait(v)
            cvt.inc(v.tensor_scalar(sv1, sv1, 0.0, None, ALU.max))
            cvt.wait(v)
            cvt.inc(v.tensor_tensor(sv1, sv1, sv2, ALU.add))
            csn.wait(v, 3)
            cvt.wait(v)
            cvt.inc(v.tensor_tensor(sv1, sv1, alv, ALU.mult))
            cvt.wait(v)
            cvt.inc(v.tensor_scalar(g_v, sv1, -1.0, None, ALU.mult))
            csn.wait(v, 4)
            cvt.wait(v)
            cvt.inc(v.tensor_scalar(scb[:], scb[:], 1.0, None, ALU.add))
            cvt.wait(v)
            cvt.inc(v.reciprocal(beta_t[:], scb[:]))

        @block.scalar
        def _(s):
            pre.wait(s)
            cvt.wait(s, 2 + 3)
            csn.wait(s)
            csn.inc(s.activation(scb2[:], scb2[:], AF.Exp, scale=c_m1[:, 0:1]))
            csn.wait(s)
            csn.inc(s.activation(scb2[:], scb2[:], AF.Ln, bias=c_one[:, 0:1]))
            csn.wait(s)
            csn.inc(s.activation(albc[:], albc[:], AF.Exp))
            cvt.wait(s, 9)
            sv1 = scb[:].rearrange("p (c e) -> p c e", e=4)
            csn.wait(s)
            csn.inc(s.activation(sv1, batv(0, 4), AF.Exp, scale=c_m1[:, 0:1]))
        pt.close()

        if stage >= 2:
            pc = ExitStack()
            E3 = pc.enter_context
            tv1 = [E3(nc.sbuf_tensor(f"tv1{i}", [128, T // 2], BF16)) for i in range(2)]
            tv2 = [E3(nc.sbuf_tensor(f"tv2{i}", [128, T // 2], BF16)) for i in range(2)]
            tmp3 = E3(nc.sbuf_tensor("tmp3", [128, T // 2], BF16))
            tmp4 = E3(nc.sbuf_tensor("tmp4", [128, T // 2], BF16))
            cws = E3(nc.sbuf_tensor("cws", [128, 8 * 4], F32))
            HT = T // 2

            def cwcol(r, j):
                return cws[:].rearrange("p (r j) -> p r j", j=4)[:, r, j:j + 1]

            SSEQ = ([("v", k) for k in range(4)] + [("h", 0)]
                    + [("v", k) for k in range(4, 8)] + [("h", 1)]
                    + [("v", k) for k in range(8, 14)])
            VSILU = {}
            HSILU = {}
            for _i, (_kd, _k) in enumerate(SSEQ):
                (VSILU if _kd == "v" else HSILU)[_k] = _i + 1
            ROWDONE = {r: max(VSILU[2 * r], VSILU[2 * r + 1]) for r in range(7)}
            ROWDONE[7] = HSILU[1]
            CSC2_END = len(SSEQ)

            @block.gpsimd
            def _(g):
                mmv.wait(g, NTT * GPV)
                stk.wait(g, 16)
                g.dma_start(out=cws[:].rearrange("p (r j) -> p r j", j=4),
                            in_=convw.rearrange("(r p) j -> p r j", p=128)).then_inc(lcw.sem, 16)
                lcw.n += 16
                lcw.wait(g, 16)
                cvc.wait(g, 8)
                for k in range(2):
                    r, hf = 7, 1 - k
                    off = hf * HT
                    sl4 = [qkvrowraw(r, 1 + j + off, 1 + j + off + HT) for j in range(4)]
                    cwb = [cwcol(r, j).broadcast_to([128, HT]) for j in range(4)]
                    cgc.wait(g)
                    cgc.inc(g.tensor_tensor(tmp3[:], sl4[0], cwb[0], ALU.mult))
                    if k >= 1:
                        csc2.wait(g, HSILU[k - 1])
                    cgc.wait(g)
                    cgc.inc(g.tensor_tensor(tmp4[:], sl4[1], cwb[1], ALU.mult))
                    cgc.wait(g)
                    cgc.inc(g.tensor_tensor(tmp3[:], tmp3[:], tmp4[:], ALU.add))
                    cgc.wait(g)
                    cgc.inc(g.tensor_tensor(tmp4[:], sl4[2], cwb[2], ALU.mult))
                    cgc.wait(g)
                    cgc.inc(g.tensor_tensor(tmp3[:], tmp3[:], tmp4[:], ALU.add))
                    cgc.wait(g)
                    cgc.inc(g.tensor_tensor(tmp4[:], sl4[3], cwb[3], ALU.mult))
                    cgc.wait(g)
                    cgc.inc(g.tensor_tensor(tmp4[:], tmp3[:], tmp4[:], ALU.add))

            @block.vector
            def _(v):
                lcw.wait(v, 16)
                cvp.wait(v, 1 + NTT * EVT)
                cvv.wait(v, NTT * EVV)
                for r in range(8):
                    t_, rr = (qkvA, r) if r < 4 else (qkvB, r - 4)
                    cvc.wait(v)
                    cvc.inc(v.memset(t_[:, rr * RS:rr * RS + 4], 0.0))
                for k in range(14):
                    r, hf = k // 2, 1 - (k % 2)
                    off = hf * HT
                    t1p, t2p = tv1[k % 2], tv2[k % 2]
                    sl4 = [qkvrowraw(r, 1 + j + off, 1 + j + off + HT) for j in range(4)]
                    cvc.wait(v)
                    cvc.inc(v.tensor_scalar(t1p[:], sl4[0], cwcol(r, 0), None, ALU.mult))
                    if k >= 2:
                        csc2.wait(v, VSILU[k - 2])
                    cvc.wait(v)
                    cvc.inc(v.scalar_tensor_tensor(t2p[:], sl4[1], cwcol(r, 1), t1p[:], ALU.mult, ALU.add))
                    cvc.wait(v)
                    cvc.inc(v.scalar_tensor_tensor(t1p[:], sl4[2], cwcol(r, 2), t2p[:], ALU.mult, ALU.add))
                    cvc.wait(v)
                    cvc.inc(v.scalar_tensor_tensor(t2p[:], sl4[3], cwcol(r, 3), t1p[:], ALU.mult, ALU.add))

            @block.scalar
            def _(s):
                def hsilu(k):
                    cgc.wait(s, 7 * (k + 1))
                    hf = 1 - k
                    csc2.wait(s)
                    csc2.inc(s.activation(qkvrowraw(7, 4 + hf * HT, 4 + hf * HT + HT),
                                          tmp4[:], AF.Silu))

                def vsilu(k):
                    r, hf = k // 2, 1 - (k % 2)
                    cvc.wait(s, 8 + 4 * (k + 1))
                    csc2.wait(s)
                    csc2.inc(s.activation(qkvrowraw(r, 4 + hf * HT, 4 + hf * HT + HT),
                                          tv2[k % 2][:], AF.Silu))

                for kind, k in SSEQ:
                    (vsilu if kind == "v" else hsilu)(k)


        if stage >= 3:
            nsc4 = E3(nc.sbuf_tensor("nsc4", [128, 2048], F32))
            rsr2 = E3(nc.sbuf_tensor("rsr2", [1, 1024], F32))
            PIPE = 2

            def nq(i):
                return nsc4[:, (i % 4) * 512:(i % 4) * 512 + 512]

            def rhalf(i):
                return rsr2[0:1, (i % 2) * 512:(i % 2) * 512 + 512]

            SEN = []
            MEN = []
            for k in range(64 + PIPE):
                if k < 64:
                    SEN.append(("sq", k))
                    MEN.append(("sum", k))
                if k >= PIPE:
                    SEN.append(("ln", k - PIPE))
                    SEN.append(("ex", k - PIPE))
                    MEN.append(("bc", k - PIPE))
            NTOK = {op: i + 1 for i, op in enumerate(SEN)}
            MTOKN = {op: i + 1 for i, op in enumerate(MEN)}

            @block.scalar
            def _(s):
                for k in range(64 + PIPE):
                    if k < 64:
                        r, tt = divmod(k, 16)
                        csc2.wait(s, ROWDONE[r])
                        if k >= 4:
                            mmn2.wait(s, MTOKN[("sum", k - 4)])
                        csn2.wait(s)
                        csn2.inc(s.activation(nq(k), qkvrowraw(r, 4 + tt * 512, 4 + (tt + 1) * 512),
                                              AF.Square))
                    if k >= PIPE:
                        j = k - PIPE
                        mmn2.wait(s, MTOKN[("bc", j - 2)] if j >= 2 else MTOKN[("sum", j)])
                        csn2.wait(s)
                        csn2.inc(s.activation(rhalf(j), B[j % 4][0:1, 0:512], AF.Ln,
                                              bias=c_epsl[0:1, 0:1]))
                        csn2.wait(s)
                        csn2.inc(s.activation(rhalf(j), rhalf(j), AF.Exp,
                                              scale=c_nh[0:1, 0:1]))

            @block.tensor
            def _(t):
                for k in range(64 + PIPE):
                    if k < 64:
                        csn2.wait(t, NTOK[("sq", k)])
                        ii = t.matmul(B[k % 4][0:1, 0:512], ones_cf[:, 0:1], nq(k),
                                      start=True, stop=True)
                        mmn2.inc(ii)
                    if k >= PIPE:
                        j = k - PIPE
                        csn2.wait(t, NTOK[("ex", j)])
                        if j >= 4:
                            cvn2.wait(t, j - 3)
                        ii = t.matmul(B[4 + j % 4][:, 0:512], ones_rf[0:1, :], rhalf(j),
                                      start=True, stop=True)
                        mmn2.inc(ii)

            @block.vector
            def _(v):
                for j in range(64):
                    r, tt = divmod(j, 16)
                    mmn2.wait(v, MTOKN[("bc", j)])
                    sl2 = qkvrowraw(r, 4 + tt * 512, 4 + (tt + 1) * 512)
                    cvn2.wait(v)
                    cvn2.inc(v.tensor_tensor(sl2, sl2, B[4 + j % 4][:, 0:512], ALU.mult))


        if stage >= 4:

            kfm_d = nc.dram_tensor("kfm_d", [256, T], BF16).ap()
            vfm_d = nc.dram_tensor("vfm_d", [512, T], BF16).ap()

            @block.gpsimd
            def _(g):
                for r in range(2, 4):
                    cvn2.wait(g, 64)
                    g.dma_start(out=kfm_d[(r - 2) * 128:(r - 1) * 128, :],
                                in_=qkvrowraw(r, 4, 4 + T)).then_inc(stk.sem, 16)
                    stk.n += 16
                for r in range(4, 8):
                    csc2.wait(g, CSC2_END)
                    g.dma_start(out=vfm_d[(r - 4) * 128:(r - 3) * 128, :],
                                in_=qkvrowraw(r, 4, 4 + T)).then_inc(stk.sem, 16)
                    stk.n += 16


        if stage >= 2:
            pc.close()
        if stage >= 4:
            pkv.close()
        if stage >= 5:

            pr = ExitStack()
            E4 = pr.enter_context
            G2 = E4(nc.sbuf_tensor("G2", [128, 8], F32))
            negG2 = E4(nc.sbuf_tensor("negG2", [128, 8], F32))
            GC2 = E4(nc.sbuf_tensor("GC2", [128, 8], F32))
            gdiff2 = E4(nc.sbuf_tensor("gdiff2", [128, 8], F32))
            gam2 = E4(nc.sbuf_tensor("gam2", [128, 8], F32))
            etil2 = E4(nc.sbuf_tensor("etil2", [128, 8], F32))
            gamC2 = E4(nc.sbuf_tensor("gamC2", [128, 8], F32))
            bg4 = E4(nc.sbuf_tensor("bg4", [128, 4], F32))
            grow8 = E4(nc.sbuf_tensor("grow8", [1, 1024], F32))
            ngrow8 = E4(nc.sbuf_tensor("ngrow8", [1, 1024], F32))
            scrA = E4(nc.sbuf_tensor("scrA", [128, 1024], F32))
            scrB = E4(nc.sbuf_tensor("scrB", [128, 1024], F32))
            scrC = E4(nc.sbuf_tensor("scrC", [128, 1024], F32))
            dexp8 = E4(nc.sbuf_tensor("dexp8", [128, 1024], F32))
            dexp28 = E4(nc.sbuf_tensor("dexp28", [128, 1024], F32))
            alow8 = E4(nc.sbuf_tensor("alow8", [128, 1024], BF16))
            a8tmp = E4(nc.sbuf_tensor("a8tmp", [128, 1024], BF16))
            aqk8 = E4(nc.sbuf_tensor("aqk8", [128, 1024], BF16))
            R8 = E4(nc.sbuf_tensor("R8", [128, 1024], BF16))
            Y8 = E4(nc.sbuf_tensor("Y8", [128, 1024], BF16))
            Z8 = E4(nc.sbuf_tensor("Z8", [128, 1024], BF16))
            Bblk = [E4(nc.sbuf_tensor(f"Bblk{i}", [128, 1024], BF16)) for i in range(2)]
            Wb4 = [E4(nc.sbuf_tensor(f"Wb4{i}", [128, 512], BF16)) for i in range(2)]
            Uv4 = [E4(nc.sbuf_tensor(f"Uv4{i}", [128, 512], BF16)) for i in range(2)]
            WtT4 = [E4(nc.sbuf_tensor(f"WtT4{i}", [128, 512], BF16)) for i in range(2)]
            M4 = E4(nc.sbuf_tensor("M4", [128, 512], BF16))
            Kt4 = E4(nc.sbuf_tensor("Kt4", [128, 512], BF16))
            S4 = E4(nc.sbuf_tensor("S4", [128, 512], F32))
            S4b = E4(nc.sbuf_tensor("S4b", [128, 512], BF16))
            o1f = E4(nc.sbuf_tensor("o1f", [128, 512], F32))
            o4t = [E4(nc.sbuf_tensor(f"o4t{i}", [128, 512], BF16)) for i in range(2)]
            ktch2 = [E4(nc.sbuf_tensor(f"ktch2{i}", [128, 512], BF16, is_transpose_dest=True)) for i in range(2)]
            vtch2 = [E4(nc.sbuf_tensor(f"vtch2{i}", [128, 1024], BF16, is_transpose_dest=True)) for i in range(2)]

            NP = NCH // 2
            NEWT = 7
            P0 = 2
            CVT_END = 11

            NRND = 6
            VNAMES = (["gevac", "neg"] + [f"grow{h}" for h in range(8)]
                      + [f"ngrow{h}" for h in range(8)]
                      + ["gcevac", "gdiff", "dargA1", "dargA2", "dargB1", "dargB2",
                         "w3", "a0", "a1", "a2", "z1a", "z1b", "aq1", "aq2"]
                      + [x for r in range(NRND) for x in (f"r8a_{r}", f"r8b_{r}")]
                      + [f"{n}_{c}" for c in "AB" for n in ("bgam", "bbk", "bbv")]
                      + [f"{n}_{c}" for c in "AB"
                         for n in ("m4", "ov", "oadd", "kt", "s1", "s2")])
            VT = {n: i + 1 for i, n in enumerate(VNAMES)}
            NVR2 = len(VNAMES)
            SNAMES = (["dexpA", "dexpB", "gam", "etil", "gamC", "yiva", "yivb"]
                      + [x for r in range(NRND)
                         for x in ([f"yeva_{r}", f"yevb_{r}"]
                                   + ([f"zeva_{r}", f"zevb_{r}"] if r < NRND - 1 else []))]
                      + ["wf_A", "uv_A", "wtev_A", "wf_B", "uv_B", "wtev_B",
                         "o1f_A", "s4b_A", "o1f_B", "s4b_B"])
            ST = {n: i + 1 for i, n in enumerate(SNAMES)}
            NSR2 = len(SNAMES)
            MNAMES = (["cumsum", "gtr", "gc", "dargA", "dargB", "kkkq", "tia", "tib"]
                      + [x for r in range(NRND)
                         for x in ([f"p8a_{r}", f"p8b_{r}", f"y8a_{r}", f"y8b_{r}"]
                                   + ([f"t8a_{r}", f"t8b_{r}"] if r < NRND - 1 else []))]
                      + ["up_A", "wtt_A", "up_B", "wtt_B"]
                      + [f"{n}_{c}" for c in "AB"
                         for n in ("m0", "o1g", "o2g", "sng")])
            MT = {n: i + 1 for i, n in enumerate(MNAMES)}
            NMM2 = len(MNAMES)

            def vt(p, n):
                return p * NVR2 + P0 + VT[n]

            def st(p, n):
                return p * NSR2 + ST[n]

            def mt(p, n):
                return p * NMM2 + MT[n]

            def v8(t):
                return t[:].rearrange("p (v w) -> p v w", w=128)

            def colbc8(t8):
                return t8[:].rearrange("p (v o) -> p v o", o=1).broadcast_to([128, 8, 128])

            def colbc4(t8, hf):
                return t8[:, hf * 4:hf * 4 + 4].rearrange(
                    "p (v o) -> p v o", o=1).broadcast_to([128, 4, 128])

            def betacol8(p):
                return beta_t[:, 8 * p:8 * p + 8].rearrange(
                    "p (v o) -> p v o", o=1).broadcast_to([128, 8, 128])

            def betacol4(p, hf):
                return beta_t[:, 8 * p + hf * 4:8 * p + hf * 4 + 4].rearrange(
                    "p (v o) -> p v o", o=1).broadcast_to([128, 4, 128])

            def pairbc(src, w):
                return src.rearrange("p (kh o w) -> p kh o w", o=1, w=w).broadcast_to([128, 2, 2, w])

            def kkbc(off):
                return BB[0][:, off:off + 512].rearrange(
                    "p (c k o w) -> p c k o w", c=2, k=2, o=1, w=128
                ).broadcast_to([128, 2, 2, 2, 128])

            def ckdw(t):
                return t[:].rearrange("p (c k d w) -> p c k d w", c=2, k=2, d=2, w=128)

            def qrow(h2, ci):
                base = h2 * RS + 4
                return qkvA[:, base + ci * 128: base + (ci + 1) * 128]

            def col(h):
                return slice(h * 128, (h + 1) * 128)

            @block.sync
            def _(sy):
                stk.wait(sy, 112)
                for p in range(NP):
                    if p >= 2:
                        cvr.wait(sy, vt(p - 2, "kt_B"))
                    for hf in range(2):
                        cg = 2 * p + hf
                        sy.dma_start_transpose(ktch2[p % 2][:, hf * 256:hf * 256 + 256],
                                               kfm_d[:, cg * 128:(cg + 1) * 128]).then_inc(lkv.sem, 16)
                        sy.dma_start_transpose(vtch2[p % 2][:, hf * 512:hf * 512 + 512],
                                               vfm_d[:, cg * 128:(cg + 1) * 128]).then_inc(lkv.sem, 16)
                        lkv.n += 32

            @block.tensor
            def _(t):
                cvt.wait(t, CVT_END)
                cvn2.wait(t, 64)
                csc2.wait(t, CSC2_END)
                cvr.wait(t, P0)
                for p in range(NP):
                    if p > 0:
                        cvr.wait(t, vt(p - 1, "ov_B"))
                        csr.wait(t, st(p - 1, "o1f_B"))
                    ii = t.matmul(BB[0][:, 0:8], onesUT[:], g_t[:, 8 * p:8 * p + 8],
                                  start=True, stop=True)
                    mmr.inc(ii)
                    cvr.wait(t, vt(p, "gevac"))
                    for h in range(8):
                        ii = t.transpose(BB[0][0:1, col(h)], G2[:, h:h + 1], identf[:])
                    mmr.inc(ii)
                    ii = t.matmul(BB[2][:, 0:8], ones_ff[:], g_t[:, 8 * p:8 * p + 8],
                                  start=True, stop=True)
                    mmr.inc(ii)
                    cvr.wait(t, vt(p, "grow7"))
                    for h in range(8):
                        ii = t.matmul(BB[1][:, col(h)], ones_rf[0:1, :],
                                      grow8[0:1, col(h)], start=True, stop=True)
                    mmr.inc(ii)
                    cvr.wait(t, vt(p, "ngrow7"))
                    if p > 0:
                        cvr.wait(t, vt(p - 1, "s2_B"))
                        csr.wait(t, st(p - 1, "s4b_B"))
                    for h in range(8):
                        ii = t.matmul(BB[3][:, col(h)], ones_rf[0:1, :],
                                      ngrow8[0:1, col(h)], start=True, stop=True)
                    mmr.inc(ii)
                    cvr.wait(t, vt(p, "grow7"))
                    for c in range(2):
                        for kh in range(2):
                            ii = t.matmul(BB[0][:, (c * 2 + kh) * 128:(c * 2 + kh) * 128 + 128],
                                          qrow(2 + kh, 2 * p + c), qrow(2 + kh, 2 * p + c),
                                          start=True, stop=True)
                    for c in range(2):
                        for kh in range(2):
                            ii = t.matmul(BB[0][:, 512 + (c * 2 + kh) * 128:512 + (c * 2 + kh) * 128 + 128],
                                          qrow(2 + kh, 2 * p + c), qrow(kh, 2 * p + c),
                                          start=True, stop=True)
                    mmr.inc(ii)
                    cvr.wait(t, vt(p, "z1a"))
                    cvr.wait(t, vt(p, "dargA1"))
                    if p > 0:
                        csr.wait(t, st(p - 1, "s4b_B"))
                    for v_ in range(4):
                        ii = t.matmul(BB[1][:, col(v_)], Z8[:, col(v_)],
                                      ident8[:, 0:128], start=True, stop=True)
                    mmr.inc(ii)
                    cvr.wait(t, vt(p, "z1b"))
                    for v_ in range(4, 8):
                        ii = t.matmul(BB[1][:, col(v_)], Z8[:, col(v_)],
                                      ident8[:, 0:128], start=True, stop=True)
                    mmr.inc(ii)
                    cvr.wait(t, vt(p, "a2"))
                    for r in range(NRND):
                        for wv, vr in (("a", range(4)), ("b", range(4, 8))):
                            if r == 0:
                                csr.wait(t, st(p, f"yiv{wv}"))
                            else:
                                csr.wait(t, st(p, f"yev{wv}_{r - 1}"))
                            for v_ in vr:
                                ii = t.matmul(BB[2][:, col(v_)], alow8[:, col(v_)],
                                              Y8[:, col(v_)], start=True, stop=True)
                            mmr.inc(ii)
                        for wv, vr, gate in (("a", range(4), "w3"), ("b", range(4, 8), "aq1")):
                            cvr.wait(t, vt(p, f"r8{wv}_{r}"))
                            if r == 0:
                                cvr.wait(t, vt(p, gate))
                            else:
                                csr.wait(t, st(p, f"zev{wv}_{r - 1}"))
                            for v_ in vr:
                                ii = t.matmul(BB[0][:, col(v_)], Z8[:, col(v_)],
                                              R8[:, col(v_)], start=True, stop=True)
                            mmr.inc(ii)
                        if r < NRND - 1:
                            for wv, vr in (("a", range(4)), ("b", range(4, 8))):
                                csr.wait(t, st(p, f"yev{wv}_{r}"))
                                for v_ in vr:
                                    ii = t.matmul(BB[1][:, col(v_)], Y8[:, col(v_)],
                                                  ident8[:, 0:128], start=True, stop=True)
                                mmr.inc(ii)
                    for hf, cn in ((0, "A"), (1, "B")):
                        csr.wait(t, st(p, f"yevb_{NRND - 1}"))
                        cvr.wait(t, vt(p, f"bbv_{cn}"))
                        if hf == 1:
                            csr.wait(t, st(p, "uv_A"))
                        for h in range(4):
                            ii = t.matmul(BB[2][:, h * 256:h * 256 + 256],
                                          Y8[:, col(hf * 4 + h)],
                                          Bblk[hf][:, h * 256:(h + 1) * 256],
                                          start=True, stop=True)
                        mmr.inc(ii)
                        csr.wait(t, st(p, f"wf_{cn}"))
                        for h in range(4):
                            ii = t.matmul(BB[3][:, hf * 512 + h * 128:hf * 512 + h * 128 + 128],
                                          Wb4[hf][:, col(h)], ident8[:, 0:128],
                                          start=True, stop=True)
                        mmr.inc(ii)
                    for hf, cn in ((0, "A"), (1, "B")):
                        cg = 2 * p + hf
                        csr.wait(t, st(p, f"zev{'ab'[hf]}_{NRND - 2}"))
                        csr.wait(t, st(p, f"wtev_{cn}"))
                        if hf == 1:
                            csr.wait(t, st(p, "s4b_A"))
                        for h in range(4):
                            ii = t.matmul(BB[1][:, hf * 512 + h * 128:hf * 512 + h * 128 + 128],
                                          WtT4[hf][:, col(h)], S4b[:, col(h)],
                                          start=True, stop=True)
                        mmr.inc(ii)
                        cvr.wait(t, vt(p, f"m4_{cn}"))
                        if hf == 1:
                            cvr.wait(t, vt(p, "ov_A"))
                        for h in range(4):
                            ii = t.matmul(BB[0][:, hf * 512 + h * 128:hf * 512 + h * 128 + 128],
                                          aqk8[:, col(hf * 4 + h)], M4[:, col(h)],
                                          start=True, stop=True)
                        mmr.inc(ii)
                        oh = 1 - hf
                        if hf == 1:
                            csr.wait(t, st(p, "o1f_A"))
                        for h in range(4):
                            ii = t.matmul(BB[0][:, oh * 512 + h * 128:oh * 512 + h * 128 + 128],
                                          qrow(h // 2, cg), S4b[:, col(h)],
                                          start=True, stop=True)
                        mmr.inc(ii)
                        cvr.wait(t, vt(p, f"kt_{cn}"))
                        for h in range(4):
                            ii = t.matmul(BB[3][:, hf * 512 + h * 128:hf * 512 + h * 128 + 128],
                                          Kt4[:, col(h)], M4[:, col(h)],
                                          start=True, stop=True)
                        mmr.inc(ii)

            @block.vector
            def _(v):
                csn2.wait(v, 64 * 3)
                stk.wait(v, 112)
                cvr.inc(v.memset(S4[:], 0.0))
                cvr.inc(v.memset(S4b[:], 0.0))
                for p in range(NP):
                    def V(instr):
                        cvr.inc(instr)

                    def W():
                        cvr.wait(v)
                    mmr.wait(v, mt(p, "cumsum"))
                    W(); V(v.tensor_copy(G2[:], BB[0][:, 0:8]))
                    W(); V(v.tensor_scalar(negG2[:], G2[:], -1.0, None, ALU.mult))
                    mmr.wait(v, mt(p, "gtr"))
                    for h in range(8):
                        W(); V(v.tensor_copy(grow8[0:1, col(h)], BB[0][0:1, col(h)]))
                    for h in range(8):
                        W(); V(v.tensor_scalar(ngrow8[0:1, col(h)], grow8[0:1, col(h)],
                                               -1.0, None, ALU.mult))
                    mmr.wait(v, mt(p, "gc"))
                    W(); V(v.tensor_copy(GC2[:], BB[2][:, 0:8]))
                    W(); V(v.tensor_tensor(gdiff2[:], GC2[:], negG2[:], ALU.add))
                    mmr.wait(v, mt(p, "dargA"))
                    W(); V(v.tensor_tensor(v8(scrA), v8(BB[1]), colbc8(negG2), ALU.add))
                    W(); V(v.tensor_scalar(scrA[:], scrA[:], 0.0, None, ALU.min))
                    mmr.wait(v, mt(p, "dargB"))
                    W(); V(v.tensor_tensor(v8(scrB), v8(BB[3]), colbc8(G2), ALU.add))
                    W(); V(v.tensor_scalar(scrB[:], scrB[:], 0.0, None, ALU.min))
                    mmr.wait(v, mt(p, "kkkq"))
                    csr.wait(v, st(p, "dexpB"))
                    W(); V(v.tensor_tensor(ckdw(scrC), ckdw(dexp28), kkbc(0), ALU.mult))
                    W(); V(v.tensor_tensor(v8(scrC), v8(scrC), betacol8(p), ALU.mult))
                    W(); V(v.tensor_tensor(a8tmp[:], scrC[:], lmask8[:], ALU.mult))
                    W(); V(v.tensor_tensor(alow8[:], a8tmp[:], ident8[:], ALU.add))
                    W(); V(v.tensor_tensor(Z8[:, 0:512], ident8[:, 0:512],
                                           a8tmp[:, 0:512], ALU.subtract))
                    W(); V(v.tensor_tensor(Z8[:, 512:1024], ident8[:, 512:1024],
                                           a8tmp[:, 512:1024], ALU.subtract))
                    csr.wait(v, st(p, "dexpA"))
                    W(); V(v.tensor_tensor(ckdw(scrC), ckdw(dexp8), kkbc(512), ALU.mult))
                    W(); V(v.tensor_tensor(aqk8[:], scrC[:], imask8[:], ALU.mult))
                    HSL = (("a", slice(0, 512)), ("b", slice(512, 1024)))
                    for r in range(NRND):
                        for wv, sl in HSL:
                            mmr.wait(v, mt(p, f"p8{wv}_{r}"))
                            W(); V(v.tensor_tensor(R8[:, sl], twoI8[:, sl],
                                                   BB[2][:, sl], ALU.subtract))
                    for hf, cn in ((0, "A"), (1, "B")):
                        csr.wait(v, st(p, "gam"))
                        W(); V(v.tensor_tensor(bg4[:], beta_t[:, 8 * p + hf * 4:8 * p + hf * 4 + 4],
                                               gam2[:, hf * 4:hf * 4 + 4], ALU.mult))
                        lkv.wait(v, 64 * (p + 1))
                        bb = Bblk[hf][:].rearrange("p (h z w) -> p h z w", z=2, w=128)
                        bgbc = bg4[:].rearrange("p (v o) -> p v o", o=1).broadcast_to([128, 4, 128])
                        W(); V(v.tensor_tensor(bb[:, :, 0, :],
                                               pairbc(ktch2[p % 2][:, hf * 256:hf * 256 + 256], 128),
                                               bgbc, ALU.mult))
                        vsl = vtch2[p % 2][:, hf * 512:hf * 512 + 512].rearrange("p (h w) -> p h w", w=128)
                        W(); V(v.tensor_tensor(bb[:, :, 1, :], vsl, betacol4(p, hf), ALU.mult))
                    for hf, cn in ((0, "A"), (1, "B")):
                        cg = 2 * p + hf
                        mmr.wait(v, mt(p, f"m0_{cn}"))
                        csr.wait(v, st(p, f"uv_{cn}"))
                        W(); V(v.tensor_tensor(M4[:], Uv4[hf][:], BB[1][:, hf * 512:hf * 512 + 512],
                                               ALU.subtract))
                        mmr.wait(v, mt(p, f"o2g_{cn}"))
                        oh = 1 - hf
                        W(); V(v.tensor_tensor(
                            scrA[:, 0:512].rearrange("p (h w) -> p h w", w=128),
                            BB[0][:, oh * 512:oh * 512 + 512].rearrange("p (h w) -> p h w", w=128),
                            colbc4(gam2, hf), ALU.mult))
                        csr.wait(v, st(p, f"o1f_{cn}"))
                        sto_.wait(v, 16 * max(0, cg - 1))
                        W(); V(v.tensor_tensor(o4t[hf][:], scrA[:, 0:512], o1f[:], ALU.add))
                        csr.wait(v, st(p, "etil"))
                        W(); V(v.tensor_tensor(
                            Kt4[:].rearrange("p (h w) -> p h w", w=128),
                            pairbc(ktch2[p % 2][:, hf * 256:hf * 256 + 256], 128),
                            colbc4(etil2, hf), ALU.mult))
                        csr.wait(v, st(p, "gamC"))
                        W(); V(v.tensor_tensor(
                            scrA[:, 0:512].rearrange("p (h w) -> p h w", w=128),
                            S4[:].rearrange("p (h w) -> p h w", w=128),
                            colbc4(gamC2, hf), ALU.mult))
                        mmr.wait(v, mt(p, f"sng_{cn}"))
                        W(); V(v.tensor_tensor(S4[:], scrA[:, 0:512],
                                               BB[3][:, hf * 512:hf * 512 + 512], ALU.add))

            @block.scalar
            def _(s):
                for p in range(NP):
                    csr.wait(s)
                    cvr.wait(s, vt(p, "dargA2"))
                    csr.inc(s.activation(dexp8[:], scrA[:], AF.Exp))
                    csr.wait(s)
                    cvr.wait(s, vt(p, "dargB2"))
                    csr.inc(s.activation(dexp28[:], scrB[:], AF.Exp))
                    csr.wait(s)
                    csr.inc(s.activation(gam2[:], G2[:], AF.Exp))
                    csr.wait(s)
                    cvr.wait(s, vt(p, "gdiff"))
                    csr.inc(s.activation(etil2[:], gdiff2[:], AF.Exp))
                    csr.wait(s)
                    csr.inc(s.activation(gamC2[:], GC2[:], AF.Exp))
                    SH = (("a", slice(0, 512)), ("b", slice(512, 1024)))
                    for wv, sl in SH:
                        mmr.wait(s, mt(p, f"ti{wv}"))
                        csr.wait(s)
                        csr.inc(s.copy(Y8[:, sl], BB[1][:, sl]))
                    for r in range(NRND):
                        for wv, sl in SH:
                            mmr.wait(s, mt(p, f"y8{wv}_{r}"))
                            csr.wait(s)
                            csr.inc(s.copy(Y8[:, sl], BB[0][:, sl]))
                        if r < NRND - 1:
                            for wv, sl in SH:
                                mmr.wait(s, mt(p, f"t8{wv}_{r}"))
                                csr.wait(s)
                                csr.inc(s.copy(Z8[:, sl], BB[1][:, sl]))
                    b2v = BB[2][:].rearrange("p (h q) -> p h q", q=256)
                    for hf, cn in ((0, "A"), (1, "B")):
                        wfv = Wb4[hf][:].rearrange("p (h w) -> p h w", w=128)
                        uvv = Uv4[hf][:].rearrange("p (h w) -> p h w", w=128)
                        mmr.wait(s, mt(p, f"up_{cn}"))
                        csr.wait(s)
                        csr.inc(s.copy(wfv, b2v[:, :, 0:128]))
                        csr.wait(s)
                        csr.inc(s.copy(uvv, b2v[:, :, 128:256]))
                        mmr.wait(s, mt(p, f"wtt_{cn}"))
                        csr.wait(s)
                        csr.inc(s.copy(WtT4[hf][:], BB[3][:, hf * 512:hf * 512 + 512]))
                    for hf, cn in ((0, "A"), (1, "B")):
                        mmr.wait(s, mt(p, f"o1g_{cn}"))
                        if hf == 1:
                            cvr.wait(s, vt(p, "oadd_A"))
                        csr.wait(s)
                        csr.inc(s.copy(o1f[:], BB[0][:, hf * 512:hf * 512 + 512]))
                        cvr.wait(s, vt(p, f"s2_{cn}"))
                        csr.wait(s)
                        csr.inc(s.copy(S4b[:], S4[:]))

            @block.gpsimd
            def _(g):
                for cg in range(NCH):
                    p, hf = divmod(cg, 2)
                    cvr.wait(g, vt(p, "oadd_" + ("A" if hf == 0 else "B")))
                    g.dma_start(out=o_d[cg * 128:(cg + 1) * 128, :],
                                in_=o4t[hf][:]).then_inc(sto_.sem, 16)
                    sto_.n += 16
            pr.close()


        if stage >= 6:
            pqa.close()
            pg = ExitStack()
            E5 = pg.enter_context
            xnt2 = E5(nc.sbuf_tensor("xnt2", [128, KC * 512], BF16))
            wzs = E5(nc.sbuf_tensor("wzs", [128, KC * 512], BF16))
            nwr = E5(nc.sbuf_tensor("nwr", [1, 128], F32))
            nwb = E5(nc.sbuf_tensor("nwb", [128, 128], F32))
            ot = [E5(nc.sbuf_tensor(f"ot{i}", [128, 512], BF16)) for i in range(2)]
            sz2 = E5(nc.sbuf_tensor("sz2", [128, 1024], F32))
            u8 = E5(nc.sbuf_tensor("u8", [128, 4096], F32))
            yt8 = E5(nc.sbuf_tensor("yt8", [128, 4096], BF16))
            ms8 = E5(nc.sbuf_tensor("ms8", [128, 32], F32))
            rs8 = E5(nc.sbuf_tensor("rs8", [128, 32], F32))
            sq512 = E5(nc.sbuf_tensor("sq512", [128, 512], F32))

            NCSG = 10
            NCVG = 26

            def silutok(c):
                return (c // 8) * NCSG + (c % 8) + 1

            def umtok(c):
                return 1 + (c // 8) * NCVG + (c % 8) * 3 + 1

            def rstok(g_):
                return 1 + g_ * NCVG + 25

            def yttok(g_):
                return 1 + g_ * NCVG + 26

            def szh(j):
                return sz2[:, (j % 2) * 512:(j % 2) * 512 + 512]

            @block.sync
            def _(sy):
                cvr.wait(sy, P0 + NP * NVR2)
                csr.wait(sy, NP * NSR2)
                sy.dma_start(out=nwr[0:1, :], in_=normw[:].unsqueeze(0)).then_inc(lgn.sem, 16)
                lgn.n += 16
                for ci in range(NCH):
                    if ci % 4 == 0:
                        tt = ci // 4
                        if tt >= 1:
                            mmg.wait(sy, tt * 4 + 1)
                        sy.dma_start(out=xnt2[:].rearrange("p (kc t) -> p kc t", t=512),
                                     in_=xnfm[:, tt * 512:(tt + 1) * 512].rearrange("(kc p) t -> p kc t", p=128),
                                     ).then_inc(lgx.sem, 16)
                        lgx.n += 16
                    if ci >= 2:
                        cvg.wait(sy, umtok(ci - 2))
                    sy.dma_start(out=ot[ci % 2][:], in_=o_d[ci * 128:(ci + 1) * 128, :]).then_inc(lgo.sem, 16)
                    lgo.n += 16

            @block.gpsimd
            def _(g):
                sto_.wait(g, 16 * NCH)
                cvr.wait(g, P0 + NP * NVR2)
                csr.wait(g, NP * NSR2)
                g.dma_start(out=wzs[:].rearrange("p (kc n) -> p kc n", n=512),
                            in_=wz.rearrange("(kc p) n -> p kc n", p=128)).then_inc(lgw.sem, 16)
                lgw.n += 16
                for ci in range(NCH):
                    cvg.wait(g, yttok(ci // 8))
                    g.dma_start(out=y1[ci * 128:(ci + 1) * 128, :],
                                in_=yt8[:, (ci % 8) * 512:(ci % 8 + 1) * 512]).then_inc(stg.sem, 16)
                    stg.n += 16
                stg.wait(g)

            @block.tensor
            def _(t):
                lgn.wait(t, 16)
                sto_.wait(t, 16 * NCH)
                cvr.wait(t, P0 + NP * NVR2)
                ii = t.matmul(B[4][:, 0:128], ones_rf[0:1, :], nwr[0:1, :], start=True, stop=True)
                mmg.inc(ii)
                lgw.wait(t, 16)
                for ci in range(NCH):
                    tt = ci // 4
                    lgx.wait(t, 16 * (tt + 1))
                    if ci >= 4:
                        csg.wait(t, silutok(ci - 4))
                    for kc in range(KC):
                        ii = t.matmul(B[ci % 4][:, 0:512],
                                      xnt2[:].rearrange("p (kc t) -> p kc t", t=512)[:, kc, (ci % 4) * 128:(ci % 4 + 1) * 128],
                                      wzs[:].rearrange("p (kc n) -> p kc n", n=512)[:, kc, :],
                                      start=(kc == 0), stop=(kc == KC - 1))
                    mmg.inc(ii)

            @block.vector
            def _(v):
                mmg.wait(v, 1)
                cvg.inc(v.tensor_copy(nwb[:], B[4][:, 0:128]))
                for g_ in range(NCH // 8):
                    for j in range(8):
                        c = g_ * 8 + j
                        csg.wait(v, silutok(c))
                        lgo.wait(v, 16 * (c + 1))
                        cvg.wait(v)
                        usl = u8[:, j * 512:(j + 1) * 512]
                        cvg.inc(v.tensor_tensor(usl, ot[c % 2][:], szh(j), ALU.mult))
                        cvg.wait(v)
                        cvg.inc(v.tensor_tensor(sq512[:], usl, usl, ALU.mult))
                        cvg.wait(v)
                        cvg.inc(v.tensor_reduce(out=ms8[:, 4 * j:4 * j + 4],
                                                in_=sq512[:].rearrange("p (h w) -> p h w", w=128),
                                                axis=mybir.AxisListType.X, op=ALU.add))
                    csg.wait(v, g_ * NCSG + NCSG)
                    uv = u8[:].rearrange("p (j h w) -> p j h w", h=4, w=128)
                    rsbc = rs8[:].rearrange("p (j h o) -> p j h o", h=4, o=1).broadcast_to([128, 8, 4, 128])
                    cvg.wait(v)
                    cvg.inc(v.tensor_tensor(uv, uv, rsbc, ALU.mult))
                    stg.wait(v, 16 * 8 * g_)
                    u32 = u8[:].rearrange("p (s w) -> p s w", w=128)
                    nbc = nwb[:].rearrange("p (o w) -> p o w", o=1).broadcast_to([128, 32, 128])
                    yv = yt8[:].rearrange("p (s w) -> p s w", w=128)
                    cvg.wait(v)
                    cvg.inc(v.tensor_tensor(yv, u32, nbc, ALU.mult))

            @block.scalar
            def _(s):
                for g_ in range(NCH // 8):
                    for j in range(8):
                        c = g_ * 8 + j
                        mmg.wait(s, c + 2)
                        if c >= 2:
                            cvg.wait(s, umtok(c - 2))
                        csg.wait(s)
                        csg.inc(s.activation(szh(j), B[c % 4][:], AF.Silu))
                    cvg.wait(s, 1 + g_ * NCVG + 24)
                    if g_ >= 1:
                        cvg.wait(s, rstok(g_ - 1))
                    csg.wait(s)
                    csg.inc(s.activation(rs8[:], ms8[:], AF.Ln, bias=c_epsr[:, 0:1],
                                         scale=c_i128[:, 0:1]))
                    csg.wait(s)
                    csg.inc(s.activation(rs8[:], rs8[:], AF.Exp, scale=c_nh[:, 0:1]))
            pg.close()

    return nc



def kernel(**inputs):
    C = np.ascontiguousarray
    BFD = ml_dtypes.bfloat16
    x = inputs["L0_attn_x"][0]
    xs = [C(x[c * TPC:(c + 1) * TPC]) for c in range(NCORES)]

    nc = build_lnx(TPC)
    res = run_bass_kernel_spmd(nc, [{"x": xs[c]} for c in range(NCORES)],
                               core_ids=list(range(NCORES)))
    xn0 = np.concatenate([r["xn"] for r in res.results], 0)
    xn0fm = C(xn0.T)

    hd = HD
    nc = build_l0mix()
    ins = []
    for c in range(NCORES):
        j1, j2 = c, c + 8
        ins.append({
            "xnfm": xn0fm,
            "wq1": C(inputs["L0_attn_Wq1"][:, 2 * c * hd:(2 * c + 2) * hd]),
            "wq2": C(inputs["L0_attn_Wq2"][:, 2 * c * hd:(2 * c + 2) * hd]),
            "wk": C(np.concatenate([inputs["L0_attn_Wk"][:, j1 * hd:(j1 + 1) * hd],
                                    inputs["L0_attn_Wk"][:, j2 * hd:(j2 + 1) * hd]], 1)),
            "wv": C(np.concatenate([inputs["L0_attn_Wv"][:, j1 * hd:(j1 + 1) * hd],
                                    inputs["L0_attn_Wv"][:, j2 * hd:(j2 + 1) * hd]], 1)),
            "wlam": C(inputs["L0_attn_Wlam"][:, 2 * c:2 * c + 2]),
        })
    res = run_bass_kernel_spmd(nc, ins, core_ids=list(range(NCORES)))
    y0fm = np.concatenate([r["y0"] for r in res.results], 0)

    nc = build_ffn()
    ins = [{"xnfm": C(xn0fm[:, c * TPC:(c + 1) * TPC]),
            "wg": inputs["L0_ffn_w_gate"], "wu": inputs["L0_ffn_w_up"],
            "wd": inputs["L0_ffn_w_down"]} for c in range(NCORES)]
    res = run_bass_kernel_spmd(nc, ins, core_ids=list(range(NCORES)))
    g0 = [C(r["g"]) for r in res.results]

    nc = build_cmb(2048, True)
    ins = [{"x": xs[c], "yloc": C(y0fm[:, c * TPC:(c + 1) * TPC]),
            "wO": inputs["L0_attn_Wo"], "g": g0[c]} for c in range(NCORES)]
    res = run_bass_kernel_spmd(nc, ins, core_ids=list(range(NCORES)))
    x1 = [C(r["xo"]) for r in res.results]
    xn1 = np.concatenate([r["xn"] for r in res.results], 0)
    xn1fm = C(xn1.T)

    qT = C(inputs["L1_attn_W_qkv"].T)
    zT = C(inputs["L1_attn_W_z"].T)
    bT = C(inputs["L1_attn_W_b"].T)
    aT = C(inputs["L1_attn_W_a"].T)
    cw = inputs["L1_attn_conv_w"][:, 0, :]
    z4 = np.zeros(4, np.float32)
    nc = build_l1mix()
    ins = []
    for c in range(NCORES):
        ins.append({
            "xnfm": xn1fm,
            "wq": C(qT[:, 256 * c:256 * c + 256]),
            "wk": C(qT[:, 2048 + 256 * c:2048 + 256 * c + 256]),
            "wv": C(qT[:, 4096 + 512 * c:4096 + 512 * c + 512]),
            "wz": C(zT[:, 512 * c:512 * c + 512]),
            "wba": C(np.concatenate([bT[:, 4 * c:4 * c + 4], aT[:, 4 * c:4 * c + 4]], 1)),
            "convw": C(np.concatenate([cw[256 * c:256 * c + 256],
                                       cw[2048 + 256 * c:2048 + 256 * c + 256],
                                       cw[4096 + 512 * c:4096 + 512 * c + 512]], 0)),
            "dtb8": np.concatenate([z4, inputs["L1_attn_dt_bias"][4 * c:4 * c + 4]]),
            "alog8": np.concatenate([z4, inputs["L1_attn_A_log"][4 * c:4 * c + 4]]),
            "normw": inputs["L1_attn_norm_w"],
        })
    res = run_bass_kernel_spmd(nc, ins, core_ids=list(range(NCORES)))
    y1 = np.concatenate([r["y1"] for r in res.results], 1)
    y1fm = C(y1.T)

    nc = build_ffn()
    ins = [{"xnfm": C(xn1fm[:, c * TPC:(c + 1) * TPC]),
            "wg": inputs["L1_ffn_w_gate"], "wu": inputs["L1_ffn_w_up"],
            "wd": inputs["L1_ffn_w_down"]} for c in range(NCORES)]
    res = run_bass_kernel_spmd(nc, ins, core_ids=list(range(NCORES)))
    g1 = [C(r["g"]) for r in res.results]

    woT = C(inputs["L1_attn_W_out"].T)
    nc = build_cmb(4096, False)
    ins = [{"x": x1[c], "yloc": C(y1fm[:, c * TPC:(c + 1) * TPC]),
            "wO": woT, "g": g1[c]} for c in range(NCORES)]
    res = run_bass_kernel_spmd(nc, ins, core_ids=list(range(NCORES)))
    x2 = np.concatenate([r["xo"] for r in res.results], 0)
    return x2[None]



# version 13
# speedup vs baseline: 1.1568x; 1.1568x over previous
import math
from contextlib import ExitStack

import numpy as np
import ml_dtypes
import concourse.bass as bass
import concourse.mybir as mybir
from concourse.bass_utils import run_bass_kernel_spmd

F32 = mybir.dt.float32
BF16 = mybir.dt.bfloat16
AF = mybir.ActivationFunctionType
ALU = mybir.AluOpType
AX = mybir.AxisListType

NCORES = 8
NTT0 = 16
T = 8192
D = 2048
TPC = T // NCORES
HD = 128
FFI = 8192
LN_EPS = 1e-5


class Ch:

    def __init__(self, sem):
        self.sem = sem
        self.n = 0
        self._waited = {}

    def inc(self, instr, k=1):
        instr.then_inc(self.sem, k)
        self.n += k
        return self.n

    def wait(self, eng, n=None):
        n = self.n if n is None else n
        if n <= 0:
            return
        key = eng.engine
        if self._waited.get(key, 0) >= n:
            return
        eng.wait_ge(self.sem, n)
        self._waited[key] = n


def chans(ctx, nc, *names):
    return [Ch(ctx.enter_context(nc.semaphore(n))) for n in names]


class Prelude:

    def __init__(self, nc, ctx, block):
        self.nc = nc
        self.ctx = ctx
        self.items = []
        self.ch = Ch(ctx.enter_context(nc.semaphore("c_init")))
        self._emitted = False

    def add(self, name, arr, dtype):
        h = self.ctx.enter_context(self.nc.sbuf_tensor("c_" + name, list(arr.shape), dtype))
        t = self.nc.inline_tensor(np.ascontiguousarray(arr), "ct_" + name).ap()
        self.items.append((name, t, h))
        return h

    def scalar_col(self, name, value):
        return self.add(name, np.full((128, 1), value, np.float32), F32)

    def emit_loads(self, eng, ch_self=None):
        for (_, t, h) in self.items:
            eng.dma_start(out=h[:], in_=t).then_inc(self.ch.sem, 16)
            self.ch.n += 16
        self._emitted = True

    def wait(self, eng):
        self.ch.wait(eng)



def build_lnx(rows):
    nc = bass.Bass("TRN2", target_bir_lowering=False)
    x = nc.dram_tensor("x", [rows, D], F32, kind="ExternalInput").ap()
    xn = nc.dram_tensor("xn", [rows, D], BF16, kind="ExternalOutput").ap()
    nt = rows // 128
    with ExitStack() as ctx:
        E = ctx.enter_context
        xt = [E(nc.sbuf_tensor(f"xt{i}", [128, D], F32)) for i in range(2)]
        xo = [E(nc.sbuf_tensor(f"xo{i}", [128, D], BF16)) for i in range(2)]
        sq = E(nc.sbuf_tensor("sq", [128, D], F32))
        ssum = E(nc.sbuf_tensor("ssum", [128, 1], F32))
        ssq = E(nc.sbuf_tensor("ssq", [128, 1], F32))
        mu = E(nc.sbuf_tensor("mu", [128, 1], F32))
        nmu = E(nc.sbuf_tensor("nmu", [128, 1], F32))
        var = E(nc.sbuf_tensor("var", [128, 1], F32))
        rstd = E(nc.sbuf_tensor("rstd", [128, 1], F32))
        ld, cv, cs, st = chans(ctx, nc, "ld", "cv", "cs", "st")
        block = E(nc.Block())
        pre = Prelude(nc, ctx, block)
        c_eps = pre.scalar_col("eps", LN_EPS)
        c_nh = pre.scalar_col("nh", -0.5)

        VOPT = 6
        SOPT = 3

        @block.sync
        def _(sync):
            pre.emit_loads(sync)
            for i in range(nt):
                if i >= 2:
                    st.wait(sync, 16 * (i - 1))
                sync.dma_start(out=xt[i % 2][:], in_=x[i * 128:(i + 1) * 128, :]).then_inc(ld.sem, 16)
                ld.n += 16

        @block.scalar
        def _(s):
            pre.wait(s)
            for i in range(nt):
                ld.wait(s, 16 * (i + 1))
                cv.wait(s, i * VOPT)
                cs.wait(s)
                cs.inc(s.activation(sq[:], xt[i % 2][:], AF.Square, accum_out=ssq[:, 0:1]))
                cv.wait(s, i * VOPT + 5)
                cs.wait(s)
                cs.inc(s.activation(var[:], var[:], AF.Ln, bias=c_eps[:, 0:1]))
                cs.wait(s)
                cs.inc(s.activation(rstd[:], var[:], AF.Exp, scale=c_nh[:, 0:1]))

        @block.vector
        def _(v):
            for i in range(nt):
                ld.wait(v, 16 * (i + 1))
                cs.wait(v, i * SOPT + 1)
                cv.wait(v)
                cv.inc(v.reduce_sum(ssum[:], xt[i % 2][:], axis=AX.X))
                cv.wait(v)
                cv.inc(v.tensor_scalar(mu[:], ssum[:], 1.0 / D, None, ALU.mult))
                cv.wait(v)
                cv.inc(v.tensor_scalar(nmu[:], mu[:], -1.0, None, ALU.mult))
                cv.wait(v)
                cv.inc(v.tensor_tensor(var[:], mu[:], nmu[:], ALU.mult))
                cv.wait(v)
                cv.inc(v.scalar_tensor_tensor(var[:], ssq[:], 1.0 / D, var[:], ALU.mult, ALU.add))
                cs.wait(v, i * SOPT + 3)
                cv.wait(v)
                cv.inc(v.scalar_tensor_tensor(
                    xo[i % 2][:], xt[i % 2][:], nmu[:, 0:1], rstd[:, 0:1].broadcast_to([128, D]),
                    ALU.add, ALU.mult))

        @block.gpsimd
        def _(g):
            for i in range(nt):
                cv.wait(g, i * VOPT + 6)
                g.dma_start(out=xn[i * 128:(i + 1) * 128, :], in_=xo[i % 2][:]).then_inc(st.sem, 16)
                st.n += 16
            st.wait(g)
    return nc


def build_ffn():
    nc = bass.Bass("TRN2", target_bir_lowering=False)
    xnfm = nc.dram_tensor("xnfm", [D, TPC], BF16, kind="ExternalInput").ap()
    wg = nc.dram_tensor("wg", [D, FFI], F32, kind="ExternalInput").ap()
    wu = nc.dram_tensor("wu", [D, FFI], F32, kind="ExternalInput").ap()
    wd = nc.dram_tensor("wd", [FFI, D], F32, kind="ExternalInput").ap()
    g_out = nc.dram_tensor("g", [TPC, D], F32, kind="ExternalOutput").ap()
    hsp_d = nc.dram_tensor("hsp", [TPC, FFI], BF16).ap()

    KC = D // 128
    NT = FFI // 512
    MT = TPC // 128

    with ExitStack() as ctx:
        E = ctx.enter_context
        block = E(nc.Block())
        ld, lw, mmc, ca, cv, st = chans(ctx, nc, "ld", "lw", "mmc", "ca", "cv", "st")
        ld2, lw2, mm2, ca2, st2 = chans(ctx, nc, "ld2", "lw2", "mm2", "ca2", "st2")

        p1v = ExitStack()
        E1v = p1v.enter_context
        silu_t = [E1v(nc.sbuf_tensor(f"silu_t{i}", [128, 512], F32)) for i in range(2)]
        h_t = E1v(nc.sbuf_tensor("h_t", [128, 512], BF16))
        hs_t = [E1v(nc.sbuf_tensor(f"hs_t{i}", [128, 512], BF16)) for i in range(2)]
        a_t = E1v(nc.sbuf_tensor("a_t", [128, 512], BF16))
        m1 = E1v(nc.sbuf_tensor("m1", [128, 128], BF16))
        n1 = E1v(nc.sbuf_tensor("n1", [128, 128], BF16))
        m2 = E1v(nc.sbuf_tensor("m2", [128, 128], BF16))
        n2 = E1v(nc.sbuf_tensor("n2", [128, 128], BF16))
        t2 = E1v(nc.sbuf_tensor("t2", [128, 128], BF16))
        msk = E1v(nc.sbuf_tensor("msk", [128, 128], BF16))
        hfmB = E1v(nc.sbuf_tensor("hfmB", [128, 16 * TPC], BF16, is_transpose_dest=True))

        def hfmB_v(cc):
            return hfmB[:].rearrange("p (cc t) -> p cc t", t=TPC)[:, cc, :]

        p1 = ExitStack()
        E1 = p1.enter_context
        xnsb = E1(nc.sbuf_tensor("xnsb", [128, KC * TPC], BF16))
        wgsb = [E1(nc.sbuf_tensor(f"wgsb{i}", [128, KC * 512], BF16)) for i in range(2)]
        wusb = [E1(nc.sbuf_tensor(f"wusb{i}", [128, KC * 512], BF16)) for i in range(2)]
        psg = [E1(nc.psum_tensor(f"psg{i}", [128, 512], F32)) for i in range(2)]
        psu = [E1(nc.psum_tensor(f"psu{i}", [128, 512], F32)) for i in range(2)]

        VOPT1 = 17

        def wv(sb, buf, kc):
            return sb[buf][:].rearrange("p (kc n) -> p kc n", n=512)[:, kc, :]

        def xn_v(kc, mt):
            return xnsb[:].rearrange("p (kc t) -> p kc t", t=TPC)[:, kc, mt * 128:(mt + 1) * 128]

        @block.gpsimd
        def _(g):
            g.dma_start(out=xnsb[:].rearrange("p (kc t) -> p kc t", t=TPC),
                        in_=xnfm.rearrange("(kc p) t -> p kc t", p=128)).then_inc(ld.sem, 16)
            ld.n += 16
            for nt in range(NT):
                if nt >= 2:
                    mmc.wait(g, (nt - 1) * MT)
                for (w_, sb_) in ((wg, wgsb), (wu, wusb)):
                    g.dma_start(
                        out=sb_[nt % 2][:].rearrange("p (kc n) -> p kc n", n=512),
                        in_=w_[:, nt * 512:(nt + 1) * 512].rearrange("(kc p) n -> p kc n", p=128),
                    ).then_inc(lw.sem, 16)
                    lw.n += 16

        @block.tensor
        def _(t):
            ld.wait(t, 16)
            for nt in range(NT):
                lw.wait(t, 32 * (nt + 1))
                for mt in range(MT):
                    i = nt * MT + mt
                    ca.wait(t, max(0, i - 1))
                    cv.wait(t, max(0, (i - 2) * VOPT1 + 1))
                    for kc in range(KC):
                        t.matmul(psg[i % 2][:], xn_v(kc, mt), wv(wgsb, nt % 2, kc),
                                 start=(kc == 0), stop=(kc == KC - 1))
                    for kc in range(KC):
                        ii = t.matmul(psu[i % 2][:], xn_v(kc, mt), wv(wusb, nt % 2, kc),
                                      start=(kc == 0), stop=(kc == KC - 1))
                    mmc.inc(ii)

        @block.scalar
        def _(s):
            for nt in range(NT):
                for mt in range(MT):
                    i = nt * MT + mt
                    mmc.wait(s, i + 1)
                    cv.wait(s, max(0, (i - 2) * VOPT1 + 1))
                    ca.wait(s)
                    ca.inc(s.activation(silu_t[i % 2][:], psg[i % 2][:], AF.Silu))

        @block.vector
        def _(v):
            for nt in range(NT):
                for mt in range(MT):
                    i = nt * MT + mt
                    base = i * VOPT1
                    ca.wait(v, i + 1)
                    mmc.wait(v, i + 1)
                    def V(instr):
                        cv.inc(instr)
                    cv.wait(v)
                    V(v.tensor_tensor(h_t[:], silu_t[i % 2][:], psu[i % 2][:], ALU.mult))
                    cv.wait(v)
                    V(v.tensor_tensor(a_t[:], h_t[:], h_t[:], ALU.mult))
                    av = a_t[:].rearrange("p (g f) -> p g f", f=4)
                    hv = h_t[:].rearrange("p (g f) -> p g f", f=4)
                    hsv = hs_t[i % 2][:].rearrange("p (g f) -> p g f", f=4)
                    cv.wait(v)
                    V(v.tensor_tensor(m1[:], av[:, :, 0], av[:, :, 1], ALU.max))
                    V(v.tensor_tensor(n1[:], av[:, :, 0], av[:, :, 1], ALU.min))
                    V(v.tensor_tensor(m2[:], av[:, :, 2], av[:, :, 3], ALU.max))
                    cv.wait(v)
                    V(v.tensor_tensor(n2[:], av[:, :, 2], av[:, :, 3], ALU.min))
                    cv.wait(v)
                    V(v.tensor_tensor(m1[:], m1[:], m2[:], ALU.min))
                    V(v.tensor_tensor(n1[:], n1[:], n2[:], ALU.max))
                    cv.wait(v)
                    V(v.tensor_tensor(t2[:], m1[:], n1[:], ALU.max))
                    st.wait(v, 16 * max(0, i - 1))
                    for j in range(4):
                        cv.wait(v)
                        V(v.tensor_tensor(msk[:], av[:, :, j], t2[:], ALU.is_ge))
                        cv.wait(v)
                        V(v.tensor_tensor(hsv[:, :, j], hv[:, :, j], msk[:], ALU.mult))

        @block.sync
        def _(sy):
            for nt in range(NT):
                for mt in range(MT):
                    i = nt * MT + mt
                    cv.wait(sy, (i + 1) * VOPT1)
                    sy.dma_start(out=hsp_d[mt * 128:(mt + 1) * 128, nt * 512:(nt + 1) * 512],
                                 in_=hs_t[i % 2][:]).then_inc(st.sem, 16)
                    st.n += 16
                if nt < 4:
                    st.wait(sy, 16 * MT * (nt + 1))
                    for cc in range(4 * nt, 4 * nt + 4):
                        sy.dma_start_transpose(hfmB_v(cc),
                                               hsp_d[:, cc * 128:(cc + 1) * 128]).then_inc(ld2.sem, 16)
                        ld2.n += 16

        p1.close()

        p2 = ExitStack()
        E2 = p2.enter_context
        hfmA = E2(nc.sbuf_tensor("hfmA", [128, 48 * TPC], BF16, is_transpose_dest=True))
        wdsb = [E2(nc.sbuf_tensor(f"wdsb{i}", [128, 16 * 512], BF16)) for i in range(2)]
        gsb = E2(nc.sbuf_tensor("gsb", [128, MT * 512], F32))
        psd = [E2(nc.psum_tensor(f"psd{i}", [128, 512], F32)) for i in range(MT)]

        def hfm_v(cc):
            t_, c_ = (hfmB, cc) if cc < 16 else (hfmA, cc - 16)
            return t_[:].rearrange("p (cc t) -> p cc t", t=TPC)[:, c_, :]

        def hfm_mt(cc, mt):
            t_, c_ = (hfmB, cc) if cc < 16 else (hfmA, cc - 16)
            return t_[:].rearrange("p (cc t) -> p cc t", t=TPC)[:, c_, mt * 128:(mt + 1) * 128]

        def wd_v(buf, j):
            return wdsb[buf][:].rearrange("p (j n) -> p j n", n=512)[:, j, :]

        @block.scalar
        def _(s):
            for cc in range(16, 64):
                st.wait(s, 16 * MT * (cc // 4 + 1))
                s.dma_start_transpose(hfm_v(cc), hsp_d[:, cc * 128:(cc + 1) * 128]).then_inc(ld2.sem, 16)
                ld2.n += 16
            for oc in range(4):
                for mt in range(MT):
                    i = oc * MT + mt
                    mm2.wait(s, oc * 4 + 4)
                    ca2.wait(s)
                    ca2.inc(s.activation(
                        gsb[:].rearrange("p (mt n) -> p mt n", n=512)[:, mt, :],
                        psd[mt][:], AF.Silu))

        @block.gpsimd
        def _(g):
            for oc in range(4):
                for ccg in range(4):
                    it = oc * 4 + ccg
                    if it >= 2:
                        mm2.wait(g, it - 1)
                    g.dma_start(
                        out=wdsb[it % 2][:].rearrange("p (j n) -> p j n", n=512),
                        in_=wd[ccg * 2048:(ccg + 1) * 2048, oc * 512:(oc + 1) * 512]
                            .rearrange("(j p) n -> p j n", p=128),
                    ).then_inc(lw2.sem, 16)
                    lw2.n += 16

        @block.tensor
        def _(t):
            for oc in range(4):
                ca2.wait(t, oc * MT)
                for ccg in range(4):
                    it = oc * 4 + ccg
                    ld2.wait(t, 16 * 16 * (ccg + 1))
                    lw2.wait(t, 16 * (it + 1))
                    for mt in range(MT):
                        for j in range(16):
                            cc = ccg * 16 + j
                            ii = t.matmul(psd[mt][:], hfm_mt(cc, mt), wd_v(it % 2, j),
                                          start=(ccg == 0 and j == 0), stop=(ccg == 3 and j == 15))
                    mm2.inc(ii)

        @block.sync
        def _(sy):
            for oc in range(4):
                for mt in range(MT):
                    i = oc * MT + mt
                    ca2.wait(sy, i + 1)
                    sy.dma_start(out=g_out[mt * 128:(mt + 1) * 128, oc * 512:(oc + 1) * 512],
                                 in_=gsb[:].rearrange("p (mt n) -> p mt n", n=512)[:, mt, :]
                                 ).then_inc(st2.sem, 16)
                    st2.n += 16
            st2.wait(sy)
        p2.close()
    return nc





def build_l0mix(nqt=NTT0):
    nc = bass.Bass("TRN2", target_bir_lowering=False)
    xnfm = nc.dram_tensor("xnfm", [D, T], BF16, kind="ExternalInput").ap()
    wq1 = nc.dram_tensor("wq1", [D, 256], F32, kind="ExternalInput").ap()
    wq2 = nc.dram_tensor("wq2", [D, 256], F32, kind="ExternalInput").ap()
    wk = nc.dram_tensor("wk", [D, 256], F32, kind="ExternalInput").ap()
    wv = nc.dram_tensor("wv", [D, 256], F32, kind="ExternalInput").ap()
    wlam = nc.dram_tensor("wlam", [D, 2], F32, kind="ExternalInput").ap()
    y0 = nc.dram_tensor("y0", [256, T], BF16, kind="ExternalOutput").ap()

    KC = D // 128
    NTT = T // 512
    SC = 1.0 / math.sqrt(128.0)
    import ml_dtypes as _md

    ASEQ = []
    for qt in range(nqt):
        for hp in range(2):
            for ktg in range(qt + 1):
                for which in range(2):
                    ASEQ.append((qt, hp, ktg, which))
    CSEQ = [(qt, hp) for qt in range(nqt) for hp in range(2)]

    with ExitStack() as ctx:
        E = ctx.enter_context
        block = E(nc.Block())
        pre = Prelude(nc, ctx, block)
        c_sc = pre.scalar_col("sc", SC)
        c_h = pre.scalar_col("h", 0.5)
        c_m1 = pre.scalar_col("m1", -1.0)
        ones_cf = pre.add("ones_cf", np.ones((128, 1), np.float32), F32)
        ones_rf = pre.add("ones_rf", np.ones((1, 128), np.float32), F32)
        ones_bb = pre.add("ones_bb", np.ones((128, 128), _md.bfloat16), BF16)
        identf = pre.add("identf", np.eye(128, dtype=np.float32), F32)
        dm = np.zeros((128, 2048), np.float32)
        for j in range(4):
            kk, qq = np.meshgrid(np.arange(128), np.arange(512), indexing="ij")
            dm[:, j * 512:(j + 1) * 512] = (128 * j + kk <= qq)
        dmask = pre.add("dmask", dm.astype(_md.bfloat16), BF16)

        lx, lw_, mmp, cve, csq, mmn, cbs = chans(ctx, nc, "lx", "lw", "mmp", "cve", "csq", "mmn", "cbs")
        ca, cvm, cvc, sty, cls = chans(ctx, nc, "ca", "cvm", "cvc", "sty", "cls")

        q1sb = E(nc.sbuf_tensor("q1sb", [128, 2 * T], BF16))
        q2sb = E(nc.sbuf_tensor("q2sb", [128, 2 * T], BF16))
        ksb = E(nc.sbuf_tensor("ksb", [128, 2 * T], BF16))
        vtok = E(nc.sbuf_tensor("vtok", [128, (T // 128) * 256], BF16))
        lam_t = E(nc.sbuf_tensor("lam_t", [128, (T // 128) * 2], F32))
        lamrow = E(nc.sbuf_tensor("lamrow", [1, 512], F32))
        nB = E(nc.sbuf_tensor("nB", [128, 1], F32))
        bmx = E(nc.sbuf_tensor("bmx", [1, 2], F32))
        red1 = E(nc.sbuf_tensor("red1", [1, 1], F32))
        e11 = E(nc.sbuf_tensor("e11", [1, 1], F32))
        sqs = E(nc.sbuf_tensor("sqs", [128, 512], F32))
        psb = [E(nc.sbuf_tensor(f"psb{i}", [128, 2048], BF16)) for i in range(2)]
        b1sb = E(nc.sbuf_tensor("b1sb", [128, 512], F32))
        b2sb = E(nc.sbuf_tensor("b2sb", [128, 512], F32))
        t1sb = E(nc.sbuf_tensor("t1sb", [128, 512], F32))
        w2sb = E(nc.sbuf_tensor("w2sb", [128, 512], F32))
        ytile = [E(nc.sbuf_tensor(f"ytile{i}", [128, 512], BF16)) for i in range(2)]

        S_ps = [E(nc.psum_tensor(f"S_ps{i}", [128, 1024], F32)) for i in range(2)]
        O_ps = [E(nc.psum_tensor(f"O_ps{i}", [128, 512], F32)) for i in range(2)]
        l_ps = [E(nc.psum_tensor(f"l_ps{i}", [128, 512], F32)) for i in range(2)]

        def psq(oi):
            if oi < 2:
                return S_ps[0][:, oi * 512:(oi + 1) * 512]
            if oi < 4:
                return S_ps[1][:, (oi - 2) * 512:(oi - 1) * 512]
            return O_ps[oi - 4][:, 0:512]

        def psv(sub):
            return l_ps[sub % 2][:, 0:258]

        def psn(i):
            return l_ps[i % 2][0:1, 0:512]

        def sqb(i):
            return (sqs if i % 2 == 0 else b1sb)[:]

        def q1v(h, sl):
            return q1sb[:].rearrange("p (h t) -> p h t", t=T)[:, h, sl]

        def q2v(h, sl):
            return q2sb[:].rearrange("p (h t) -> p h t", t=T)[:, h, sl]

        def kvv(h, sl):
            return ksb[:].rearrange("p (h t) -> p h t", t=T)[:, h, sl]

        def vtv(i128, h):
            return vtok[:].rearrange("p (i c) -> p i c", c=256)[:, i128, h * 128:(h + 1) * 128]

        def lamtv(i128):
            return lam_t[:].rearrange("p (i c) -> p i c", c=2)[:, i128, :]

        pp = ExitStack()
        E1 = pp.enter_context
        xnt = [E1(nc.sbuf_tensor("xnt0", [128, KC * 512], BF16))]
        wsb = {nm: E1(nc.sbuf_tensor(nm + "s", [128, KC * 256], BF16))
               for nm in ("wq1", "wq2", "wk")}
        wvlsb = E1(nc.sbuf_tensor("wvls", [128, KC * 258], BF16))

        def xv(kc, sl):
            return xnt[0][:].rearrange("p (kc t) -> p kc t", t=512)[:, kc, sl]

        def wvw(nm, kc, sl):
            return wsb[nm][:].rearrange("p (kc n) -> p kc n", n=256)[:, kc, sl]

        OUTS = [("wq1", 0, q1v), ("wq1", 1, q1v), ("wq2", 0, q2v), ("wq2", 1, q2v),
                ("wk", 0, kvv), ("wk", 1, kvv)]
        GPT = 10
        EVT = 26
        NPROJ_MM = NTT * GPT
        CVE_P = 1 + NTT * EVT
        CVE_MS = CVE_P + 1

        def cve_red(i):
            return CVE_MS + (i + 1) * 2

        CSQ_SQ = 96
        CSQ_LN = CSQ_SQ + 1
        CSQ_EXP = CSQ_LN + 1
        MMN_NORM = 96
        MMN_B = MMN_NORM + 1
        CBS_RED = 1
        CBS_NEG = 2
        CBS_NB = 3
        NCOMB = 12
        GSEQ = []
        _pending = None
        for _idx, (_qt, _hp, _ktg, _wh) in enumerate(ASEQ):
            GSEQ.append(("qk", _idx, 0))
            GSEQ.append(("qk", _idx, 1))
            if _pending is not None:
                GSEQ.append(("pv", _pending))
                _pending = None
            if _ktg == _qt and _wh == 1:
                GSEQ.append(("pv", _idx))
            else:
                _pending = _idx
        if _pending is not None:
            GSEQ.append(("pv", _pending))
        QKGRP = {}
        PVGRP = {}
        for _g, item in enumerate(GSEQ):
            if item[0] == "qk":
                QKGRP[(item[1], item[2])] = _g + 1
            else:
                PVGRP[item[1]] = _g + 1

        @block.sync
        def _(sy):
            pre.emit_loads(sy)
            for tt in range(NTT):
                if tt >= 1:
                    mmp.wait(sy, tt * GPT)
                sy.dma_start(
                    out=xnt[0][:].rearrange("p (kc t) -> p kc t", t=512),
                    in_=xnfm[:, tt * 512:(tt + 1) * 512].rearrange("(kc p) t -> p kc t", p=128),
                ).then_inc(lx.sem, 16)
                lx.n += 16

        @block.gpsimd
        def _(g):
            for nm, src in (("wq1", wq1), ("wq2", wq2), ("wk", wk)):
                g.dma_start(out=wsb[nm][:].rearrange("p (kc n) -> p kc n", n=256),
                            in_=src.rearrange("(kc p) n -> p kc n", p=128)).then_inc(lw_.sem, 16)
                lw_.n += 16
            g.dma_start(out=wvlsb[:].rearrange("p (kc n) -> p kc n", n=258)[:, :, 0:256],
                        in_=wv.rearrange("(kc p) n -> p kc n", p=128)).then_inc(lw_.sem, 16)
            lw_.n += 16
            g.dma_start(out=wvlsb[:].rearrange("p (kc n) -> p kc n", n=258)[:, :, 256:258],
                        in_=wlam.rearrange("(kc p) n -> p kc n", p=128)).then_inc(lw_.sem, 16)
            lw_.n += 16
            for ci in range(len(CSEQ)):
                qt, hp = CSEQ[ci]
                cvc.wait(g, ci * NCOMB + NCOMB)
                g.dma_start(out=y0[hp * 128:(hp + 1) * 128, qt * 512:(qt + 1) * 512],
                            in_=ytile[ci % 2][:]).then_inc(sty.sem, 16)
                sty.n += 16
            sty.wait(g)

        @block.tensor
        def _(t):
            lw_.wait(t, 16 * 5)
            pre.wait(t)
            for tt in range(NTT):
                lx.wait(t, 16 * (tt + 1))
                cve.wait(t, max(0, (tt - 1) * EVT + EVT) + (1 if tt > 0 else 0))
                for oi, (nm, h, _) in enumerate(OUTS):
                    for kc in range(KC):
                        ii = t.matmul(psq(oi), wvw(nm, kc, slice(h * 128, (h + 1) * 128)),
                                      xv(kc, slice(0, 512)),
                                      start=(kc == 0), stop=(kc == KC - 1))
                    mmp.inc(ii)
                for sub in range(4):
                    if sub >= 2:
                        cve.wait(t, 1 + tt * EVT + 6 + 5 * sub)
                    for kc in range(KC):
                        ii = t.matmul(psv(sub),
                                      xv(kc, slice(sub * 128, (sub + 1) * 128)),
                                      wvlsb[:].rearrange("p (kc n) -> p kc n", n=258)[:, kc, :],
                                      start=(kc == 0), stop=(kc == KC - 1))
                    mmp.inc(ii)
            for i in range(96):
                csq.wait(t, i + 1)
                cve.wait(t, max(CVE_MS, CVE_MS + 2 * i - 2))
                ii = t.matmul(psn(i), ones_cf[:, 0:1], sqb(i), start=True, stop=True)
                mmn.inc(ii)
            cbs.wait(t, CBS_NEG)
            ii = t.matmul(l_ps[0][:, 0:1], ones_rf[0:1, :], e11[0:1, 0:1], start=True, stop=True)
            mmn.inc(ii)
            cbs.wait(t, CBS_NB)
            prev_combine_ci = None

            def emit_pv(idx):
                qt, hp, ktg, which = ASEQ[idx]
                first = ktg == 0
                last = ktg == qt
                ci = CSEQ.index((qt, hp))
                ca.wait(t, 2 * (idx + 1))
                if ktg == qt:
                    cvm.wait(t, 2 * sum(1 for a in ASEQ[:idx + 1] if a[2] == a[0]))
                if first:
                    cvc.wait(t, max(0, (ci - 1) * NCOMB + NCOMB))
                    cls.wait(t, max(0, (ci - 1) * 7 + 7))
                for j in range(4):
                    kt = ktg * 4 + j
                    t.matmul(l_ps[which][:, 0:512], ones_bb[:],
                             psb[which][:, j * 512:(j + 1) * 512],
                             start=(first and j == 0), stop=(last and j == 3))
                    ii = t.matmul(O_ps[which][:, 0:512], vtv(kt, which),
                                  psb[which][:, j * 512:(j + 1) * 512],
                                  start=(first and j == 0), stop=(last and j == 3))
                mmp.inc(ii)

            pending = None
            for idx, (qt, hp, ktg, which) in enumerate(ASEQ):
                qsrc = q1v if which == 0 else q2v
                ci = CSEQ.index((qt, hp))
                for half in range(2):
                    ca.wait(t, max(0, 2 * idx + half - 1))
                    if prev_combine_ci is not None and half == 0:
                        cvc.wait(t, prev_combine_ci * NCOMB + 1)
                        prev_combine_ci = None
                    for j in range(2):
                        kt = ktg * 4 + half * 2 + j
                        ii = t.matmul(S_ps[half][:, j * 512:(j + 1) * 512],
                                      kvv(which, slice(kt * 128, (kt + 1) * 128)),
                                      qsrc(hp, slice(qt * 512, (qt + 1) * 512)),
                                      start=True, stop=True)
                    mmp.inc(ii)
                if pending is not None:
                    emit_pv(pending)
                    pending = None
                if not (ktg == qt and which == 1):
                    pending = idx
                    continue
                emit_pv(idx)
                if True:
                    ca.wait(t, 2 * (idx + 1))
                    cvc.wait(t, max(0, (ci - 1) * NCOMB + 1))
                    for j in range(4):
                        ii = t.transpose(S_ps[0][0:1, 512 + j * 128:512 + (j + 1) * 128],
                                         lamtv(qt * 4 + j)[:, hp:hp + 1], identf[:])
                    mmn.inc(ii)
                    cls.wait(t, ci * 7 + 4)
                    cvc.wait(t, ci * NCOMB + 3)
                    ii = t.matmul(l_ps[0][:, 0:512], ones_rf[0:1, :], lamrow[0:1, :], start=True, stop=True)
                    mmn.inc(ii)
                    prev_combine_ci = ci
            if pending is not None:
                emit_pv(pending)

        @block.scalar
        def _(s):
            pre.wait(s)
            cve.wait(s, CVE_P)
            for i in range(96):
                oi, tt = i % 6, i // 6
                nm, h, view = OUTS[oi]
                mmn.wait(s, max(0, i - 1))
                csq.wait(s)
                csq.inc(s.activation(sqb(i), view(h, slice(tt * 512, (tt + 1) * 512)), AF.Square))
            cve.wait(s, cve_red(95))
            csq.wait(s)
            csq.inc(s.activation(bmx[0:1, 0:2], bmx[0:1, 0:2], AF.Ln))
            cbs.wait(s, CBS_RED)
            csq.wait(s)
            csq.inc(s.activation(e11[0:1, 0:1], red1[0:1, 0:1], AF.Exp, scale=c_h[0:1, 0:1]))
            cbs.wait(s, CBS_NB)
            for idx, (qt, hp, ktg, which) in enumerate(ASEQ):
                for half in range(2):
                    mmp.wait(s, NPROJ_MM + QKGRP[(idx, half)])
                    ca.wait(s)
                    ca.inc(s.activation(psb[which][:, half * 1024:(half + 1) * 1024],
                                        S_ps[half][:], AF.Exp,
                                        bias=nB[:, 0:1], scale=c_sc[:, 0:1]))
                if ktg == qt and which == 1:
                    ci = CSEQ.index((qt, hp))
                    cvc.wait(s, ci * NCOMB + 1)
                    cls.wait(s)
                    cls.inc(s.activation(lamrow[0:1, :], lamrow[0:1, :], AF.Exp,
                                         scale=c_m1[0:1, 0:1]))
                    cvc.wait(s, ci * NCOMB + 2)
                    cls.wait(s)
                    cls.inc(s.activation(lamrow[0:1, :], lamrow[0:1, :], AF.Ln))
                    cls.wait(s)
                    cls.inc(s.activation(lamrow[0:1, :], lamrow[0:1, :], AF.Exp,
                                         scale=c_m1[0:1, 0:1]))
                    mmp.wait(s, NPROJ_MM + PVGRP[idx])
                    if ci >= 1:
                        cvc.wait(s, (ci - 1) * NCOMB + 9)
                    cls.wait(s)
                    cls.inc(s.activation(b1sb[:], l_ps[0][:, 0:512], AF.Ln))
                    cls.wait(s)
                    cls.inc(s.activation(b1sb[:], b1sb[:], AF.Exp, scale=c_m1[:, 0:1]))
                    if ci >= 1:
                        cvc.wait(s, (ci - 1) * NCOMB + 10)
                    cls.wait(s)
                    cls.inc(s.activation(w2sb[:], l_ps[1][:, 0:512], AF.Ln))
                    cls.wait(s)
                    cls.inc(s.activation(w2sb[:], w2sb[:], AF.Exp, scale=c_m1[:, 0:1]))

        @block.vector
        def _(v):
            cve.inc(v.nop())
            for tt in range(NTT):
                for oi, (nm, h, view) in enumerate(OUTS):
                    mmp.wait(v, tt * GPT + oi + 1)
                    cve.wait(v)
                    cve.inc(v.tensor_copy(view(h, slice(tt * 512, (tt + 1) * 512)), psq(oi)))
                for sub in range(4):
                    mmp.wait(v, tt * GPT + 7 + sub)
                    i128 = tt * 4 + sub
                    cve.wait(v)
                    cve.inc(v.tensor_copy(vtv(i128, 0), psv(sub)[:, 0:128]))
                    cve.wait(v)
                    cve.inc(v.tensor_copy(vtv(i128, 1), psv(sub)[:, 128:256]))
                    cve.wait(v)
                    cve.inc(v.tensor_copy(lamtv(i128), psv(sub)[:, 256:258]))
                    cve.wait(v)
                    cve.inc(v.nop())
                    cve.wait(v)
                    cve.inc(v.nop())
            cve.inc(v.memset(bmx[0:1, 0:2], 0.0))
            for i in range(96):
                mmn.wait(v, i + 1)
                cidx = 0 if (i % 6) < 4 else 1
                cve.wait(v)
                cve.inc(v.reduce_max(red1[0:1, 0:1], psn(i), axis=AX.X))
                cve.wait(v)
                cve.inc(v.tensor_tensor(bmx[0:1, cidx:cidx + 1], bmx[0:1, cidx:cidx + 1],
                                        red1[0:1, 0:1], ALU.max))
            csq.wait(v, CSQ_LN)
            cbs.wait(v)
            cbs.inc(v.tensor_tensor(red1[0:1, 0:1], bmx[0:1, 0:1], bmx[0:1, 1:2], ALU.add))
            csq.wait(v, CSQ_EXP)
            cbs.wait(v)
            cbs.inc(v.tensor_scalar(e11[0:1, 0:1], e11[0:1, 0:1], -SC, None, ALU.mult))
            mmn.wait(v, MMN_B)
            cbs.wait(v)
            cbs.inc(v.tensor_copy(nB[:, 0:1], l_ps[0][:, 0:1]))
            ei = 0
            for idx, (qt, hp, ktg, which) in enumerate(ASEQ):
                ei += 2
                if ktg == qt:
                    ca.wait(v, ei)
                    for half in range(2):
                        cvm.wait(v)
                        cvm.inc(v.tensor_tensor(psb[which][:, half * 1024:(half + 1) * 1024],
                                                psb[which][:, half * 1024:(half + 1) * 1024],
                                                dmask[:, half * 1024:(half + 1) * 1024], ALU.mult))
                if ktg == qt and which == 1:
                    ci = CSEQ.index((qt, hp))
                    mmp.wait(v, NPROJ_MM + PVGRP[idx])
                    mmn.wait(v, MMN_B + ci * 2 + 1)
                    cvc.wait(v)
                    cvc.inc(v.tensor_copy(lamrow[0:1, :], S_ps[0][0:1, 512:1024]))
                    cls.wait(v, ci * 7 + 1)
                    cvc.wait(v)
                    cvc.inc(v.tensor_scalar(lamrow[0:1, :], lamrow[0:1, :], 1.0, None, ALU.add))
                    cls.wait(v, ci * 7 + 5)
                    cvc.wait(v)
                    cvc.inc(v.tensor_tensor(t1sb[:], l_ps[0][:, 0:512], b1sb[:], ALU.mult))
                    cvc.wait(v)
                    cvc.inc(v.tensor_scalar(t1sb[:], t1sb[:], -1.0, 2.0, ALU.mult, ALU.add))
                    cvc.wait(v)
                    cvc.inc(v.tensor_tensor(b1sb[:], b1sb[:], t1sb[:], ALU.mult))
                    cls.wait(v, ci * 7 + 7)
                    cvc.wait(v)
                    cvc.inc(v.tensor_tensor(sqs[:, 0:512], l_ps[1][:, 0:512], w2sb[:], ALU.mult))
                    cvc.wait(v)
                    cvc.inc(v.tensor_scalar(sqs[:, 0:512], sqs[:, 0:512], -1.0, 2.0, ALU.mult, ALU.add))
                    cvc.wait(v)
                    cvc.inc(v.tensor_tensor(w2sb[:], w2sb[:], sqs[:, 0:512], ALU.mult))
                    cvc.wait(v)
                    cvc.inc(v.tensor_tensor(t1sb[:], O_ps[0][:, 0:512], b1sb[:], ALU.mult))
                    cvc.wait(v)
                    cvc.inc(v.tensor_tensor(b2sb[:], O_ps[1][:, 0:512], w2sb[:], ALU.mult))
                    mmn.wait(v, MMN_B + ci * 2 + 2)
                    cvc.wait(v)
                    cvc.inc(v.tensor_tensor(b2sb[:], b2sb[:], l_ps[0][:, 0:512], ALU.mult))
                    sty.wait(v, 16 * max(0, ci - 1))
                    cvc.wait(v)
                    cvc.inc(v.tensor_tensor(ytile[ci % 2][:], t1sb[:], b2sb[:], ALU.subtract))
        pp.close()
    return nc


def build_cmb(CH, with_ln):
    nc = bass.Bass("TRN2", target_bir_lowering=False)
    x_in = nc.dram_tensor("x", [TPC, D], F32, kind="ExternalInput").ap()
    yloc = nc.dram_tensor("yloc", [CH, TPC], BF16, kind="ExternalInput").ap()
    wO = nc.dram_tensor("wO", [CH, D], F32, kind="ExternalInput").ap()
    g_in = nc.dram_tensor("g", [TPC, D], F32, kind="ExternalInput").ap()
    xo = nc.dram_tensor("xo", [TPC, D], F32, kind="ExternalOutput").ap()
    if with_ln:
        xn_out = nc.dram_tensor("xn", [TPC, D], BF16, kind="ExternalOutput").ap()
    NH = CH // 2048
    KCH = 2048 // 128
    MT = TPC // 128
    VOPF = 9 if with_ln else 3
    F0 = (MT if NH > 1 else 0)
    NST = 2 if with_ln else 1

    with ExitStack() as ctx:
        E = ctx.enter_context
        block = E(nc.Block())
        pre = Prelude(nc, ctx, block)
        c_eps = pre.scalar_col("eps", LN_EPS)
        c_nh = pre.scalar_col("nh", -0.5)
        ly, lw_, lx, mmc, cve, csc, sto = chans(ctx, nc, "ly", "lw", "lx", "mmc", "cve", "csc", "sto")
        ysb = E(nc.sbuf_tensor("ysb", [128, KCH * TPC], BF16))
        wsb = E(nc.sbuf_tensor("wsb", [128, KCH * D], BF16))
        if NH > 1:
            s_sb = E(nc.sbuf_tensor("s_sb", [128, MT * D], F32))
        xt = [E(nc.sbuf_tensor("xt0", [128, D], F32))]
        gt_ = [E(nc.sbuf_tensor("gt0", [128, D], F32))]
        xot = [E(nc.sbuf_tensor("xot0", [128, D], F32))]
        xnt = [E(nc.sbuf_tensor("xnt0", [128, D], BF16))] if with_ln else [None]
        u_sb = E(nc.sbuf_tensor("u_sb", [128, D], F32))
        sqs = E(nc.sbuf_tensor("sqs", [128, D], F32)) if with_ln else u_sb
        ssum = E(nc.sbuf_tensor("ssum", [128, 1], F32))
        ssq = E(nc.sbuf_tensor("ssq", [128, 1], F32))
        mu = E(nc.sbuf_tensor("mu", [128, 1], F32))
        nmu = E(nc.sbuf_tensor("nmu", [128, 1], F32))
        var = E(nc.sbuf_tensor("var", [128, 1], F32))
        rstd = E(nc.sbuf_tensor("rstd", [128, 1], F32))
        ps = [E(nc.psum_tensor(f"ps{i}", [128, D], F32)) for i in range(2)]

        def yv(kc, sl):
            return ysb[:].rearrange("p (kc t) -> p kc t", t=TPC)[:, kc, sl]

        def wv(kc):
            return wsb[:].rearrange("p (kc n) -> p kc n", n=D)[:, kc, :]

        def sv(mt):
            return s_sb[:].rearrange("p (mt n) -> p mt n", n=D)[:, mt, :]

        def consume_tok(gi):
            h, mt = divmod(gi, MT)
            if NH > 1 and h == 0:
                return mt + 1
            return F0 + mt * VOPF + 1

        @block.sync
        def _(sy):
            pre.emit_loads(sy)
            for h in range(NH):
                if h > 0:
                    mmc.wait(sy, h * MT)
                sy.dma_start(out=ysb[:].rearrange("p (kc t) -> p kc t", t=TPC),
                             in_=yloc[h * 2048:(h + 1) * 2048, :].rearrange("(kc p) t -> p kc t", p=128)
                             ).then_inc(ly.sem, 16)
                ly.n += 16
            for mt in range(MT):
                if mt >= 1:
                    cve.wait(sy, F0 + (mt - 1) * VOPF + 3)
                sy.dma_start(out=xt[0][:], in_=x_in[mt * 128:(mt + 1) * 128, :]).then_inc(lx.sem, 16)
                sy.dma_start(out=gt_[0][:], in_=g_in[mt * 128:(mt + 1) * 128, :]).then_inc(lx.sem, 16)
                lx.n += 32

        @block.gpsimd
        def _(g):
            for h in range(NH):
                if h > 0:
                    mmc.wait(g, h * MT)
                g.dma_start(out=wsb[:].rearrange("p (kc n) -> p kc n", n=D),
                            in_=wO[h * 2048:(h + 1) * 2048, :].rearrange("(kc p) n -> p kc n", p=128)
                            ).then_inc(lw_.sem, 16)
                lw_.n += 16
            for mt in range(MT):
                cve.wait(g, F0 + mt * VOPF + 3)
                g.dma_start(out=xo[mt * 128:(mt + 1) * 128, :], in_=xot[0][:]).then_inc(sto.sem, 16)
                sto.n += 16
                if with_ln:
                    cve.wait(g, F0 + mt * VOPF + 9)
                    g.dma_start(out=xn_out[mt * 128:(mt + 1) * 128, :], in_=xnt[0][:]).then_inc(sto.sem, 16)
                    sto.n += 16
            sto.wait(g)

        @block.tensor
        def _(t):
            pre.wait(t)
            for h in range(NH):
                ly.wait(t, 16 * (h + 1))
                lw_.wait(t, 16 * (h + 1))
                for mt in range(MT):
                    gi = h * MT + mt
                    if gi >= 2:
                        cve.wait(t, consume_tok(gi - 2))
                    for kc in range(KCH):
                        for oc in range(4):
                            ii = t.matmul(ps[mt % 2][:, oc * 512:(oc + 1) * 512],
                                          yv(kc, slice(mt * 128, (mt + 1) * 128)),
                                          wv(kc)[:, oc * 512:(oc + 1) * 512],
                                          start=(kc == 0), stop=(kc == KCH - 1))
                    mmc.inc(ii)

        @block.vector
        def _(v):
            if NH > 1:
                for mt in range(MT):
                    mmc.wait(v, mt + 1)
                    cve.wait(v)
                    cve.inc(v.tensor_copy(sv(mt), ps[mt % 2][:]))
            hl = NH - 1
            for mt in range(MT):
                gi = hl * MT + mt
                mmc.wait(v, gi + 1)
                lx.wait(v, 32 * (mt + 1))
                cve.wait(v)
                if NH > 1:
                    cve.inc(v.tensor_tensor(u_sb[:], ps[mt % 2][:], sv(mt), ALU.add))
                else:
                    cve.inc(v.tensor_copy(u_sb[:], ps[mt % 2][:]))
                cve.wait(v)
                cve.inc(v.tensor_tensor(u_sb[:], u_sb[:], gt_[0][:], ALU.mult))
                sto.wait(v, 16 * mt * NST)
                cve.wait(v)
                cve.inc(v.tensor_tensor(xot[0][:], xt[0][:], u_sb[:], ALU.add))
                if with_ln:
                    cve.wait(v)
                    cve.inc(v.reduce_sum(ssum[:], xot[0][:], axis=AX.X))
                    cve.wait(v)
                    cve.inc(v.tensor_scalar(mu[:], ssum[:], 1.0 / D, None, ALU.mult))
                    cve.wait(v)
                    cve.inc(v.tensor_scalar(nmu[:], mu[:], -1.0, None, ALU.mult))
                    cve.wait(v)
                    cve.inc(v.tensor_tensor(var[:], mu[:], nmu[:], ALU.mult))
                    csc.wait(v, mt * 3 + 1)
                    cve.wait(v)
                    cve.inc(v.scalar_tensor_tensor(var[:], ssq[:], 1.0 / D, var[:], ALU.mult, ALU.add))
                    csc.wait(v, mt * 3 + 3)
                    cve.wait(v)
                    cve.inc(v.scalar_tensor_tensor(
                        xnt[0][:], xot[0][:], nmu[:, 0:1], rstd[:, 0:1].broadcast_to([128, D]),
                        ALU.add, ALU.mult))

        if with_ln:
            @block.scalar
            def _(s):
                pre.wait(s)
                for mt in range(MT):
                    cve.wait(s, F0 + mt * VOPF + 3)
                    csc.wait(s)
                    csc.inc(s.activation(sqs[:], xot[0][:], AF.Square, accum_out=ssq[:, 0:1]))
                    cve.wait(s, F0 + mt * VOPF + 8)
                    csc.wait(s)
                    csc.inc(s.activation(var[:], var[:], AF.Ln, bias=c_eps[:, 0:1]))
                    csc.wait(s)
                    csc.inc(s.activation(rstd[:], var[:], AF.Exp, scale=c_nh[:, 0:1]))
    return nc


def build_l1mix(stage=6):
    nc = bass.Bass("TRN2", target_bir_lowering=False)
    xnfm = nc.dram_tensor("xnfm", [D, T], BF16, kind="ExternalInput").ap()
    wq = nc.dram_tensor("wq", [D, 256], F32, kind="ExternalInput").ap()
    wk = nc.dram_tensor("wk", [D, 256], F32, kind="ExternalInput").ap()
    wv = nc.dram_tensor("wv", [D, 512], F32, kind="ExternalInput").ap()
    wz = nc.dram_tensor("wz", [D, 512], F32, kind="ExternalInput").ap()
    wba = nc.dram_tensor("wba", [D, 8], F32, kind="ExternalInput").ap()
    convw = nc.dram_tensor("convw", [1024, 4], F32, kind="ExternalInput").ap()
    dtb8 = nc.dram_tensor("dtb8", [8], F32, kind="ExternalInput").ap()
    alog8 = nc.dram_tensor("alog8", [8], F32, kind="ExternalInput").ap()
    normw = nc.dram_tensor("normw", [128], F32, kind="ExternalInput").ap()
    y1 = nc.dram_tensor("y1", [T, 512], BF16, kind="ExternalOutput").ap()
    ktok_d = nc.dram_tensor("ktok_d", [T, 256], BF16).ap()
    vtok_d = nc.dram_tensor("vtok_d", [T, 512], BF16).ap()
    o_d = nc.dram_tensor("o_d", [T, 512], BF16).ap()

    KC = D // 128
    NTT = T // 512
    NCH = T // 128
    RS = 8200
    import ml_dtypes as _md

    with ExitStack() as ctx:
        E = ctx.enter_context
        block = E(nc.Block())
        pre = Prelude(nc, ctx, block)
        c_m1 = pre.scalar_col("m1", -1.0)
        c_nh = pre.scalar_col("nh", -0.5)
        c_one = pre.scalar_col("one", 1.0)
        c_epsl = pre.scalar_col("epsl", 1e-24)
        c_epsr = pre.scalar_col("epsr", 1e-6)
        c_i128 = pre.scalar_col("i128", 1.0 / 128.0)
        ones_cf = pre.add("ones_cf", np.ones((128, 1), np.float32), F32)
        ones_rf = pre.add("ones_rf", np.ones((1, 128), np.float32), F32)
        ones_bb = pre.add("ones_bb", np.ones((128, 128), _md.bfloat16), BF16)
        identf = pre.add("identf", np.eye(128, dtype=np.float32), F32)
        su = np.triu(np.ones((128, 128), np.float32), 1)
        si = np.triu(np.ones((128, 128), np.float32), 0)
        sl_ = np.tril(np.ones((128, 128), np.float32), -1)
        imask8 = pre.add("imask8", np.tile(si, (1, 8)).astype(_md.bfloat16), BF16)
        lmask8 = pre.add("lmask8", np.tile(sl_, (1, 8)).astype(_md.bfloat16), BF16)
        ident8 = pre.add("ident8", np.tile(np.eye(128, dtype=np.float32), (1, 8)).astype(_md.bfloat16), BF16)
        twoI8 = pre.add("twoI8", np.tile(2 * np.eye(128, dtype=np.float32), (1, 8)).astype(_md.bfloat16), BF16)
        onesUT = pre.add("onesUT", np.triu(np.ones((128, 128), np.float32), 0).astype(np.float32), F32)
        ones_ff = pre.add("ones_ff", np.ones((128, 128), np.float32), F32)

        lx, lwt, mmq, cvp, csp = chans(ctx, nc, "lx", "lwt", "mmq", "cvp", "csp")
        mmt, cvt, stt_ = chans(ctx, nc, "mmt", "cvt", "stt")
        mmv, cvv, lba = chans(ctx, nc, "mmv", "cvv", "lba")
        cvc, csc2, mmn2, cvn2, csn2, lcw = chans(ctx, nc, "cvc", "csc2", "mmn2", "cvn2", "csn2", "lcw")
        stk, cgc = chans(ctx, nc, "stk", "cgc")
        mmn_, cvn, csn = chans(ctx, nc, "mmn", "cvn", "csn")
        mmr, cvr, csr, lkv, sto_ = chans(ctx, nc, "mmr", "cvr", "csr", "lkv", "sto")
        mmg, cvg, csg, lg, stg = chans(ctx, nc, "mmg", "cvg", "csg", "lg", "stg")
        lgn, lgx, lgo, lgw = chans(ctx, nc, "lgn", "lgx", "lgo", "lgw")

        beta_t = E(nc.sbuf_tensor("beta_t", [128, NCH * 4], F32))
        g_t = E(nc.sbuf_tensor("g_t", [128, NCH * 4], F32))
        pqa = ExitStack()
        pkv = ExitStack()
        qkvA = pqa.enter_context(nc.sbuf_tensor("qkvA", [128, 4 * RS], BF16))
        qkvB = pkv.enter_context(nc.sbuf_tensor("qkvB", [128, 4 * RS], BF16))

        def qkvrow(r, sl2):
            t_, rr = (qkvA, r) if r < 4 else (qkvB, r - 4)
            base = rr * RS + 4
            return t_[:, base + sl2.start: base + sl2.stop]

        def qkvrowraw(r, a, b):
            t_, rr = (qkvA, r) if r < 4 else (qkvB, r - 4)
            return t_[:, rr * RS + a: rr * RS + b]

        def betav(ci, h=None):
            v_ = beta_t[:].rearrange("p (c e) -> p c e", e=4)[:, ci, :]
            return v_ if h is None else v_[:, h:h + 1]

        def gv(ci, h=None):
            v_ = g_t[:].rearrange("p (c e) -> p c e", e=4)[:, ci, :]
            return v_ if h is None else v_[:, h:h + 1]

        BB = [E(nc.psum_tensor(f"BB{i}", [128, 1024], F32)) for i in range(4)]

        class BankView:
            def __init__(self, t, off):
                self.t, self.off = t, off

            def __getitem__(self, idx):
                if not isinstance(idx, tuple):
                    idx = (idx, slice(None))
                rs, cs = idx
                a = cs.start if cs.start is not None else 0
                b = cs.stop if cs.stop is not None else 512
                return self.t[rs, self.off + a:self.off + b]

        B = [BankView(BB[i // 2], (i % 2) * 512) for i in range(8)]

        def col(h):
            return slice(h * 128, (h + 1) * 128)

        ppx = ExitStack()
        xnt = [ppx.enter_context(nc.sbuf_tensor(f"xnt{i}", [128, KC * 512], BF16))
               for i in range(2)]
        bafm = ppx.enter_context(nc.sbuf_tensor("bafm", [16, T], BF16))
        bafm_d = nc.dram_tensor("bafm_d", [16, T], BF16).ap()

        def xv(tt, kc, sl2):
            return xnt[tt % 2][:].rearrange("p (kc t) -> p kc t", t=512)[:, kc, sl2]

        def wvw(ws, n, kc, sl2):
            return ws[:].rearrange("p (kc n) -> p kc n", n=n)[:, kc, sl2]

        GPT = 5
        EVT = 5
        GPV = 4
        EVV = 4

        @block.sync
        def _(sy):
            pre.emit_loads(sy)
            for it_ in range(2 * NTT):
                tt = it_ % NTT
                j = it_ - 2
                if j >= 0:
                    if j < NTT:
                        mmq.wait(sy, (j + 1) * GPT)
                    else:
                        mmv.wait(sy, (j - NTT + 1) * GPV)
                sy.dma_start(out=xnt[it_ % 2][:].rearrange("p (kc t) -> p kc t", t=512),
                             in_=xnfm[:, tt * 512:(tt + 1) * 512].rearrange("(kc p) t -> p kc t", p=128),
                             ).then_inc(lx.sem, 16)
                lx.n += 16

        pp1 = ExitStack()
        E1 = pp1.enter_context
        wqs = E1(nc.sbuf_tensor("wqs", [128, KC * 256], BF16))
        wks = E1(nc.sbuf_tensor("wks", [128, KC * 256], BF16))
        wbas = E1(nc.sbuf_tensor("wbas", [128, KC * 8], BF16))

        @block.gpsimd
        def _(g):
            for ws, src, n in ((wqs, wq, 256), (wks, wk, 256), (wbas, wba, 8)):
                g.dma_start(out=ws[:].rearrange("p (kc n) -> p kc n", n=n),
                            in_=src.rearrange("(kc p) n -> p kc n", p=128)).then_inc(lwt.sem, 16)
                lwt.n += 16

        @block.tensor
        def _(t):
            pre.wait(t)
            lwt.wait(t, 16 * 3)
            for tt in range(NTT):
                lx.wait(t, 16 * (tt + 1))
                cvp.wait(t, max(0, (tt - 1) * EVT + EVT) + (1 if tt > 0 else 0))
                for oi in range(4):
                    ws = wqs if oi < 2 else wks
                    csl = slice((oi % 2) * 128, (oi % 2) * 128 + 128)
                    for kc in range(KC):
                        ii = t.matmul(B[oi][:, 0:512], wvw(ws, 256, kc, csl),
                                      xv(tt, kc, slice(0, 512)), start=(kc == 0), stop=(kc == KC - 1))
                    mmq.inc(ii)
                cvp.wait(t, 1 + tt * EVT)
                for kc in range(KC):
                    ii = t.matmul(B[4][0:8, 0:512], wvw(wbas, 8, kc, slice(0, 8)),
                                  xv(tt, kc, slice(0, 512)), start=(kc == 0), stop=(kc == KC - 1))
                mmq.inc(ii)

        @block.vector
        def _(v):
            cvp.inc(v.memset(bafm[:], 0.0))
            for tt in range(NTT):
                base = 1 + tt * GPT
                tsl = slice(tt * 512, (tt + 1) * 512)
                for oi in range(4):
                    mmq.wait(v, tt * GPT + oi + 1)
                    cvp.wait(v)
                    cvp.inc(v.tensor_copy(qkvrow(oi, tsl), B[oi][:, 0:512]))
                mmq.wait(v, tt * GPT + 5)
                cvp.wait(v)
                cvp.inc(v.tensor_copy(bafm[0:8, tsl], B[4][0:8, 0:512]))
        pp1.close()

        pp2 = ExitStack()
        E2a = pp2.enter_context
        wvs = E2a(nc.sbuf_tensor("wvs", [128, KC * 512], BF16))

        @block.gpsimd
        def _(g):
            mmq.wait(g, NTT * GPT)
            g.dma_start(out=wvs[:].rearrange("p (kc n) -> p kc n", n=512),
                        in_=wv.rearrange("(kc p) n -> p kc n", p=128)).then_inc(lwt.sem, 16)
            lwt.n += 16
            cvp.wait(g, 1 + NTT * EVT)
            g.dma_start(out=bafm_d, in_=bafm[:]).then_inc(stk.sem, 16)
            stk.n += 16

        @block.tensor
        def _(t):
            lwt.wait(t, 16 * 4)
            for tt in range(NTT):
                lx.wait(t, 16 * (NTT + tt + 1))
                cvv.wait(t, max(0, (tt - 1) * EVV + EVV))
                for sub in range(4):
                    for kc in range(KC):
                        ii = t.matmul(B[sub][:, 0:512], wvw(wvs, 512, kc, col(sub)),
                                      xv(tt, kc, slice(0, 512)), start=(kc == 0), stop=(kc == KC - 1))
                    mmv.inc(ii)

        @block.vector
        def _(v):
            for tt in range(NTT):
                base = tt * GPV
                tsl = slice(tt * 512, (tt + 1) * 512)
                for sub in range(4):
                    mmv.wait(v, base + sub + 1)
                    cvv.wait(v)
                    cvv.inc(v.tensor_copy(qkvrow(4 + sub, tsl), B[sub][:, 0:512]))
        pp2.close()
        ppx.close()

        pt = ExitStack()
        E2 = pt.enter_context
        dtbbc = E2(nc.sbuf_tensor("dtbbc", [128, 8], F32))
        albc = E2(nc.sbuf_tensor("albc", [128, 8], F32))
        dtbr = E2(nc.sbuf_tensor("dtbr", [1, 8], F32))
        alr = E2(nc.sbuf_tensor("alr", [1, 8], F32))
        batok = E2(nc.sbuf_tensor("batok", [128, NCH * 16], BF16, is_transpose_dest=True))
        scb = E2(nc.sbuf_tensor("scb", [128, NCH * 4], F32))
        scb2 = E2(nc.sbuf_tensor("scb2", [128, NCH * 4], F32))

        def batv(c0, c1):
            return batok[:].rearrange("p (c e) -> p c e", e=16)[:, :, c0:c1]

        @block.sync
        def _(sy):
            mmv.wait(sy, NTT * GPV)
            sy.dma_start(out=dtbr[0:1, :], in_=dtb8[:].unsqueeze(0)).then_inc(lg.sem, 16)
            sy.dma_start(out=alr[0:1, :], in_=alog8[:].unsqueeze(0)).then_inc(lg.sem, 16)
            lg.n += 32
            stk.wait(sy, 16)
            for ci in range(NCH):
                sy.dma_start_transpose(
                    batok[:].rearrange("p (c e) -> p c e", e=16)[:, ci, :],
                    bafm_d[:, ci * 128:(ci + 1) * 128]).then_inc(lba.sem, 16)
                lba.n += 16

        @block.tensor
        def _(t):
            lg.wait(t, 32)
            cvv.wait(t, NTT * EVV)
            t.matmul(B[1][:, 0:8], ones_rf[0:1, :], dtbr[0:1, :], start=True, stop=True)
            ii = t.matmul(B[2][:, 0:8], ones_rf[0:1, :], alr[0:1, :], start=True, stop=True)
            mmt.inc(ii)

        @block.vector
        def _(v):
            mmt.wait(v, 1)
            cvt.inc(v.tensor_copy(dtbbc[:], B[1][:, 0:8]))
            cvt.wait(v)
            cvt.inc(v.tensor_copy(albc[:], B[2][:, 0:8]))
            lba.wait(v, 16 * NCH)
            sv1 = scb[:].rearrange("p (c e) -> p c e", e=4)
            sv2 = scb2[:].rearrange("p (c e) -> p c e", e=4)
            g_v = g_t[:].rearrange("p (c e) -> p c e", e=4)
            dtbv = dtbbc[:].rearrange("p (o e) -> p o e", o=1)[:, :, 4:8].broadcast_to([128, NCH, 4])
            alv = albc[:].rearrange("p (o e) -> p o e", o=1)[:, :, 4:8].broadcast_to([128, NCH, 4])
            cvt.wait(v)
            cvt.inc(v.tensor_tensor(sv1, batv(4, 8), dtbv, ALU.add))
            cvt.wait(v)
            cvt.inc(v.tensor_scalar(sv2, sv1, -1.0, None, ALU.mult))
            cvt.wait(v)
            cvt.inc(v.tensor_tensor(sv2, sv1, sv2, ALU.max))
            csn.wait(v, 2)
            cvt.wait(v)
            cvt.inc(v.tensor_scalar(sv1, sv1, 0.0, None, ALU.max))
            cvt.wait(v)
            cvt.inc(v.tensor_tensor(sv1, sv1, sv2, ALU.add))
            csn.wait(v, 3)
            cvt.wait(v)
            cvt.inc(v.tensor_tensor(sv1, sv1, alv, ALU.mult))
            cvt.wait(v)
            cvt.inc(v.tensor_scalar(g_v, sv1, -1.0, None, ALU.mult))
            csn.wait(v, 4)
            cvt.wait(v)
            cvt.inc(v.tensor_scalar(scb[:], scb[:], 1.0, None, ALU.add))
            cvt.wait(v)
            cvt.inc(v.reciprocal(beta_t[:], scb[:]))

        @block.scalar
        def _(s):
            pre.wait(s)
            cvt.wait(s, 2 + 3)
            csn.wait(s)
            csn.inc(s.activation(scb2[:], scb2[:], AF.Exp, scale=c_m1[:, 0:1]))
            csn.wait(s)
            csn.inc(s.activation(scb2[:], scb2[:], AF.Ln, bias=c_one[:, 0:1]))
            csn.wait(s)
            csn.inc(s.activation(albc[:], albc[:], AF.Exp))
            cvt.wait(s, 9)
            sv1 = scb[:].rearrange("p (c e) -> p c e", e=4)
            csn.wait(s)
            csn.inc(s.activation(sv1, batv(0, 4), AF.Exp, scale=c_m1[:, 0:1]))
        pt.close()

        if stage >= 2:
            pc = ExitStack()
            E3 = pc.enter_context
            tv1 = [E3(nc.sbuf_tensor(f"tv1{i}", [128, T // 2], BF16)) for i in range(2)]
            tv2 = [E3(nc.sbuf_tensor(f"tv2{i}", [128, T // 2], BF16)) for i in range(2)]
            tmp3 = E3(nc.sbuf_tensor("tmp3", [128, T // 2], BF16))
            tmp4 = E3(nc.sbuf_tensor("tmp4", [128, T // 2], BF16))
            cws = E3(nc.sbuf_tensor("cws", [128, 8 * 4], F32))
            HT = T // 2

            def cwcol(r, j):
                return cws[:].rearrange("p (r j) -> p r j", j=4)[:, r, j:j + 1]

            SSEQ = ([("v", k) for k in range(4)] + [("h", 0)]
                    + [("v", k) for k in range(4, 8)] + [("h", 1)]
                    + [("v", k) for k in range(8, 14)])
            VSILU = {}
            HSILU = {}
            for _i, (_kd, _k) in enumerate(SSEQ):
                (VSILU if _kd == "v" else HSILU)[_k] = _i + 1
            ROWDONE = {r: max(VSILU[2 * r], VSILU[2 * r + 1]) for r in range(7)}
            ROWDONE[7] = HSILU[1]
            CSC2_END = len(SSEQ)

            @block.gpsimd
            def _(g):
                mmv.wait(g, NTT * GPV)
                stk.wait(g, 16)
                g.dma_start(out=cws[:].rearrange("p (r j) -> p r j", j=4),
                            in_=convw.rearrange("(r p) j -> p r j", p=128)).then_inc(lcw.sem, 16)
                lcw.n += 16
                lcw.wait(g, 16)
                cvc.wait(g, 8)
                for k in range(2):
                    r, hf = 7, 1 - k
                    off = hf * HT
                    sl4 = [qkvrowraw(r, 1 + j + off, 1 + j + off + HT) for j in range(4)]
                    cwb = [cwcol(r, j).broadcast_to([128, HT]) for j in range(4)]
                    cgc.wait(g)
                    cgc.inc(g.tensor_tensor(tmp3[:], sl4[0], cwb[0], ALU.mult))
                    if k >= 1:
                        csc2.wait(g, HSILU[k - 1])
                    cgc.wait(g)
                    cgc.inc(g.tensor_tensor(tmp4[:], sl4[1], cwb[1], ALU.mult))
                    cgc.wait(g)
                    cgc.inc(g.tensor_tensor(tmp3[:], tmp3[:], tmp4[:], ALU.add))
                    cgc.wait(g)
                    cgc.inc(g.tensor_tensor(tmp4[:], sl4[2], cwb[2], ALU.mult))
                    cgc.wait(g)
                    cgc.inc(g.tensor_tensor(tmp3[:], tmp3[:], tmp4[:], ALU.add))
                    cgc.wait(g)
                    cgc.inc(g.tensor_tensor(tmp4[:], sl4[3], cwb[3], ALU.mult))
                    cgc.wait(g)
                    cgc.inc(g.tensor_tensor(tmp4[:], tmp3[:], tmp4[:], ALU.add))

            @block.vector
            def _(v):
                lcw.wait(v, 16)
                cvp.wait(v, 1 + NTT * EVT)
                cvv.wait(v, NTT * EVV)
                for r in range(8):
                    t_, rr = (qkvA, r) if r < 4 else (qkvB, r - 4)
                    cvc.wait(v)
                    cvc.inc(v.memset(t_[:, rr * RS:rr * RS + 4], 0.0))
                for k in range(14):
                    r, hf = k // 2, 1 - (k % 2)
                    off = hf * HT
                    t1p, t2p = tv1[k % 2], tv2[k % 2]
                    sl4 = [qkvrowraw(r, 1 + j + off, 1 + j + off + HT) for j in range(4)]
                    cvc.wait(v)
                    cvc.inc(v.tensor_scalar(t1p[:], sl4[0], cwcol(r, 0), None, ALU.mult))
                    if k >= 2:
                        csc2.wait(v, VSILU[k - 2])
                    cvc.wait(v)
                    cvc.inc(v.scalar_tensor_tensor(t2p[:], sl4[1], cwcol(r, 1), t1p[:], ALU.mult, ALU.add))
                    cvc.wait(v)
                    cvc.inc(v.scalar_tensor_tensor(t1p[:], sl4[2], cwcol(r, 2), t2p[:], ALU.mult, ALU.add))
                    cvc.wait(v)
                    cvc.inc(v.scalar_tensor_tensor(t2p[:], sl4[3], cwcol(r, 3), t1p[:], ALU.mult, ALU.add))

            @block.scalar
            def _(s):
                def hsilu(k):
                    cgc.wait(s, 7 * (k + 1))
                    hf = 1 - k
                    csc2.wait(s)
                    csc2.inc(s.activation(qkvrowraw(7, 4 + hf * HT, 4 + hf * HT + HT),
                                          tmp4[:], AF.Silu))

                def vsilu(k):
                    r, hf = k // 2, 1 - (k % 2)
                    cvc.wait(s, 8 + 4 * (k + 1))
                    csc2.wait(s)
                    csc2.inc(s.activation(qkvrowraw(r, 4 + hf * HT, 4 + hf * HT + HT),
                                          tv2[k % 2][:], AF.Silu))

                for kind, k in SSEQ:
                    (vsilu if kind == "v" else hsilu)(k)


        if stage >= 3:
            nsc4 = E3(nc.sbuf_tensor("nsc4", [128, 2048], F32))
            rsr2 = E3(nc.sbuf_tensor("rsr2", [1, 1024], F32))
            PIPE = 2

            def nq(i):
                return nsc4[:, (i % 4) * 512:(i % 4) * 512 + 512]

            def rhalf(i):
                return rsr2[0:1, (i % 2) * 512:(i % 2) * 512 + 512]

            SEN = []
            MEN = []
            for k in range(64 + PIPE):
                if k < 64:
                    SEN.append(("sq", k))
                    MEN.append(("sum", k))
                if k >= PIPE:
                    SEN.append(("ln", k - PIPE))
                    SEN.append(("ex", k - PIPE))
                    MEN.append(("bc", k - PIPE))
            NTOK = {op: i + 1 for i, op in enumerate(SEN)}
            MTOKN = {op: i + 1 for i, op in enumerate(MEN)}

            @block.scalar
            def _(s):
                for k in range(64 + PIPE):
                    if k < 64:
                        r, tt = divmod(k, 16)
                        csc2.wait(s, ROWDONE[r])
                        if k >= 4:
                            mmn2.wait(s, MTOKN[("sum", k - 4)])
                        csn2.wait(s)
                        csn2.inc(s.activation(nq(k), qkvrowraw(r, 4 + tt * 512, 4 + (tt + 1) * 512),
                                              AF.Square))
                    if k >= PIPE:
                        j = k - PIPE
                        mmn2.wait(s, MTOKN[("bc", j - 2)] if j >= 2 else MTOKN[("sum", j)])
                        csn2.wait(s)
                        csn2.inc(s.activation(rhalf(j), B[j % 4][0:1, 0:512], AF.Ln,
                                              bias=c_epsl[0:1, 0:1]))
                        csn2.wait(s)
                        csn2.inc(s.activation(rhalf(j), rhalf(j), AF.Exp,
                                              scale=c_nh[0:1, 0:1]))

            @block.tensor
            def _(t):
                for k in range(64 + PIPE):
                    if k < 64:
                        csn2.wait(t, NTOK[("sq", k)])
                        ii = t.matmul(B[k % 4][0:1, 0:512], ones_cf[:, 0:1], nq(k),
                                      start=True, stop=True)
                        mmn2.inc(ii)
                    if k >= PIPE:
                        j = k - PIPE
                        csn2.wait(t, NTOK[("ex", j)])
                        if j >= 4:
                            cvn2.wait(t, j - 3)
                        ii = t.matmul(B[4 + j % 4][:, 0:512], ones_rf[0:1, :], rhalf(j),
                                      start=True, stop=True)
                        mmn2.inc(ii)

            @block.vector
            def _(v):
                for j in range(64):
                    r, tt = divmod(j, 16)
                    mmn2.wait(v, MTOKN[("bc", j)])
                    sl2 = qkvrowraw(r, 4 + tt * 512, 4 + (tt + 1) * 512)
                    cvn2.wait(v)
                    cvn2.inc(v.tensor_tensor(sl2, sl2, B[4 + j % 4][:, 0:512], ALU.mult))


        if stage >= 4:

            kfm_d = nc.dram_tensor("kfm_d", [256, T], BF16).ap()
            vfm_d = nc.dram_tensor("vfm_d", [512, T], BF16).ap()

            @block.gpsimd
            def _(g):
                for r in range(2, 4):
                    cvn2.wait(g, 64)
                    g.dma_start(out=kfm_d[(r - 2) * 128:(r - 1) * 128, :],
                                in_=qkvrowraw(r, 4, 4 + T)).then_inc(stk.sem, 16)
                    stk.n += 16
                for r in range(4, 8):
                    csc2.wait(g, ROWDONE[r] if r < 7 else HSILU[1])
                    g.dma_start(out=vfm_d[(r - 4) * 128:(r - 3) * 128, :],
                                in_=qkvrowraw(r, 4, 4 + T)).then_inc(stk.sem, 16)
                    stk.n += 16


        if stage >= 2:
            pc.close()
        if stage >= 5:

            pr = ExitStack()
            E4 = pr.enter_context
            G2 = E4(nc.sbuf_tensor("G2", [128, 8], F32))
            negG2 = E4(nc.sbuf_tensor("negG2", [128, 8], F32))
            GC2 = E4(nc.sbuf_tensor("GC2", [128, 8], F32))
            gdiff2 = E4(nc.sbuf_tensor("gdiff2", [128, 8], F32))
            gam2 = E4(nc.sbuf_tensor("gam2", [128, 8], F32))
            etil2 = E4(nc.sbuf_tensor("etil2", [128, 8], F32))
            gamC2 = E4(nc.sbuf_tensor("gamC2", [128, 8], F32))
            bg4 = E4(nc.sbuf_tensor("bg4", [128, 4], F32))
            grow8 = E4(nc.sbuf_tensor("grow8", [1, 1024], F32))
            ngrow8 = E4(nc.sbuf_tensor("ngrow8", [1, 1024], F32))
            scrA = E4(nc.sbuf_tensor("scrA", [128, 1024], F32))
            scrB = E4(nc.sbuf_tensor("scrB", [128, 1024], F32))
            scrC = E4(nc.sbuf_tensor("scrC", [128, 1024], F32))
            dexp8 = E4(nc.sbuf_tensor("dexp8", [128, 1024], F32))
            dexp28 = E4(nc.sbuf_tensor("dexp28", [128, 1024], F32))
            alow8 = E4(nc.sbuf_tensor("alow8", [128, 1024], BF16))
            a8tmp = E4(nc.sbuf_tensor("a8tmp", [128, 1024], BF16))
            aqk8 = E4(nc.sbuf_tensor("aqk8", [128, 1024], BF16))
            R8 = E4(nc.sbuf_tensor("R8", [128, 1024], BF16))
            Y8 = E4(nc.sbuf_tensor("Y8", [128, 1024], BF16))
            Z8 = E4(nc.sbuf_tensor("Z8", [128, 1024], BF16))
            Bblk = [E4(nc.sbuf_tensor(f"Bblk{i}", [128, 1024], BF16)) for i in range(2)]
            Wb4 = [E4(nc.sbuf_tensor(f"Wb4{i}", [128, 512], BF16)) for i in range(2)]
            Uv4 = [E4(nc.sbuf_tensor(f"Uv4{i}", [128, 512], BF16)) for i in range(2)]
            WtT4 = [E4(nc.sbuf_tensor(f"WtT4{i}", [128, 512], BF16)) for i in range(2)]
            M4 = E4(nc.sbuf_tensor("M4", [128, 512], BF16))
            Kt4 = E4(nc.sbuf_tensor("Kt4", [128, 512], BF16))
            S4 = E4(nc.sbuf_tensor("S4", [128, 512], F32))
            S4b = E4(nc.sbuf_tensor("S4b", [128, 512], BF16))
            o1f = E4(nc.sbuf_tensor("o1f", [128, 512], F32))
            o4t = [E4(nc.sbuf_tensor(f"o4t{i}", [128, 512], BF16)) for i in range(2)]
            ktch2 = [E4(nc.sbuf_tensor(f"ktch2{i}", [128, 512], BF16, is_transpose_dest=True)) for i in range(2)]
            vtch2 = [E4(nc.sbuf_tensor(f"vtch2{i}", [128, 1024], BF16, is_transpose_dest=True)) for i in range(2)]

            NP = NCH // 2
            NEWT = 7
            P0 = 2
            CVT_END = 11

            NRND = 6
            VNAMES = (["gevac", "neg"] + [f"grow{h}" for h in range(8)]
                      + [f"ngrow{h}" for h in range(8)]
                      + ["gcevac", "gdiff", "dargA1", "dargA2", "dargB1", "dargB2",
                         "w3", "a0", "a1", "a2", "z1a", "z1b", "aq1", "aq2"]
                      + [x for r in range(NRND) for x in (f"r8a_{r}", f"r8b_{r}")]
                      + [f"{n}_{c}" for c in "AB" for n in ("bgam", "bbk", "bbv")]
                      + [f"{n}_{c}" for c in "AB"
                         for n in ("m4", "ov", "oadd", "kt", "s1", "s2")])
            VT = {n: i + 1 for i, n in enumerate(VNAMES)}
            NVR2 = len(VNAMES)
            SNAMES = (["dexpA", "dexpB", "gam", "etil", "gamC", "yiva", "yivb"]
                      + [x for r in range(NRND)
                         for x in ([f"yeva_{r}", f"yevb_{r}"]
                                   + ([f"zeva_{r}", f"zevb_{r}"] if r < NRND - 1 else []))]
                      + ["wf_A", "uv_A", "wtev_A", "wf_B", "uv_B", "wtev_B",
                         "o1f_A", "s4b_A", "o1f_B", "s4b_B"])
            ST = {n: i + 1 for i, n in enumerate(SNAMES)}
            NSR2 = len(SNAMES)
            MNAMES = (["cumsum", "gtr", "gc", "dargA", "dargB", "kkkq", "tia", "tib"]
                      + [x for r in range(NRND)
                         for x in ([f"p8a_{r}", f"p8b_{r}", f"y8a_{r}", f"y8b_{r}"]
                                   + ([f"t8a_{r}", f"t8b_{r}"] if r < NRND - 1 else []))]
                      + ["up_A", "wtt_A", "up_B", "wtt_B"]
                      + [f"{n}_{c}" for c in "AB"
                         for n in ("m0", "o1g", "o2g", "sng")])
            MT = {n: i + 1 for i, n in enumerate(MNAMES)}
            NMM2 = len(MNAMES)

            def vt(p, n):
                return p * NVR2 + P0 + VT[n]

            def st(p, n):
                return p * NSR2 + ST[n]

            def mt(p, n):
                return p * NMM2 + MT[n]

            def v8(t):
                return t[:].rearrange("p (v w) -> p v w", w=128)

            def colbc8(t8):
                return t8[:].rearrange("p (v o) -> p v o", o=1).broadcast_to([128, 8, 128])

            def colbc4(t8, hf):
                return t8[:, hf * 4:hf * 4 + 4].rearrange(
                    "p (v o) -> p v o", o=1).broadcast_to([128, 4, 128])

            def betacol8(p):
                return beta_t[:, 8 * p:8 * p + 8].rearrange(
                    "p (v o) -> p v o", o=1).broadcast_to([128, 8, 128])

            def betacol4(p, hf):
                return beta_t[:, 8 * p + hf * 4:8 * p + hf * 4 + 4].rearrange(
                    "p (v o) -> p v o", o=1).broadcast_to([128, 4, 128])

            def pairbc(src, w):
                return src.rearrange("p (kh o w) -> p kh o w", o=1, w=w).broadcast_to([128, 2, 2, w])

            def kkbc(off):
                return BB[0][:, off:off + 512].rearrange(
                    "p (c k o w) -> p c k o w", c=2, k=2, o=1, w=128
                ).broadcast_to([128, 2, 2, 2, 128])

            def ckdw(t):
                return t[:].rearrange("p (c k d w) -> p c k d w", c=2, k=2, d=2, w=128)

            def qrow(h2, ci):
                base = h2 * RS + 4
                return qkvA[:, base + ci * 128: base + (ci + 1) * 128]

            def col(h):
                return slice(h * 128, (h + 1) * 128)

            @block.sync
            def _(sy):
                stk.wait(sy, 112)
                for p in range(NP):
                    if p >= 2:
                        cvr.wait(sy, vt(p - 2, "kt_B"))
                    for hf in range(2):
                        cg = 2 * p + hf
                        sy.dma_start_transpose(ktch2[p % 2][:, hf * 256:hf * 256 + 256],
                                               kfm_d[:, cg * 128:(cg + 1) * 128]).then_inc(lkv.sem, 16)
                        sy.dma_start_transpose(vtch2[p % 2][:, hf * 512:hf * 512 + 512],
                                               vfm_d[:, cg * 128:(cg + 1) * 128]).then_inc(lkv.sem, 16)
                        lkv.n += 32

            @block.tensor
            def _(t):
                cvt.wait(t, CVT_END)
                cvn2.wait(t, 64)
                csc2.wait(t, ROWDONE[3])
                cvr.wait(t, P0)
                for p in range(NP):
                    if p > 0:
                        cvr.wait(t, vt(p - 1, "ov_B"))
                        csr.wait(t, st(p - 1, "o1f_B"))
                    ii = t.matmul(BB[0][:, 0:8], onesUT[:], g_t[:, 8 * p:8 * p + 8],
                                  start=True, stop=True)
                    mmr.inc(ii)
                    cvr.wait(t, vt(p, "gevac"))
                    for h in range(8):
                        ii = t.transpose(BB[0][0:1, col(h)], G2[:, h:h + 1], identf[:])
                    mmr.inc(ii)
                    ii = t.matmul(BB[2][:, 0:8], ones_ff[:], g_t[:, 8 * p:8 * p + 8],
                                  start=True, stop=True)
                    mmr.inc(ii)
                    cvr.wait(t, vt(p, "grow7"))
                    for h in range(8):
                        ii = t.matmul(BB[1][:, col(h)], ones_rf[0:1, :],
                                      grow8[0:1, col(h)], start=True, stop=True)
                    mmr.inc(ii)
                    cvr.wait(t, vt(p, "ngrow7"))
                    if p > 0:
                        cvr.wait(t, vt(p - 1, "s2_B"))
                        csr.wait(t, st(p - 1, "s4b_B"))
                    for h in range(8):
                        ii = t.matmul(BB[3][:, col(h)], ones_rf[0:1, :],
                                      ngrow8[0:1, col(h)], start=True, stop=True)
                    mmr.inc(ii)
                    cvr.wait(t, vt(p, "grow7"))
                    for c in range(2):
                        for kh in range(2):
                            ii = t.matmul(BB[0][:, (c * 2 + kh) * 128:(c * 2 + kh) * 128 + 128],
                                          qrow(2 + kh, 2 * p + c), qrow(2 + kh, 2 * p + c),
                                          start=True, stop=True)
                    for c in range(2):
                        for kh in range(2):
                            ii = t.matmul(BB[0][:, 512 + (c * 2 + kh) * 128:512 + (c * 2 + kh) * 128 + 128],
                                          qrow(2 + kh, 2 * p + c), qrow(kh, 2 * p + c),
                                          start=True, stop=True)
                    mmr.inc(ii)
                    cvr.wait(t, vt(p, "z1a"))
                    cvr.wait(t, vt(p, "dargA1"))
                    if p > 0:
                        csr.wait(t, st(p - 1, "s4b_B"))
                    for v_ in range(4):
                        ii = t.matmul(BB[1][:, col(v_)], Z8[:, col(v_)],
                                      ident8[:, 0:128], start=True, stop=True)
                    mmr.inc(ii)
                    cvr.wait(t, vt(p, "z1b"))
                    for v_ in range(4, 8):
                        ii = t.matmul(BB[1][:, col(v_)], Z8[:, col(v_)],
                                      ident8[:, 0:128], start=True, stop=True)
                    mmr.inc(ii)
                    cvr.wait(t, vt(p, "a2"))
                    for r in range(NRND):
                        for wv, vr in (("a", range(4)), ("b", range(4, 8))):
                            if r == 0:
                                csr.wait(t, st(p, f"yiv{wv}"))
                            else:
                                csr.wait(t, st(p, f"yev{wv}_{r - 1}"))
                            for v_ in vr:
                                ii = t.matmul(BB[2][:, col(v_)], alow8[:, col(v_)],
                                              Y8[:, col(v_)], start=True, stop=True)
                            mmr.inc(ii)
                        for wv, vr, gate in (("a", range(4), "w3"), ("b", range(4, 8), "aq1")):
                            cvr.wait(t, vt(p, f"r8{wv}_{r}"))
                            if r == 0:
                                cvr.wait(t, vt(p, gate))
                            else:
                                csr.wait(t, st(p, f"zev{wv}_{r - 1}"))
                            for v_ in vr:
                                ii = t.matmul(BB[0][:, col(v_)], Z8[:, col(v_)],
                                              R8[:, col(v_)], start=True, stop=True)
                            mmr.inc(ii)
                        if r < NRND - 1:
                            for wv, vr in (("a", range(4)), ("b", range(4, 8))):
                                csr.wait(t, st(p, f"yev{wv}_{r}"))
                                for v_ in vr:
                                    ii = t.matmul(BB[1][:, col(v_)], Y8[:, col(v_)],
                                                  ident8[:, 0:128], start=True, stop=True)
                                mmr.inc(ii)
                    for hf, cn in ((0, "A"), (1, "B")):
                        csr.wait(t, st(p, f"yevb_{NRND - 1}"))
                        cvr.wait(t, vt(p, f"bbv_{cn}"))
                        if hf == 1:
                            csr.wait(t, st(p, "uv_A"))
                        for h in range(4):
                            ii = t.matmul(BB[2][:, h * 256:h * 256 + 256],
                                          Y8[:, col(hf * 4 + h)],
                                          Bblk[hf][:, h * 256:(h + 1) * 256],
                                          start=True, stop=True)
                        mmr.inc(ii)
                        csr.wait(t, st(p, f"wf_{cn}"))
                        for h in range(4):
                            ii = t.matmul(BB[3][:, hf * 512 + h * 128:hf * 512 + h * 128 + 128],
                                          Wb4[hf][:, col(h)], ident8[:, 0:128],
                                          start=True, stop=True)
                        mmr.inc(ii)
                    for hf, cn in ((0, "A"), (1, "B")):
                        cg = 2 * p + hf
                        csr.wait(t, st(p, f"zev{'ab'[hf]}_{NRND - 2}"))
                        csr.wait(t, st(p, f"wtev_{cn}"))
                        if hf == 1:
                            csr.wait(t, st(p, "s4b_A"))
                        for h in range(4):
                            ii = t.matmul(BB[1][:, hf * 512 + h * 128:hf * 512 + h * 128 + 128],
                                          WtT4[hf][:, col(h)], S4b[:, col(h)],
                                          start=True, stop=True)
                        mmr.inc(ii)
                        cvr.wait(t, vt(p, f"m4_{cn}"))
                        if hf == 1:
                            cvr.wait(t, vt(p, "ov_A"))
                        for h in range(4):
                            ii = t.matmul(BB[0][:, hf * 512 + h * 128:hf * 512 + h * 128 + 128],
                                          aqk8[:, col(hf * 4 + h)], M4[:, col(h)],
                                          start=True, stop=True)
                        mmr.inc(ii)
                        oh = 1 - hf
                        if hf == 1:
                            csr.wait(t, st(p, "o1f_A"))
                        for h in range(4):
                            ii = t.matmul(BB[0][:, oh * 512 + h * 128:oh * 512 + h * 128 + 128],
                                          qrow(h // 2, cg), S4b[:, col(h)],
                                          start=True, stop=True)
                        mmr.inc(ii)
                        cvr.wait(t, vt(p, f"kt_{cn}"))
                        for h in range(4):
                            ii = t.matmul(BB[3][:, hf * 512 + h * 128:hf * 512 + h * 128 + 128],
                                          Kt4[:, col(h)], M4[:, col(h)],
                                          start=True, stop=True)
                        mmr.inc(ii)

            @block.vector
            def _(v):
                csn2.wait(v, 64 * 3)
                csc2.wait(v, CSC2_END)
                mmn2.wait(v, 2 * 64)
                cvr.inc(v.memset(S4[:], 0.0))
                cvr.inc(v.memset(S4b[:], 0.0))
                for p in range(NP):
                    def V(instr):
                        cvr.inc(instr)

                    def W():
                        cvr.wait(v)
                    mmr.wait(v, mt(p, "cumsum"))
                    W(); V(v.tensor_copy(G2[:], BB[0][:, 0:8]))
                    W(); V(v.tensor_scalar(negG2[:], G2[:], -1.0, None, ALU.mult))
                    mmr.wait(v, mt(p, "gtr"))
                    for h in range(8):
                        W(); V(v.tensor_copy(grow8[0:1, col(h)], BB[0][0:1, col(h)]))
                    for h in range(8):
                        W(); V(v.tensor_scalar(ngrow8[0:1, col(h)], grow8[0:1, col(h)],
                                               -1.0, None, ALU.mult))
                    mmr.wait(v, mt(p, "gc"))
                    W(); V(v.tensor_copy(GC2[:], BB[2][:, 0:8]))
                    W(); V(v.tensor_tensor(gdiff2[:], GC2[:], negG2[:], ALU.add))
                    mmr.wait(v, mt(p, "dargA"))
                    W(); V(v.tensor_tensor(v8(scrA), v8(BB[1]), colbc8(negG2), ALU.add))
                    W(); V(v.tensor_scalar(scrA[:], scrA[:], 0.0, None, ALU.min))
                    mmr.wait(v, mt(p, "dargB"))
                    W(); V(v.tensor_tensor(v8(scrB), v8(BB[3]), colbc8(G2), ALU.add))
                    W(); V(v.tensor_scalar(scrB[:], scrB[:], 0.0, None, ALU.min))
                    mmr.wait(v, mt(p, "kkkq"))
                    csr.wait(v, st(p, "dexpB"))
                    W(); V(v.tensor_tensor(ckdw(scrC), ckdw(dexp28), kkbc(0), ALU.mult))
                    W(); V(v.tensor_tensor(v8(scrC), v8(scrC), betacol8(p), ALU.mult))
                    W(); V(v.tensor_tensor(a8tmp[:], scrC[:], lmask8[:], ALU.mult))
                    W(); V(v.tensor_tensor(alow8[:], a8tmp[:], ident8[:], ALU.add))
                    W(); V(v.tensor_tensor(Z8[:, 0:512], ident8[:, 0:512],
                                           a8tmp[:, 0:512], ALU.subtract))
                    W(); V(v.tensor_tensor(Z8[:, 512:1024], ident8[:, 512:1024],
                                           a8tmp[:, 512:1024], ALU.subtract))
                    csr.wait(v, st(p, "dexpA"))
                    W(); V(v.tensor_tensor(ckdw(scrC), ckdw(dexp8), kkbc(512), ALU.mult))
                    W(); V(v.tensor_tensor(aqk8[:], scrC[:], imask8[:], ALU.mult))
                    HSL = (("a", slice(0, 512)), ("b", slice(512, 1024)))
                    for r in range(NRND):
                        for wv, sl in HSL:
                            mmr.wait(v, mt(p, f"p8{wv}_{r}"))
                            W(); V(v.tensor_tensor(R8[:, sl], twoI8[:, sl],
                                                   BB[2][:, sl], ALU.subtract))
                    for hf, cn in ((0, "A"), (1, "B")):
                        csr.wait(v, st(p, "gam"))
                        W(); V(v.tensor_tensor(bg4[:], beta_t[:, 8 * p + hf * 4:8 * p + hf * 4 + 4],
                                               gam2[:, hf * 4:hf * 4 + 4], ALU.mult))
                        lkv.wait(v, 64 * (p + 1))
                        bb = Bblk[hf][:].rearrange("p (h z w) -> p h z w", z=2, w=128)
                        bgbc = bg4[:].rearrange("p (v o) -> p v o", o=1).broadcast_to([128, 4, 128])
                        W(); V(v.tensor_tensor(bb[:, :, 0, :],
                                               pairbc(ktch2[p % 2][:, hf * 256:hf * 256 + 256], 128),
                                               bgbc, ALU.mult))
                        vsl = vtch2[p % 2][:, hf * 512:hf * 512 + 512].rearrange("p (h w) -> p h w", w=128)
                        W(); V(v.tensor_tensor(bb[:, :, 1, :], vsl, betacol4(p, hf), ALU.mult))
                    for hf, cn in ((0, "A"), (1, "B")):
                        cg = 2 * p + hf
                        mmr.wait(v, mt(p, f"m0_{cn}"))
                        csr.wait(v, st(p, f"uv_{cn}"))
                        W(); V(v.tensor_tensor(M4[:], Uv4[hf][:], BB[1][:, hf * 512:hf * 512 + 512],
                                               ALU.subtract))
                        mmr.wait(v, mt(p, f"o2g_{cn}"))
                        oh = 1 - hf
                        W(); V(v.tensor_tensor(
                            scrA[:, 0:512].rearrange("p (h w) -> p h w", w=128),
                            BB[0][:, oh * 512:oh * 512 + 512].rearrange("p (h w) -> p h w", w=128),
                            colbc4(gam2, hf), ALU.mult))
                        csr.wait(v, st(p, f"o1f_{cn}"))
                        sto_.wait(v, 16 * max(0, cg - 1))
                        W(); V(v.tensor_tensor(o4t[hf][:], scrA[:, 0:512], o1f[:], ALU.add))
                        csr.wait(v, st(p, "etil"))
                        W(); V(v.tensor_tensor(
                            Kt4[:].rearrange("p (h w) -> p h w", w=128),
                            pairbc(ktch2[p % 2][:, hf * 256:hf * 256 + 256], 128),
                            colbc4(etil2, hf), ALU.mult))
                        csr.wait(v, st(p, "gamC"))
                        W(); V(v.tensor_tensor(
                            scrA[:, 0:512].rearrange("p (h w) -> p h w", w=128),
                            S4[:].rearrange("p (h w) -> p h w", w=128),
                            colbc4(gamC2, hf), ALU.mult))
                        mmr.wait(v, mt(p, f"sng_{cn}"))
                        W(); V(v.tensor_tensor(S4[:], scrA[:, 0:512],
                                               BB[3][:, hf * 512:hf * 512 + 512], ALU.add))

            @block.scalar
            def _(s):
                for p in range(NP):
                    csr.wait(s)
                    cvr.wait(s, vt(p, "dargA2"))
                    csr.inc(s.activation(dexp8[:], scrA[:], AF.Exp))
                    csr.wait(s)
                    cvr.wait(s, vt(p, "dargB2"))
                    csr.inc(s.activation(dexp28[:], scrB[:], AF.Exp))
                    csr.wait(s)
                    csr.inc(s.activation(gam2[:], G2[:], AF.Exp))
                    csr.wait(s)
                    cvr.wait(s, vt(p, "gdiff"))
                    csr.inc(s.activation(etil2[:], gdiff2[:], AF.Exp))
                    csr.wait(s)
                    csr.inc(s.activation(gamC2[:], GC2[:], AF.Exp))
                    SH = (("a", slice(0, 512)), ("b", slice(512, 1024)))
                    for wv, sl in SH:
                        mmr.wait(s, mt(p, f"ti{wv}"))
                        csr.wait(s)
                        csr.inc(s.copy(Y8[:, sl], BB[1][:, sl]))
                    for r in range(NRND):
                        for wv, sl in SH:
                            mmr.wait(s, mt(p, f"y8{wv}_{r}"))
                            csr.wait(s)
                            csr.inc(s.copy(Y8[:, sl], BB[0][:, sl]))
                        if r < NRND - 1:
                            for wv, sl in SH:
                                mmr.wait(s, mt(p, f"t8{wv}_{r}"))
                                csr.wait(s)
                                csr.inc(s.copy(Z8[:, sl], BB[1][:, sl]))
                    b2v = BB[2][:].rearrange("p (h q) -> p h q", q=256)
                    for hf, cn in ((0, "A"), (1, "B")):
                        wfv = Wb4[hf][:].rearrange("p (h w) -> p h w", w=128)
                        uvv = Uv4[hf][:].rearrange("p (h w) -> p h w", w=128)
                        mmr.wait(s, mt(p, f"up_{cn}"))
                        csr.wait(s)
                        csr.inc(s.copy(wfv, b2v[:, :, 0:128]))
                        csr.wait(s)
                        csr.inc(s.copy(uvv, b2v[:, :, 128:256]))
                        mmr.wait(s, mt(p, f"wtt_{cn}"))
                        csr.wait(s)
                        csr.inc(s.copy(WtT4[hf][:], BB[3][:, hf * 512:hf * 512 + 512]))
                    for hf, cn in ((0, "A"), (1, "B")):
                        mmr.wait(s, mt(p, f"o1g_{cn}"))
                        if hf == 1:
                            cvr.wait(s, vt(p, "oadd_A"))
                        csr.wait(s)
                        csr.inc(s.copy(o1f[:], BB[0][:, hf * 512:hf * 512 + 512]))
                        cvr.wait(s, vt(p, f"s2_{cn}"))
                        csr.wait(s)
                        csr.inc(s.copy(S4b[:], S4[:]))

            @block.gpsimd
            def _(g):
                for cg in range(NCH):
                    p, hf = divmod(cg, 2)
                    cvr.wait(g, vt(p, "oadd_" + ("A" if hf == 0 else "B")))
                    g.dma_start(out=o_d[cg * 128:(cg + 1) * 128, :],
                                in_=o4t[hf][:]).then_inc(sto_.sem, 16)
                    sto_.n += 16
            pr.close()
        if stage >= 4:
            pkv.close()


        if stage >= 6:
            pqa.close()
            pg = ExitStack()
            E5 = pg.enter_context
            xnt2 = E5(nc.sbuf_tensor("xnt2", [128, KC * 512], BF16))
            wzs = E5(nc.sbuf_tensor("wzs", [128, KC * 512], BF16))
            nwr = E5(nc.sbuf_tensor("nwr", [1, 128], F32))
            nwb = E5(nc.sbuf_tensor("nwb", [128, 128], F32))
            ot = [E5(nc.sbuf_tensor(f"ot{i}", [128, 512], BF16)) for i in range(2)]
            sz2 = E5(nc.sbuf_tensor("sz2", [128, 1024], F32))
            u8 = E5(nc.sbuf_tensor("u8", [128, 4096], F32))
            yt8 = E5(nc.sbuf_tensor("yt8", [128, 4096], BF16))
            ms8 = E5(nc.sbuf_tensor("ms8", [128, 32], F32))
            rs8 = E5(nc.sbuf_tensor("rs8", [128, 32], F32))
            sq512 = E5(nc.sbuf_tensor("sq512", [128, 512], F32))

            NCSG = 10
            NCVG = 26

            def silutok(c):
                return (c // 8) * NCSG + (c % 8) + 1

            def umtok(c):
                return 1 + (c // 8) * NCVG + (c % 8) * 3 + 1

            def rstok(g_):
                return 1 + g_ * NCVG + 25

            def yttok(g_):
                return 1 + g_ * NCVG + 26

            def szh(j):
                return sz2[:, (j % 2) * 512:(j % 2) * 512 + 512]

            @block.sync
            def _(sy):
                cvr.wait(sy, P0 + NP * NVR2)
                csr.wait(sy, NP * NSR2)
                sy.dma_start(out=nwr[0:1, :], in_=normw[:].unsqueeze(0)).then_inc(lgn.sem, 16)
                lgn.n += 16
                for ci in range(NCH):
                    if ci % 4 == 0:
                        tt = ci // 4
                        if tt >= 1:
                            mmg.wait(sy, tt * 4 + 1)
                        sy.dma_start(out=xnt2[:].rearrange("p (kc t) -> p kc t", t=512),
                                     in_=xnfm[:, tt * 512:(tt + 1) * 512].rearrange("(kc p) t -> p kc t", p=128),
                                     ).then_inc(lgx.sem, 16)
                        lgx.n += 16
                    if ci >= 2:
                        cvg.wait(sy, umtok(ci - 2))
                    sy.dma_start(out=ot[ci % 2][:], in_=o_d[ci * 128:(ci + 1) * 128, :]).then_inc(lgo.sem, 16)
                    lgo.n += 16

            @block.gpsimd
            def _(g):
                sto_.wait(g, 16 * NCH)
                cvr.wait(g, P0 + NP * NVR2)
                csr.wait(g, NP * NSR2)
                g.dma_start(out=wzs[:].rearrange("p (kc n) -> p kc n", n=512),
                            in_=wz.rearrange("(kc p) n -> p kc n", p=128)).then_inc(lgw.sem, 16)
                lgw.n += 16
                for ci in range(NCH):
                    cvg.wait(g, yttok(ci // 8))
                    g.dma_start(out=y1[ci * 128:(ci + 1) * 128, :],
                                in_=yt8[:, (ci % 8) * 512:(ci % 8 + 1) * 512]).then_inc(stg.sem, 16)
                    stg.n += 16
                stg.wait(g)

            @block.tensor
            def _(t):
                lgn.wait(t, 16)
                sto_.wait(t, 16 * NCH)
                cvr.wait(t, P0 + NP * NVR2)
                ii = t.matmul(B[4][:, 0:128], ones_rf[0:1, :], nwr[0:1, :], start=True, stop=True)
                mmg.inc(ii)
                lgw.wait(t, 16)
                for ci in range(NCH):
                    tt = ci // 4
                    lgx.wait(t, 16 * (tt + 1))
                    if ci >= 4:
                        csg.wait(t, silutok(ci - 4))
                    for kc in range(KC):
                        ii = t.matmul(B[ci % 4][:, 0:512],
                                      xnt2[:].rearrange("p (kc t) -> p kc t", t=512)[:, kc, (ci % 4) * 128:(ci % 4 + 1) * 128],
                                      wzs[:].rearrange("p (kc n) -> p kc n", n=512)[:, kc, :],
                                      start=(kc == 0), stop=(kc == KC - 1))
                    mmg.inc(ii)

            @block.vector
            def _(v):
                mmg.wait(v, 1)
                cvg.inc(v.tensor_copy(nwb[:], B[4][:, 0:128]))
                for g_ in range(NCH // 8):
                    for j in range(8):
                        c = g_ * 8 + j
                        csg.wait(v, silutok(c))
                        lgo.wait(v, 16 * (c + 1))
                        cvg.wait(v)
                        usl = u8[:, j * 512:(j + 1) * 512]
                        cvg.inc(v.tensor_tensor(usl, ot[c % 2][:], szh(j), ALU.mult))
                        cvg.wait(v)
                        cvg.inc(v.tensor_tensor(sq512[:], usl, usl, ALU.mult))
                        cvg.wait(v)
                        cvg.inc(v.tensor_reduce(out=ms8[:, 4 * j:4 * j + 4],
                                                in_=sq512[:].rearrange("p (h w) -> p h w", w=128),
                                                axis=mybir.AxisListType.X, op=ALU.add))
                    csg.wait(v, g_ * NCSG + NCSG)
                    uv = u8[:].rearrange("p (j h w) -> p j h w", h=4, w=128)
                    rsbc = rs8[:].rearrange("p (j h o) -> p j h o", h=4, o=1).broadcast_to([128, 8, 4, 128])
                    cvg.wait(v)
                    cvg.inc(v.tensor_tensor(uv, uv, rsbc, ALU.mult))
                    stg.wait(v, 16 * 8 * g_)
                    u32 = u8[:].rearrange("p (s w) -> p s w", w=128)
                    nbc = nwb[:].rearrange("p (o w) -> p o w", o=1).broadcast_to([128, 32, 128])
                    yv = yt8[:].rearrange("p (s w) -> p s w", w=128)
                    cvg.wait(v)
                    cvg.inc(v.tensor_tensor(yv, u32, nbc, ALU.mult))

            @block.scalar
            def _(s):
                for g_ in range(NCH // 8):
                    for j in range(8):
                        c = g_ * 8 + j
                        mmg.wait(s, c + 2)
                        if c >= 2:
                            cvg.wait(s, umtok(c - 2))
                        csg.wait(s)
                        csg.inc(s.activation(szh(j), B[c % 4][:], AF.Silu))
                    cvg.wait(s, 1 + g_ * NCVG + 24)
                    if g_ >= 1:
                        cvg.wait(s, rstok(g_ - 1))
                    csg.wait(s)
                    csg.inc(s.activation(rs8[:], ms8[:], AF.Ln, bias=c_epsr[:, 0:1],
                                         scale=c_i128[:, 0:1]))
                    csg.wait(s)
                    csg.inc(s.activation(rs8[:], rs8[:], AF.Exp, scale=c_nh[:, 0:1]))
            pg.close()

    return nc



def kernel(**inputs):
    C = np.ascontiguousarray
    BFD = ml_dtypes.bfloat16
    x = inputs["L0_attn_x"][0]
    xs = [C(x[c * TPC:(c + 1) * TPC]) for c in range(NCORES)]

    nc = build_lnx(TPC)
    res = run_bass_kernel_spmd(nc, [{"x": xs[c]} for c in range(NCORES)],
                               core_ids=list(range(NCORES)))
    xn0 = np.concatenate([r["xn"] for r in res.results], 0)
    xn0fm = C(xn0.T)

    hd = HD
    nc = build_l0mix()
    ins = []
    for c in range(NCORES):
        j1, j2 = c, c + 8
        ins.append({
            "xnfm": xn0fm,
            "wq1": C(inputs["L0_attn_Wq1"][:, 2 * c * hd:(2 * c + 2) * hd]),
            "wq2": C(inputs["L0_attn_Wq2"][:, 2 * c * hd:(2 * c + 2) * hd]),
            "wk": C(np.concatenate([inputs["L0_attn_Wk"][:, j1 * hd:(j1 + 1) * hd],
                                    inputs["L0_attn_Wk"][:, j2 * hd:(j2 + 1) * hd]], 1)),
            "wv": C(np.concatenate([inputs["L0_attn_Wv"][:, j1 * hd:(j1 + 1) * hd],
                                    inputs["L0_attn_Wv"][:, j2 * hd:(j2 + 1) * hd]], 1)),
            "wlam": C(inputs["L0_attn_Wlam"][:, 2 * c:2 * c + 2]),
        })
    res = run_bass_kernel_spmd(nc, ins, core_ids=list(range(NCORES)))
    y0fm = np.concatenate([r["y0"] for r in res.results], 0)

    nc = build_ffn()
    ins = [{"xnfm": C(xn0fm[:, c * TPC:(c + 1) * TPC]),
            "wg": inputs["L0_ffn_w_gate"], "wu": inputs["L0_ffn_w_up"],
            "wd": inputs["L0_ffn_w_down"]} for c in range(NCORES)]
    res = run_bass_kernel_spmd(nc, ins, core_ids=list(range(NCORES)))
    g0 = [C(r["g"]) for r in res.results]

    nc = build_cmb(2048, True)
    ins = [{"x": xs[c], "yloc": C(y0fm[:, c * TPC:(c + 1) * TPC]),
            "wO": inputs["L0_attn_Wo"], "g": g0[c]} for c in range(NCORES)]
    res = run_bass_kernel_spmd(nc, ins, core_ids=list(range(NCORES)))
    x1 = [C(r["xo"]) for r in res.results]
    xn1 = np.concatenate([r["xn"] for r in res.results], 0)
    xn1fm = C(xn1.T)

    qT = C(inputs["L1_attn_W_qkv"].T)
    zT = C(inputs["L1_attn_W_z"].T)
    bT = C(inputs["L1_attn_W_b"].T)
    aT = C(inputs["L1_attn_W_a"].T)
    cw = inputs["L1_attn_conv_w"][:, 0, :]
    z4 = np.zeros(4, np.float32)
    nc = build_l1mix()
    ins = []
    for c in range(NCORES):
        ins.append({
            "xnfm": xn1fm,
            "wq": C(qT[:, 256 * c:256 * c + 256]),
            "wk": C(qT[:, 2048 + 256 * c:2048 + 256 * c + 256]),
            "wv": C(qT[:, 4096 + 512 * c:4096 + 512 * c + 512]),
            "wz": C(zT[:, 512 * c:512 * c + 512]),
            "wba": C(np.concatenate([bT[:, 4 * c:4 * c + 4], aT[:, 4 * c:4 * c + 4]], 1)),
            "convw": C(np.concatenate([cw[256 * c:256 * c + 256],
                                       cw[2048 + 256 * c:2048 + 256 * c + 256],
                                       cw[4096 + 512 * c:4096 + 512 * c + 512]], 0)),
            "dtb8": np.concatenate([z4, inputs["L1_attn_dt_bias"][4 * c:4 * c + 4]]),
            "alog8": np.concatenate([z4, inputs["L1_attn_A_log"][4 * c:4 * c + 4]]),
            "normw": inputs["L1_attn_norm_w"],
        })
    res = run_bass_kernel_spmd(nc, ins, core_ids=list(range(NCORES)))
    y1 = np.concatenate([r["y1"] for r in res.results], 1)
    y1fm = C(y1.T)

    nc = build_ffn()
    ins = [{"xnfm": C(xn1fm[:, c * TPC:(c + 1) * TPC]),
            "wg": inputs["L1_ffn_w_gate"], "wu": inputs["L1_ffn_w_up"],
            "wd": inputs["L1_ffn_w_down"]} for c in range(NCORES)]
    res = run_bass_kernel_spmd(nc, ins, core_ids=list(range(NCORES)))
    g1 = [C(r["g"]) for r in res.results]

    woT = C(inputs["L1_attn_W_out"].T)
    nc = build_cmb(4096, False)
    ins = [{"x": x1[c], "yloc": C(y1fm[:, c * TPC:(c + 1) * TPC]),
            "wO": woT, "g": g1[c]} for c in range(NCORES)]
    res = run_bass_kernel_spmd(nc, ins, core_ids=list(range(NCORES)))
    x2 = np.concatenate([r["xo"] for r in res.results], 0)
    return x2[None]

